# version 08
# baseline (speedup 1.0000x reference)
import numpy as np
from contextlib import ExitStack

import concourse.bass as bass
import concourse.mybir as mybir
from concourse.bass_utils import run_bass_kernel_spmd

F32 = mybir.dt.float32
BF16 = mybir.dt.bfloat16
I32 = mybir.dt.int32
AF = mybir.ActivationFunctionType
ALU = mybir.AluOpType

D = 1024
S = 2048
NSEQ = 2
CH = 8
TT = 512
NG = S // TT
HHC = 24
A_SCALE = 0.125
B_SCALE = float(1.0 / np.sqrt(128.0))
DEQ = float(1.0 / 127.0)
ENGINES = ("tensor", "vector", "scalar", "gpsimd", "sync")

MC_BPROJ, MC_BQK, MC_BV1, MC_BQ2, MC_BK2, MC_BV2, MC_BOUT = 0, 8, 16, 24, 32, 40, 48
MC_EPS, MC_NEG8, MC_BCOMB, MC_RS = 56, 57, 58, 66


def _marks(ws):
    return [m for m in ws if m is not None and m[0] is not None]


class Sched:
    def __init__(self, sems):
        self.sems = sems
        self.ops = {e: [] for e in ENGINES}
        self.cnt = {k: 0 for k in sems}
        self.lastwait = {}
        self.rd = {}
        self.tw = {}
        self.tr = {}
        self.gq = 0

    def op(self, eng, fn, waits=(), inc=None, n=1):
        w = []
        for m in _marks(waits):
            sname, v = m
            assert v <= self.cnt[sname], (sname, v, self.cnt[sname])
            if self.lastwait.get((eng, sname), -1) < v:
                w.append((sname, v))
                self.lastwait[(eng, sname)] = v
        if inc is not None:
            self.cnt[inc] += n
        post = self.cnt[inc] if inc is not None else None
        sems = self.sems

        def emit(e, _fn=fn, _w=tuple(w), _inc=inc, _n=n):
            for sname, v in _w:
                e.wait_ge(sems[sname], v)
            instr = _fn(e) if _fn is not None else None
            if _inc is not None:
                instr.then_inc(sems[_inc], _n)
        self.ops[eng].append(emit)
        return (inc, post) if inc is not None else None

    def wqw(self, key):
        return self.tr.pop(key, []) + ([self.tw[key]] if key in self.tw else [])

    def wrote(self, key, mark):
        if mark is not None:
            self.tw[key] = mark

    def rqw(self, key):
        return [self.tw[key]] if key in self.tw else []

    def read(self, key, mark):
        if mark is not None:
            self.tr.setdefault(key, []).append(mark)


def mm_group(sch, mms, banks, reads=(), waits=(), inc="t"):
    w = list(waits)
    for b in banks:
        w += sch.wqw(("ps", b))
    for k in reads:
        w += sch.rqw(k)
    n = len(mms)
    mark = None
    for i, (o, l, r) in enumerate(mms):
        mark = sch.op(
            "tensor",
            (lambda _o=o, _l=l, _r=r, _st=(i == 0), _sp=(i == n - 1):
             (lambda e: e.matmul(_o, _l, _r, start=_st, stop=_sp)))(),
            waits=(w if i == 0 else ()),
            inc=(inc if i == n - 1 else None))
    for b in banks:
        sch.wrote(("ps", b), mark)
    for k in reads:
        sch.read(k, mark)
    return mark


def ev(sch, eng, fn, rbanks=(), wkeys=(), rkeys=(), waits=(), inc=None, n=1):
    w = list(waits)
    for b in rbanks:
        w += sch.rqw(("ps", b))
    for k in wkeys:
        w += sch.wqw(k)
    for k in rkeys:
        w += sch.rqw(k)
    mark = sch.op(eng, fn, waits=w, inc=inc, n=n)
    for b in rbanks:
        sch.read(("ps", b), mark)
    for k in wkeys:
        sch.wrote(k, mark)
    for k in rkeys:
        sch.read(k, mark)
    return mark


def dma(sch, eng, fn, wkeys=(), rkeys=(), waits=()):
    if eng == "gpsimd":
        sname = "g%d" % sch.gq
        sch.gq = (sch.gq + 1) % 4
    else:
        sname = "y"
    w = list(waits) + [(sname, sch.cnt[sname])]
    return ev(sch, eng, fn, wkeys=wkeys, rkeys=rkeys, waits=w, inc=sname, n=16)


def build(taps=None):
    taps = taps or {}
    nc = bass.Bass("TRN2", target_bir_lowering=False)

    d_x = nc.dram_tensor("x", [NSEQ, CH, 128, S], F32, kind="ExternalInput").ap()
    d_wqk = nc.dram_tensor("wqk", [CH, 2, 128, 1024], F32, kind="ExternalInput").ap()
    d_wvT = nc.dram_tensor("wvT", [CH, 128, 1024], F32, kind="ExternalInput").ap()
    d_wpc = nc.dram_tensor("wpc", [CH, 128, 1024], F32, kind="ExternalInput").ap()
    d_wg = nc.dram_tensor("wg", [HHC, 128, 1024], I32, kind="ExternalInput").ap()
    d_wu = nc.dram_tensor("wu", [HHC, 128, 1024], I32, kind="ExternalInput").ap()
    d_wdc = nc.dram_tensor("wdc", [CH, 128, 3072], I32, kind="ExternalInput").ap()
    d_wb = [nc.dram_tensor(f"wb{i}", [CH, 128, 1024], F32, kind="ExternalInput").ap()
            for i in range(6)]
    d_misc = nc.dram_tensor("misc", [128, 58], F32, kind="ExternalInput").ap()
    d_cbf = nc.dram_tensor("cbf", [128, 256], F32, kind="ExternalInput").ap()
    d_y = nc.dram_tensor("y", [NSEQ, CH, 128, S], F32, kind="ExternalOutput").ap()
    d_taps = {name: nc.dram_tensor("tap_" + name, shape, dt, kind="ExternalOutput").ap()
              for name, (shape, dt) in taps.items()}

    with ExitStack() as ctx:
        XF = ctx.enter_context(nc.sbuf_tensor("XF", [128, CH * S], F32))
        XN = ctx.enter_context(nc.sbuf_tensor("XN", [128, CH * S], BF16))
        AR = ctx.enter_context(nc.sbuf_tensor("AR", [128, 24 * S], BF16))
        WB = ctx.enter_context(nc.sbuf_tensor("WB", [128, 4 * 1024], BF16))
        SQ = ctx.enter_context(nc.sbuf_tensor("SQ", [128, 1024], BF16))
        ROW = ctx.enter_context(nc.sbuf_tensor("ROW", [1, 1536], BF16))
        ROWF = ctx.enter_context(nc.sbuf_tensor("ROWF", [1, 512], F32))
        MISC = ctx.enter_context(nc.sbuf_tensor("MISC", [128, 72], F32))
        CBF = ctx.enter_context(nc.sbuf_tensor("CBF", [128, 256], BF16))
        BV2 = ctx.enter_context(nc.sbuf_tensor("BV2", [128, 8], BF16))
        PS = ctx.enter_context(nc.psum_tensor("PS", [128, 4096], F32))
        sems = {k: ctx.enter_context(nc.semaphore("sem_" + k))
                for k in ("t", "v", "s", "y", "g0", "g1", "g2", "g3")}
        sch = Sched(sems)

        def ar(c0, n):
            return AR[:, c0:c0 + n]

        def xf(c, t0, n):
            return XF[:, S * c + t0: S * c + t0 + n]

        def xn(c, t0, n):
            return XN[:, S * c + t0: S * c + t0 + n]

        def sl(i, off, n):
            return XN[:, 4096 * i + off: 4096 * i + off + n]

        def wbs(s_, off=0, n=1024):
            return WB[:, 1024 * s_ + off: 1024 * s_ + off + n]

        def bank(b, r0=0, r1=128, c0=0, n=512):
            return PS[r0:r1, 512 * b + c0: 512 * b + c0 + n]

        def mcol(c, n=1):
            return MISC[:, c:c + n]

        ones_col = CBF[:, 128:129]
        ones_row = CBF[0:1, 128:256]
        mask_ap = bass.AP(CBF, 0, [[256, 128], [0, 8], [1, 128]])

        G_B, V_B, M_B, PT_B, WVT_B, O_B = 0, S * 8, S * 16, S * 16, S * 20, S * 20

        def arslot(col0, ncols):
            s0, s1 = col0 // S, (col0 + ncols - 1) // S
            return [("ar", s_) for s_ in range(s0, s1 + 1)]

        def xnslot(col0, ncols):
            s0, s1 = col0 // 512, (col0 + ncols - 1) // 512
            return [("xnc", s_) for s_ in range(s0, s1 + 1)]

        def tap(name, src_ap):
            if name not in d_taps:
                return
            waits = [(s_, sch.cnt[s_]) for s_ in sch.cnt]
            dma(sch, "gpsimd",
                (lambda _s=src_ap, _d=d_taps[name]: (lambda e: e.dma_start(out=_d, in_=_s)))(),
                waits=waits)
            for e in ENGINES:
                sch.op(e, None, waits=[(s_, sch.cnt[s_]) for s_ in sch.cnt])

        mi = dma(sch, "sync", lambda e: e.dma_start(out=MISC[:, 0:58], in_=d_misc))
        cb = dma(sch, "gpsimd", lambda e: e.dma_start(out=CBF[:, :], in_=d_cbf))
        sch.rd["misc"] = mi
        sch.rd["cbf"] = cb
        bv2m = sch.op("vector", lambda e: e.tensor_copy(BV2[:, :], MISC[:, MC_BV2:MC_BV2 + 8]),
                      waits=[mi], inc="v")
        sch.rd["bv2"] = bv2m

        def ln_tile(src_t0, dst_fn, dst_keys_fn):
            casts = []
            for c in range(CH):
                m = ev(sch, "vector",
                       (lambda _c=c: (lambda e: e.tensor_copy(
                           dst_fn(_c), xf(_c, src_t0, TT))))(),
                       wkeys=dst_keys_fn(c), rkeys=[("xfc", c)], inc="v")
                casts.append(m)
            mm_group(sch, [(bank(6, 0, 1), ones_col, dst_fn(c)) for c in range(CH)],
                     banks=[6], waits=[casts[-1], sch.rd["cbf"]])
            s1m = ("t", sch.cnt["t"])
            mubf = ev(sch, "scalar", lambda e: e.activation(
                ROW[0:1, 0:512], bank(6, 0, 1), AF.Copy, scale=1.0 / 1024.0),
                rbanks=[6], wkeys=[("row", 0)], waits=[s1m], inc="s")
            mm_group(sch, [(bank(6), ones_row, ROW[0:1, 0:512])], banks=[6],
                     reads=[("row", 0)], waits=[mubf])
            bcm1 = ("t", sch.cnt["t"])
            tmarks = []
            for c in range(CH):
                m = ev(sch, "vector",
                       (lambda _c=c: (lambda e: e.scalar_tensor_tensor(
                           dst_fn(_c), xf(_c, src_t0, TT), 1.0, bank(6),
                           ALU.bypass, ALU.subtract)))(),
                       rbanks=[6], rkeys=[("xfc", c)], waits=[bcm1], inc="v")
                for k in dst_keys_fn(c):
                    sch.wrote(k, m)
                tmarks.append(m)
            for c in range(CH):
                sqk = ("sq", c % 2)
                sqh = SQ[:, 512 * (c % 2):512 * (c % 2) + 512]
                sqm = ev(sch, "scalar",
                         (lambda _c=c, _h=sqh: (lambda e: e.activation(
                             _h, dst_fn(_c), AF.Square)))(),
                         wkeys=[sqk], waits=[tmarks[c]], inc="s")
                w = [sqm] + (sch.wqw(("ps", 7)) if c == 0 else [])
                mm = sch.op("tensor",
                            (lambda _h=sqh, _st=(c == 0), _sp=(c == CH - 1):
                             (lambda e: e.matmul(bank(7, 0, 1), ones_col, _h,
                                                 start=_st, stop=_sp)))(),
                            waits=w, inc="t")
                sch.read(sqk, mm)
            s2m = ("t", sch.cnt["t"])
            sch.wrote(("ps", 7), s2m)
            lnv = ev(sch, "scalar", lambda e: e.activation(
                ROWF[0:1, :], bank(7, 0, 1), AF.Ln, scale=1.0 / 1024.0,
                bias=MISC[0:1, MC_EPS:MC_EPS + 1]),
                rbanks=[7], wkeys=[("rowf",)], waits=[s2m, sch.rd["misc"]], inc="s")
            rr = ev(sch, "scalar", lambda e: e.activation(
                ROW[0:1, 512:1024], ROWF[0:1, :], AF.Exp, scale=-0.5),
                rkeys=[("rowf",)], wkeys=[("row", 1)], waits=[lnv], inc="s")
            mm_group(sch, [(bank(7), ones_row, ROW[0:1, 512:1024])], banks=[7],
                     reads=[("row", 1)], waits=[rr])
            bcm2 = ("t", sch.cnt["t"])
            rsb = ev(sch, "vector", lambda e: e.tensor_copy(SQ[:, 0:512], bank(7)),
                     rbanks=[7], wkeys=[("sq", 0)], waits=[bcm2], inc="v")
            for c in range(CH):
                t2 = ev(sch, "vector",
                        (lambda _c=c: (lambda e: e.tensor_tensor(
                            dst_fn(_c), dst_fn(_c), SQ[:, 0:512], ALU.mult)))(),
                        rkeys=[("sq", 0)], waits=[rsb], inc="v")
                for k in dst_keys_fn(c):
                    sch.wrote(k, t2)

        def proj_fm(wtile_fn, wread_keys, src_fn, src_keys_fn, dst_fn, dst_keys_fn,
                    bias_col, banks, waits=()):
            for m in range(CH):
                b = banks[m % len(banks)]
                rds = list(wread_keys)
                for k in range(CH):
                    rds += src_keys_fn(k)
                gm = mm_group(sch, [(bank(b), wtile_fn(k, m), src_fn(k))
                                    for k in range(CH)],
                              banks=[b], reads=rds, waits=list(waits))
                ev(sch, "scalar",
                   (lambda _m=m, _b=b, _bc=bias_col: (lambda e: e.activation(
                       dst_fn(_m), bank(_b), AF.Identity, bias=mcol(_bc + _m))))(),
                   rbanks=[b], wkeys=dst_keys_fn(m),
                   waits=[gm, sch.rd["misc"]], inc="s")

        for s_i in range(NSEQ):
            xw = []
            for c in range(CH):
                xw += sch.wqw(("xfc", c))
            xld = dma(sch, "sync",
                      (lambda _s=s_i: (lambda e: e.dma_start(
                          out=bass.AP(XF, 0, [[CH * S, 128], [S, CH], [1, S]]),
                          in_=d_x[_s].rearrange("c p t -> p c t"))))(),
                      waits=xw)
            for c in range(CH):
                sch.wrote(("xfc", c), xld)

            groups = [(ec, eh) for ec in range(CH) for eh in range(2)]
            waves = [groups[0:6], groups[6:12], groups[12:16]]

            def emit_mwave(wave):
                bank_w = {gi: sch.wqw(("ps", gi)) for gi in range(len(wave))}
                for oc in range(CH):
                    slot = 2 * (oc % 2)
                    dm = dma(sch, "gpsimd",
                             (lambda _oc=oc, _sl=slot: (lambda e: e.dma_start(
                                 out=WB[:, 1024 * _sl:1024 * _sl + 2048],
                                 in_=d_wqk[_oc].rearrange("w p e -> p w e"))))(),
                             wkeys=[("wb", slot), ("wb", slot + 1)])
                    mk = None
                    for gi, (ec, eh) in enumerate(wave):
                        mk = sch.op("tensor",
                                    (lambda _sl=slot, _ec=ec, _eh=eh, _b=gi,
                                     _st=(oc == 0), _sp=(oc == CH - 1):
                                     (lambda e: e.matmul(
                                         bank(_b), wbs(_sl, 128 * _ec, 128),
                                         wbs(_sl + 1, 512 * _eh, 512),
                                         start=_st, stop=_sp)))(),
                                    waits=(([dm] if gi == 0 else []) +
                                           (bank_w[gi] if oc == 0 else [])),
                                    inc=("t" if gi == len(wave) - 1 else None))
                    sch.read(("wb", slot), mk)
                    sch.read(("wb", slot + 1), mk)
                for gi, (ec, eh) in enumerate(wave):
                    sch.wrote(("ps", gi), ("t", sch.cnt["t"]))
                    ev(sch, "vector",
                       (lambda _ec=ec, _eh=eh, _b=gi: (lambda e: e.tensor_copy(
                           ar(M_B + 1024 * _ec + 512 * _eh, 512), bank(_b))))(),
                       rbanks=[gi], wkeys=arslot(M_B + 1024 * ec + 512 * eh, 512),
                       inc="v")

            def emit_ln1(g):
                ln_tile(TT * g,
                        (lambda _g=g: (lambda c: xn(c, TT * _g, TT)))(),
                        (lambda _g=g: (lambda c: xnslot(S * c + TT * _g, TT)))())

            def emit_vproj(tg):
                grp = [(tt, h2) for tt in range(4) for h2 in range(2)]
                for wave in (grp[:6], grp[6:]):
                    for gi, (tt, h2) in enumerate(wave):
                        b = gi
                        rds = [k for kk in range(CH) for k in xnslot(S * kk + TT * tg, TT)]
                        rds += arslot(WVT_B, 8192)
                        gm = mm_group(sch,
                                      [(bank(b), xn(k, TT * tg + 128 * tt, 128),
                                        ar(WVT_B + 1024 * k + 512 * h2, 512))
                                       for k in range(CH)],
                                      banks=[b], reads=rds, waits=[wvld])
                        ev(sch, "scalar",
                           (lambda _tg=tg, _tt=tt, _h2=h2, _b=b: (lambda e: e.activation(
                               ar(V_B + 1024 * (4 * _tg + _tt) + 512 * _h2, 512),
                               bank(_b), AF.Copy)))(),
                           rbanks=[b],
                           wkeys=arslot(V_B + 1024 * (4 * tg + tt) + 512 * h2, 512),
                           waits=[gm], inc="s")

            emit_mwave(waves[0])
            emit_ln1(0)
            emit_mwave(waves[1])
            emit_ln1(1)
            emit_mwave(waves[2])
            if s_i == 0:
                tap("m", AR[:, M_B:M_B + 8192])
            wvld = dma(sch, "gpsimd",
                       lambda e: e.dma_start(
                           out=bass.AP(AR, WVT_B, [[24 * S, 128], [1024, CH], [1, 1024]]),
                           in_=d_wvT.rearrange("c p e -> p c e")),
                       wkeys=arslot(WVT_B, 8192))
            emit_ln1(2)
            emit_vproj(0)
            emit_ln1(3)
            if s_i == 0:
                tap("xn1", XN[:, :])
            for tg in range(1, NG):
                emit_vproj(tg)
            if s_i == 0:
                tap("v", AR[:, V_B:V_B + 16384])
            def emit_negc(g):
                for c in range(CH):
                    sqk = ("sq", c % 2)
                    sqh = SQ[:, 512 * (c % 2):512 * (c % 2) + 512]
                    sqm = ev(sch, "scalar",
                             (lambda _c=c, _g=g, _h=sqh: (lambda e: e.activation(
                                 _h, ar(G_B + S * _c + TT * _g, TT), AF.Square)))(),
                             wkeys=[sqk], rkeys=arslot(G_B + S * c + TT * g, TT),
                             inc="s")
                    w = [sqm] + (sch.wqw(("ps", 6)) if c == 0 else [])
                    mm = sch.op("tensor",
                                (lambda _h=sqh, _st=(c == 0), _sp=(c == CH - 1):
                                 (lambda e: e.matmul(bank(6, 0, 1), ones_col, _h,
                                                     start=_st, stop=_sp)))(),
                                waits=w, inc="t")
                    sch.read(sqk, mm)
                gsm = ("t", sch.cnt["t"])
                sch.wrote(("ps", 6), gsm)
                l1 = ev(sch, "scalar", lambda e: e.activation(
                    ROWF[0:1, :], bank(6, 0, 1), AF.Ln,
                    bias=MISC[0:1, MC_EPS:MC_EPS + 1]),
                    rbanks=[6], wkeys=[("rowf",)],
                    waits=[gsm, sch.rd["misc"]], inc="s")
                l2 = ev(sch, "scalar", lambda e: e.activation(
                    bank(6, 0, 1), ROWF[0:1, :], AF.Exp, scale=0.5),
                    rbanks=[6], rkeys=[("rowf",)], waits=[l1], inc="s")
                sch.wrote(("ps", 6), l2)
                return ev(sch, "vector",
                          (lambda _g=g: (lambda e: e.tensor_scalar(
                              ROW[0:1, 512 * (_g % 2):512 * (_g % 2) + 512],
                              bank(6, 0, 1), -32.2, None, ALU.mult)))(),
                          rbanks=[6], wkeys=[("row", g % 2)], waits=[l2], inc="v")

            ngc_marks = {}
            for g in range(NG):
                for m in range(CH):
                    b = m % 6
                    rds = arslot(M_B, 8192) + \
                        [k for kk in range(CH) for k in xnslot(S * kk + TT * g, TT)]
                    gm = mm_group(sch,
                                  [(bank(b),
                                    ar(M_B + 1024 * k, 1024)[:, 128 * m:128 * m + 128],
                                    xn(k, TT * g, TT)) for k in range(CH)],
                                  banks=[b], reads=rds)
                    ev(sch, "scalar",
                       (lambda _g=g, _m=m, _b=b: (lambda e: e.activation(
                           ar(G_B + S * _m + TT * _g, TT), bank(_b), AF.Copy)))(),
                       rbanks=[b], wkeys=arslot(G_B + S * m + TT * g, TT),
                       waits=[gm], inc="s")
                if g < 2 and m == CH - 1:
                    ngc_marks[g] = emit_negc(g)

            if s_i == 0:
                tap("g", AR[:, 0:16384])
            for g in range(NG):
                ngc = ngc_marks[g]
                for kt in range(16):
                    b = kt % 2
                    rds = [k for kk in range(CH) for k in xnslot(S * kk, S)]
                    rds += [("ar", kk) for kk in range(CH)] + [("row", g % 2)]
                    mms = [(bank(b), xn(k, 128 * kt, 128),
                            ar(G_B + S * k + TT * g, TT)) for k in range(CH)]
                    mms.append((bank(b), ones_row,
                                ROW[0:1, 512 * (g % 2):512 * (g % 2) + 512]))
                    gm = mm_group(sch, mms, banks=[b], reads=rds, waits=[ngc])
                    ev(sch, "scalar",
                       (lambda _kt=kt, _b=b: (lambda e: e.activation(
                           ar(PT_B + 512 * _kt, 512), bank(_b), AF.Exp,
                           scale=A_SCALE)))(),
                       rbanks=[b], wkeys=arslot(PT_B + 512 * kt, 512),
                       waits=[gm], inc="s")
                expm = ("s", sch.cnt["s"])
                if g + 2 < NG:
                    ngc_marks[g + 2] = emit_negc(g + 2)
                if s_i == 0 and g == 0:
                    tap("pt0", AR[:, PT_B:PT_B + 8192])
                    tap("row0", ROW[0:1, :])
                for j in range(4):
                    ob = 2 if (j % 2 == 0) else 4
                    w = sch.wqw(("ps", ob)) + sch.wqw(("ps", ob + 1)) + \
                        sch.wqw(("ps", 1)) + [expm]
                    rds = arslot(PT_B, 8192) + arslot(V_B, 16384)
                    for k in rds:
                        w += sch.rqw(k)
                    for kt in range(16):
                        lhs = ar(PT_B + 512 * kt + 128 * j, 128)
                        trip = [(PS[:, 512 * ob:512 * ob + 512], lhs,
                                 ar(V_B + 1024 * kt, 512)),
                                (PS[:, 512 * ob + 512:512 * ob + 1024], lhs,
                                 ar(V_B + 1024 * kt + 512, 512)),
                                (PS[:, 512 + j:512 + j + 1], lhs, ones_col)]
                        for u, (o, l, r) in enumerate(trip):
                            sch.op("tensor",
                                   (lambda _o=o, _l=l, _r=r, _st=(kt == 0),
                                    _sp=(kt == 15):
                                    (lambda e: e.matmul(_o, _l, _r, start=_st,
                                                        stop=_sp)))(),
                                   waits=(w if (kt == 0 and u == 0) else []),
                                   inc=("t" if (kt == 15 and u == 2) else None))
                    pvm = ("t", sch.cnt["t"])
                    sch.wrote(("ps", ob), pvm)
                    sch.wrote(("ps", ob + 1), pvm)
                    sch.wrote(("ps", 1), pvm)
                    for k in rds:
                        sch.read(k, pvm)
                    rc = ev(sch, "vector",
                            (lambda _j=j: (lambda e: e.reciprocal(
                                mcol(MC_RS + _j % 2), PS[:, 512 + _j:512 + _j + 1])))(),
                            rbanks=[1], wkeys=[("rs", j % 2)], inc="v")
                    jj = 4 * g + j
                    r_i = jj % 8
                    ev(sch, "scalar",
                       (lambda _r=r_i, _ob=ob, _j=j: (lambda e: e.activation(
                           ar(O_B + 1024 * _r, 1024),
                           PS[:, 512 * _ob:512 * _ob + 1024], AF.Identity,
                           scale=mcol(MC_RS + _j % 2))))(),
                       rbanks=[ob, ob + 1], rkeys=[("rs", j % 2)],
                       wkeys=arslot(O_B + 1024 * r_i, 1024), waits=[rc], inc="s")

                if g in (1, 3):
                    par = 0 if g == 1 else 1
                    for ec in range(CH):
                        slot = ec % 4
                        dm = dma(sch, "gpsimd",
                                 (lambda _ec=ec, _sl=slot: (lambda e: e.dma_start(
                                     out=wbs(_sl), in_=d_wpc[_ec])))(),
                                 wkeys=[("wb", slot)])
                        ob = 2 if (ec % 2 == 0) else 4
                        w = sch.wqw(("ps", ob)) + sch.wqw(("ps", ob + 1)) + [dm]
                        rds = arslot(O_B, 8192) + [("wb", slot)]
                        for k in rds:
                            w += sch.rqw(k)
                        for rc_ in range(CH):
                            for ih in range(2):
                                sch.op("tensor",
                                       (lambda _sl=slot, _rc=rc_, _ih=ih, _ob=ob:
                                        (lambda e: e.matmul(
                                            PS[:, 512 * (_ob + _ih):512 * (_ob + _ih) + 512],
                                            wbs(_sl, 128 * _rc, 128),
                                            ar(O_B + 1024 * _rc + 512 * _ih, 512),
                                            start=(_rc == 0), stop=(_rc == CH - 1))))(),
                                       waits=(w if (rc_ == 0 and ih == 0) else []),
                                       inc=("t" if (rc_ == CH - 1 and ih == 1) else None))
                        opm = ("t", sch.cnt["t"])
                        sch.wrote(("ps", ob), opm)
                        sch.wrote(("ps", ob + 1), opm)
                        for k in rds:
                            sch.read(k, opm)
                        ev(sch, "vector",
                           (lambda _ec=ec, _ob=ob, _par=par: (lambda e: e.scalar_tensor_tensor(
                               bass.AP(XF, S * _ec + _par, [[CH * S, 128], [2, 1024]]),
                               PS[:, 512 * _ob:512 * _ob + 1024],
                               mcol(MC_BPROJ + _ec),
                               bass.AP(XF, S * _ec + _par, [[CH * S, 128], [2, 1024]]),
                               ALU.add, ALU.add)))(),
                           rbanks=[ob, ob + 1], wkeys=[("xfc", ec)],
                           waits=[opm, sch.rd["misc"]], inc="v")

            if s_i == 0:
                tap("o", AR[:, O_B:O_B + 8192])
                tap("xf1", XF[:, :])
            def ffn_layer(tag):
                def f_ln(g):
                    ln_tile(TT * g,
                            (lambda _g=g: (lambda c: xn(c, TT * _g, TT)))(),
                            (lambda _g=g: (lambda c: xnslot(S * c + TT * _g, TT)))())

                dm_marks = {}

                def f_dm(c):
                    slot = 2 * (c % 2)
                    dmg = dma(sch, "gpsimd",
                              (lambda _c=c, _sl=slot: (lambda e: e.dma_start(
                                  out=wbs(_sl), in_=d_wg[_c])))(),
                              wkeys=[("wb", slot)])
                    dmu = dma(sch, "gpsimd",
                              (lambda _c=c, _sl=slot: (lambda e: e.dma_start(
                                  out=wbs(_sl + 1), in_=d_wu[_c])))(),
                              wkeys=[("wb", slot + 1)])
                    dm_marks[c] = (slot, dmg, dmu)

                def f_unit(c, g):
                    if c not in dm_marks:
                        f_dm(c)
                    slot, dmg, dmu = dm_marks[c]
                    bg = 2 * (g % 2)
                    bu = bg + 1
                    rds = [("wb", slot)] +                         [k for kk in range(CH) for k in xnslot(S * kk + TT * g, TT)]
                    gmg = mm_group(sch,
                                   [(bank(bg), wbs(slot, 128 * k, 128),
                                     xn(k, TT * g, TT)) for k in range(CH)],
                                   banks=[bg], reads=rds, waits=[dmg])
                    gmu = mm_group(sch,
                                   [(bank(bu), wbs(slot + 1, 128 * k, 128),
                                     xn(k, TT * g, TT)) for k in range(CH)],
                                   banks=[bu], reads=[("wb", slot + 1)],
                                   waits=[dmu])
                    sq_i = (2 * c + g) % 2
                    sqh = SQ[:, 512 * sq_i:512 * sq_i + 512]
                    sil = ev(sch, "scalar",
                             (lambda _bg=bg, _h=sqh: (lambda e: e.activation(
                                 _h, bank(_bg), AF.Silu, scale=DEQ)))(),
                             rbanks=[bg], wkeys=[("sq", sq_i)],
                             waits=[gmg], inc="s")
                    ev(sch, "vector",
                       (lambda _bu=bu, _h=sqh, _c=c, _g=g:
                        (lambda e: e.scalar_tensor_tensor(
                            ar(S * _c + TT * _g, TT), bank(_bu), DEQ,
                            _h, ALU.mult, ALU.mult)))(),
                       rbanks=[bu], rkeys=[("sq", sq_i)],
                       wkeys=arslot(S * c + TT * g, TT),
                       waits=[sil, gmu], inc="v")

                f_ln(0)
                order = [(0, 0), "L1", (0, 1), (1, 0), (1, 1), "L2",
                         (0, 2), (1, 2), "L3", (0, 3), (1, 3)]
                order += [(c, g) for c in range(2, HHC) for g in range(NG)]
                for u in order:
                    if u == "L1":
                        f_ln(1)
                    elif u == "L2":
                        f_ln(2)
                    elif u == "L3":
                        f_ln(3)
                    else:
                        f_unit(*u)
                if s_i == 0:
                    tap(tag + "xn", XN[:, :])
                if s_i == 0:
                    tap(tag + "h", AR[:, :])
                for ec in range(CH):
                    pieces = []
                    for i in range(3):
                        p_i = 3 * ec + i
                        slot = p_i % 4
                        dm = dma(sch, "gpsimd",
                                 (lambda _ec=ec, _i=i, _sl=slot: (lambda e: e.dma_start(
                                     out=wbs(_sl),
                                     in_=d_wdc[_ec][:, 1024 * _i:1024 * _i + 1024])))(),
                                 wkeys=[("wb", slot)])
                        pieces.append((slot, dm))
                    bb = 4 * (ec % 2)
                    w0 = sch.wqw(("ps", bb)) + sch.wqw(("ps", bb + 1)) + \
                        sch.wqw(("ps", bb + 2)) + sch.wqw(("ps", bb + 3))
                    rds = [("wb", pieces[i][0]) for i in range(3)] + \
                        [("ar", s_) for s_ in range(24)]
                    for c in range(HHC):
                        slot, dm = pieces[c // 8]
                        wchunk = []
                        if c % 8 == 0:
                            wchunk.append(dm)
                        wchunk += sch.rqw(("ar", (S * c) // S))
                        if c == 0:
                            wchunk += w0
                        for g in range(NG):
                            sch.op("tensor",
                                   (lambda _sl=slot, _c=c, _g=g, _b=bb + g:
                                    (lambda e: e.matmul(
                                        bank(_b), wbs(_sl, 128 * (_c % 8), 128),
                                        ar(S * _c + TT * _g, TT),
                                        start=(_c == 0), stop=(_c == HHC - 1))))(),
                                   waits=(wchunk if g == 0 else []),
                                   inc=("t" if (c == HHC - 1 and g == NG - 1) else None))
                    dwm = ("t", sch.cnt["t"])
                    for b_ in range(4):
                        sch.wrote(("ps", bb + b_), dwm)
                    for k in rds:
                        sch.read(k, dwm)
                    for g in range(NG):
                        ev(sch, "vector",
                           (lambda _ec=ec, _g=g, _b=bb + g: (lambda e: e.scalar_tensor_tensor(
                               xf(_ec, TT * _g, TT), bank(_b), DEQ,
                               xf(_ec, TT * _g, TT), ALU.mult, ALU.add)))(),
                           rbanks=[bb + g], wkeys=[("xfc", ec)],
                           waits=[dwm], inc="v")

            ffn_layer("f1")
            if s_i == 0:
                tap("xf2", XF[:, :])

            wbld = []
            for i in range(6):
                m = dma(sch, "gpsimd",
                        (lambda _i=i: (lambda e: e.dma_start(
                            out=bass.AP(AR, S * 4 * _i,
                                        [[24 * S, 128], [1024, CH], [1, 1024]]),
                            in_=d_wb[_i].rearrange("c p e -> p c e"))))(),
                        wkeys=[("ar", s_) for s_ in range(4 * i, 4 * i + 4)])
                wbld.append(m)
            for blk in range(NG):
                t0 = TT * blk
                ln_tile(t0, (lambda c: sl(0, 512 * c, 512)),
                        lambda c: xnslot(512 * c, 512))
                proj_fm(lambda k, m: ar(1024 * k + 128 * m, 128),
                        [("ar", kk // 2) for kk in range(CH)],
                        lambda k: sl(0, 512 * k, 512),
                        lambda k: xnslot(512 * k, 512),
                        lambda m: sl(1, 512 * m, 512),
                        lambda m: xnslot(4096 + 512 * m, 512),
                        MC_BQK, banks=(0, 1), waits=[wbld[0]])
                proj_fm(lambda k, m: ar(S * 4 + 1024 * k + 128 * m, 128),
                        [("ar", 4 + kk // 2) for kk in range(CH)],
                        lambda k: sl(0, 512 * k, 512),
                        lambda k: xnslot(512 * k, 512),
                        lambda m: sl(2, 512 * m, 512),
                        lambda m: xnslot(8192 + 512 * m, 512),
                        MC_BV1, banks=(2, 3), waits=[wbld[1]])
                proj_fm(lambda k, m: ar(S * 8 + 1024 * k + 128 * m, 128),
                        [("ar", 8 + kk // 2) for kk in range(CH)],
                        lambda k: sl(1, 512 * k, 512),
                        lambda k: xnslot(4096 + 512 * k, 512),
                        lambda m: sl(3, 512 * m, 512),
                        lambda m: xnslot(12288 + 512 * m, 512),
                        MC_BQ2, banks=(0, 1), waits=[wbld[2]])
                proj_fm(lambda k, m: ar(S * 12 + 1024 * k + 128 * m, 128),
                        [("ar", 12 + kk // 2) for kk in range(CH)],
                        lambda k: sl(1, 512 * k, 512),
                        lambda k: xnslot(4096 + 512 * k, 512),
                        lambda m: sl(0, 512 * m, 512),
                        lambda m: xnslot(512 * m, 512),
                        MC_BK2, banks=(2, 3), waits=[wbld[3]])
                if s_i == 0 and blk == 0:
                    tap("bq", sl(3, 0, 4096))
                    tap("bk", sl(0, 0, 4096))
                    tap("bqkin", sl(1, 0, 4096))
                    tap("bvin", sl(2, 0, 4096))
                for tt in range(4):
                    for h2 in range(2):
                        b = (4 * tt + 2 * h2) % 4
                        rds = [("ar", 16 + kk // 2) for kk in range(CH)]
                        for k in range(CH):
                            rds += xnslot(8192 + 512 * k, 512)
                        gm = mm_group(sch,
                                      [(bank(b), sl(2, 512 * k + 128 * tt, 128),
                                        ar(S * 16 + 1024 * k + 512 * h2, 512))
                                       for k in range(CH)],
                                      banks=[b], reads=rds, waits=[wbld[4]])
                        ev(sch, "scalar",
                           (lambda _tt=tt, _h2=h2, _b=b: (lambda e: e.activation(
                               sl(1, 1024 * _tt + 512 * _h2, 512), bank(_b),
                               AF.Copy)))(),
                           rbanks=[b],
                           wkeys=xnslot(4096 + 1024 * tt + 512 * h2, 512),
                           waits=[gm], inc="s")
                if s_i == 0 and blk == 0:
                    tap("bvtm", sl(1, 0, 4096))
                if blk == 0:
                    w = sch.wqw(("ps", 7))
                    for m in range(CH):
                        gm = mm_group(sch,
                                      [(PS[:, 3584 + m:3584 + m + 1],
                                        ar(S * 20 + 1024 * k + 128 * m, 128),
                                        BV2[:, k:k + 1]) for k in range(CH)],
                                      banks=[], reads=[("ar", 20 + m // 2)],
                                      waits=(w if m == 0 else []) + [wbld[5], sch.rd["bv2"]],
                                      inc="t")
                    bcw = ("t", sch.cnt["t"])
                    sch.wrote(("ps", 7), bcw)
                    for m in range(CH):
                        ev(sch, "vector",
                           (lambda _m=m: (lambda e: e.scalar_tensor_tensor(
                               mcol(MC_BCOMB + _m), PS[:, 3584 + _m:3584 + _m + 1], 1.0,
                               MISC[:, MC_BOUT + _m:MC_BOUT + _m + 1], ALU.mult, ALU.add)))(),
                           rbanks=([7] if m == 0 else []),
                           waits=[bcw, sch.rd["misc"]], inc="v")
                    bcomb = ("v", sch.cnt["v"])
                    sch.read(("ps", 7), bcomb)

                def b_scores(jj):
                    pslot = 3 * (jj % 2)
                    w = sch.wqw(("ps", 0)) + sch.wqw(("ps", 1))
                    rds = xnslot(0, 4096) + xnslot(12288, 4096)
                    for h in range(CH):
                        sch.op("tensor",
                               (lambda _h=h, _jj=jj: (lambda e: e.matmul(
                                   PS[:, 128 * _h:128 * _h + 128],
                                   sl(0, 512 * _h + 128 * _jj, 128),
                                   sl(3, 512 * _h + 128 * _jj, 128),
                                   start=True, stop=True)))(),
                               waits=(w + [sch.rqw(k)[0] for k in rds if sch.rqw(k)]
                                      if h == 0 else []),
                               inc=("t" if h == CH - 1 else None))
                    sw = ("t", sch.cnt["t"])
                    sch.wrote(("ps", 0), sw)
                    sch.wrote(("ps", 1), sw)
                    for k in rds:
                        sch.read(k, sw)
                    exm = ev(sch, "scalar",
                             (lambda _ps=pslot: (lambda e: e.activation(
                                 wbs(_ps, 0, 1024), PS[:, 0:1024], AF.Exp,
                                 scale=B_SCALE, bias=MISC[:, MC_NEG8:MC_NEG8 + 1])))(),
                             rbanks=[0, 1], wkeys=[("wb", pslot)],
                             waits=[sw, sch.rd["misc"]], inc="s")
                    if s_i == 0 and blk == 0 and jj == 0:
                        tap("bpraw", wbs(pslot, 0, 1024))
                    return exm

                def b_tail(jj, exm):
                    pslot = 3 * (jj % 2)
                    pmm = ev(sch, "vector",
                             (lambda _ps=pslot: (lambda e: e.scalar_tensor_tensor(
                                 wbs(1, 0, 1024), wbs(_ps, 0, 1024), 0.0,
                                 mask_ap, ALU.bypass, ALU.mult)))(),
                             wkeys=[("wb", 1)], rkeys=[("wb", pslot)],
                             waits=[exm, sch.rd["cbf"]], inc="v")
                    if s_i == 0 and blk == 0 and jj == 0:
                        tap("bpm", wbs(1, 0, 1024))
                    w2 = sch.wqw(("ps", 2)) + sch.wqw(("ps", 3))
                    for h in range(CH):
                        sch.op("tensor",
                               (lambda _h=h: (lambda e: e.matmul(
                                   PS[0:1, 1024 + 128 * _h:1024 + 128 * _h + 128],
                                   ones_col, wbs(1, 128 * _h, 128),
                                   start=True, stop=True)))(),
                               waits=((w2 + [pmm]) if h == 0 else []),
                               inc=("t" if h == CH - 1 else None))
                    smw = ("t", sch.cnt["t"])
                    sch.wrote(("ps", 2), smw)
                    sch.wrote(("ps", 3), smw)
                    sch.read(("wb", 1), smw)
                    rc0 = ev(sch, "vector",
                             lambda e: e.reciprocal(PS[0:1, 2048:3072],
                                                    PS[0:1, 1024:2048]),
                             rbanks=[2, 3], wkeys=[("psrow45",)], waits=[smw] +
                             sch.wqw(("ps", 4)) + sch.wqw(("ps", 5)), inc="v")
                    sch.wrote(("ps", 4), rc0)
                    sch.wrote(("ps", 5), rc0)
                    rcm = ev(sch, "vector",
                             lambda e: e.tensor_copy(WB[0:1, 2048:3072],
                                                     PS[0:1, 2048:3072]),
                             rbanks=[4, 5], wkeys=[("wb", 2)], waits=[rc0], inc="v")
                    mm_group(sch, [(PS[:, 1024:1536], ones_row, WB[0:1, 2048:2560])],
                             banks=[2], reads=[("wb", 2)], waits=[rcm])
                    mm_group(sch, [(PS[:, 1536:2048], ones_row, WB[0:1, 2560:3072])],
                             banks=[3], reads=[("wb", 2)])
                    cbm = ("t", sch.cnt["t"])
                    ptm = ev(sch, "vector",
                             (lambda _ps=pslot: (lambda e: e.tensor_tensor(
                                 wbs(_ps, 0, 1024), wbs(1, 0, 1024),
                                 PS[:, 1024:2048], ALU.mult)))(),
                             rbanks=[2, 3], wkeys=[("wb", pslot)], rkeys=[("wb", 1)],
                             waits=[cbm], inc="v")
                    if s_i == 0 and blk == 0 and jj == 0:
                        tap("brc", WB[0:1, 2048:3072])
                        tap("bpt", wbs(pslot, 0, 1024))
                    w3 = sch.wqw(("ps", 4)) + sch.wqw(("ps", 5))
                    rds3 = xnslot(4096 + 1024 * jj, 1024)
                    for h in range(CH):
                        sch.op("tensor",
                               (lambda _h=h, _jj=jj, _ps=pslot: (lambda e: e.matmul(
                                   PS[:, 2048 + 128 * _h:2048 + 128 * _h + 128],
                                   sl(1, 1024 * _jj + 128 * _h, 128),
                                   wbs(_ps, 128 * _h, 128),
                                   start=True, stop=True)))(),
                               waits=((w3 + [ptm] +
                                       [sch.rqw(k)[0] for k in rds3 if sch.rqw(k)])
                                      if h == 0 else []),
                               inc=("t" if h == CH - 1 else None))
                    ovm = ("t", sch.cnt["t"])
                    sch.wrote(("ps", 4), ovm)
                    sch.wrote(("ps", 5), ovm)
                    sch.read(("wb", pslot), ovm)
                    for k in rds3:
                        sch.read(k, ovm)
                    ev(sch, "scalar",
                       (lambda _jj=jj: (lambda e: e.activation(
                           bass.AP(XN, 4096 * 2 + 128 * _jj,
                                   [[CH * S, 128], [512, 8], [1, 128]]),
                           PS[:, 2048:3072].rearrange("p (h x) -> p h x", x=128),
                           AF.Copy)))(),
                       rbanks=[4, 5], wkeys=xnslot(8192, 4096),
                       waits=[ovm], inc="s")

                exms = {}
                for jj in range(4):
                    exms[jj] = b_scores(jj)
                    if jj >= 1:
                        b_tail(jj - 1, exms[jj - 1])
                b_tail(3, exms[3])
                if s_i == 0 and blk == 0:
                    tap("bo", sl(2, 0, 4096))
                for m in range(CH):
                    b = m % 2
                    rds = [("ar", 20 + kk // 2) for kk in range(CH)]
                    for k in range(CH):
                        rds += xnslot(8192 + 512 * k, 512)
                    gm = mm_group(sch,
                                  [(bank(b), ar(S * 20 + 1024 * k + 128 * m, 128),
                                    sl(2, 512 * k, 512)) for k in range(CH)],
                                  banks=[b], reads=rds, waits=[wbld[5]])
                    ev(sch, "vector",
                       (lambda _m=m, _b=b, _t0=t0: (lambda e: e.scalar_tensor_tensor(
                           xf(_m, _t0, TT), bank(_b), mcol(MC_BCOMB + _m),
                           xf(_m, _t0, TT), ALU.add, ALU.add)))(),
                       rbanks=[b], wkeys=[("xfc", m)],
                       waits=[gm, bcomb], inc="v")

            if s_i == 0:
                tap("xf3", XF[:, :])
            ffn_layer("f2")

            yw = []
            for c in range(CH):
                yw += sch.rqw(("xfc", c))
            yst = dma(sch, "sync",
                      (lambda _s=s_i: (lambda e: e.dma_start(
                          out=d_y[_s].rearrange("c p t -> p c t"),
                          in_=bass.AP(XF, 0, [[CH * S, 128], [S, CH], [1, S]]))))(),
                      waits=yw)
            for c in range(CH):
                sch.read(("xfc", c), yst)

        sch.op("sync", None, waits=[("y", sch.cnt["y"])])

        block = ctx.enter_context(nc.Block())

        @block.tensor
        def _(e):
            for f in sch.ops["tensor"]:
                f(e)

        @block.vector
        def _(e):
            for f in sch.ops["vector"]:
                f(e)

        @block.scalar
        def _(e):
            for f in sch.ops["scalar"]:
                f(e)

        @block.gpsimd
        def _(e):
            for f in sch.ops["gpsimd"]:
                f(e)

        @block.sync
        def _(e):
            for f in sch.ops["sync"]:
                f(e)

    return nc


def host_inputs(L0_attn_a_w_qkv, L0_attn_a_w_proj, L0_attn_a_b_proj,
                L0_attn_b_proj_qk_w, L0_attn_b_proj_qk_b, L0_attn_b_proj_v_w,
                L0_attn_b_proj_v_b, L0_attn_b_in_proj_w, L0_attn_b_in_proj_b,
                L0_attn_b_out_proj_w, L0_attn_b_out_proj_b,
                L0_ffn_w_gate, L0_ffn_w_up, L0_ffn_w_down):
    f = lambda a: np.ascontiguousarray(np.asarray(a, dtype=np.float32))
    ii = lambda a: np.ascontiguousarray(np.asarray(a, dtype=np.int32))
    wqkv = f(L0_attn_a_w_qkv)
    wq_img = wqkv[0:1024].reshape(8, 128, 1024)
    wk_img = wqkv[1024:2048].reshape(8, 128, 1024)
    wqk = np.ascontiguousarray(np.stack([wq_img, wk_img], axis=1))
    wvT = np.ascontiguousarray(wqkv[2048:3072].T).reshape(8, 128, 1024)
    wpT = np.ascontiguousarray(f(L0_attn_a_w_proj).T)
    wpc = np.ascontiguousarray(
        wpT.reshape(8, 128, 8, 128).transpose(2, 1, 0, 3)).reshape(8, 128, 1024)
    wg = np.ascontiguousarray(
        ii(L0_ffn_w_gate).reshape(24, 128, 8, 128).transpose(0, 3, 2, 1)).reshape(24, 128, 1024)
    wu = np.ascontiguousarray(
        ii(L0_ffn_w_up).reshape(24, 128, 8, 128).transpose(0, 3, 2, 1)).reshape(24, 128, 1024)
    wdc = np.ascontiguousarray(
        ii(L0_ffn_w_down).reshape(8, 128, 24, 128).transpose(0, 3, 2, 1)).reshape(8, 128, 3072)
    inw = f(L0_attn_b_in_proj_w)
    wb_list = []
    for W in (f(L0_attn_b_proj_qk_w), f(L0_attn_b_proj_v_w), inw[0:1024],
              inw[1024:2048], inw[2048:3072], f(L0_attn_b_out_proj_w)):
        wb_list.append(np.ascontiguousarray(np.ascontiguousarray(W.T)).reshape(8, 128, 1024))
    misc = np.zeros((128, 58), np.float32)
    inb = f(L0_attn_b_in_proj_b)
    for c0, b in ((MC_BPROJ, f(L0_attn_a_b_proj)), (MC_BQK, f(L0_attn_b_proj_qk_b)),
                  (MC_BV1, f(L0_attn_b_proj_v_b)), (MC_BQ2, inb[0:1024]),
                  (MC_BK2, inb[1024:2048]), (MC_BV2, inb[2048:3072]),
                  (MC_BOUT, f(L0_attn_b_out_proj_b))):
        misc[:, c0:c0 + 8] = b.reshape(8, 128).T
    misc[:, MC_EPS] = 1e-5
    misc[:, MC_NEG8] = -8.0
    cbf = np.zeros((128, 256), np.float32)
    p = np.arange(128)
    m01 = ((p[:, None] // 4 == p[None, :] // 4) & (p[:, None] != p[None, :]))
    cbf[:, 0:128] = m01.astype(np.float32)
    cbf[:, 128:256] = 1.0
    return dict(wqk=wqk, wvT=wvT, wpc=wpc, wg=wg, wu=wu, wdc=wdc,
                wb0=wb_list[0], wb1=wb_list[1], wb2=wb_list[2], wb3=wb_list[3],
                wb4=wb_list[4], wb5=wb_list[5], misc=misc, cbf=cbf)


def kernel(L0_attn_a_x, L0_attn_a_w_qkv, L0_attn_a_w_proj, L0_attn_a_b_proj,
           L0_attn_b_proj_qk_w, L0_attn_b_proj_qk_b, L0_attn_b_proj_v_w,
           L0_attn_b_proj_v_b, L0_attn_b_in_proj_w, L0_attn_b_in_proj_b,
           L0_attn_b_out_proj_w, L0_attn_b_out_proj_b,
           L0_ffn_w_gate, L0_ffn_w_up, L0_ffn_w_down):
    x = np.ascontiguousarray(np.asarray(L0_attn_a_x, dtype=np.float32))
    shared = host_inputs(L0_attn_a_w_qkv, L0_attn_a_w_proj, L0_attn_a_b_proj,
                         L0_attn_b_proj_qk_w, L0_attn_b_proj_qk_b,
                         L0_attn_b_proj_v_w, L0_attn_b_proj_v_b,
                         L0_attn_b_in_proj_w, L0_attn_b_in_proj_b,
                         L0_attn_b_out_proj_w, L0_attn_b_out_proj_b,
                         L0_ffn_w_gate, L0_ffn_w_up, L0_ffn_w_down)
    nc = build()
    in_maps = []
    for c in range(8):
        xi = np.ascontiguousarray(
            x[2 * c:2 * c + 2].transpose(0, 2, 1)).reshape(NSEQ, CH, 128, S)
        m = dict(shared)
        m["x"] = xi
        in_maps.append(m)
    res = run_bass_kernel_spmd(nc, in_maps, core_ids=list(range(8)))
    out = np.empty((16, S, D), np.float32)
    for c in range(8):
        y = np.asarray(res.results[c]["y"])
        out[2 * c:2 * c + 2] = y.reshape(NSEQ, D, S).transpose(0, 2, 1)
    return out


# version 09
# speedup vs baseline: 1.0539x; 1.0539x over previous
import numpy as np
from contextlib import ExitStack

import concourse.bass as bass
import concourse.mybir as mybir
from concourse.bass_utils import run_bass_kernel_spmd

F32 = mybir.dt.float32
BF16 = mybir.dt.bfloat16
I32 = mybir.dt.int32
AF = mybir.ActivationFunctionType
ALU = mybir.AluOpType

D = 1024
S = 2048
NSEQ = 2
CH = 8
TT = 512
NG = S // TT
HHC = 24
A_SCALE = 0.125
B_SCALE = float(1.0 / np.sqrt(128.0))
DEQ = float(1.0 / 127.0)
ENGINES = ("tensor", "vector", "scalar", "gpsimd", "sync")

MC_BPROJ, MC_BQK, MC_BV1, MC_BQ2, MC_BK2, MC_BV2, MC_BOUT = 0, 8, 16, 24, 32, 40, 48
MC_EPS, MC_NEG8, MC_BCOMB, MC_RS = 56, 57, 58, 66


def _marks(ws):
    return [m for m in ws if m is not None and m[0] is not None]


class Sched:
    def __init__(self, sems):
        self.sems = sems
        self.ops = {e: [] for e in ENGINES}
        self.cnt = {k: 0 for k in sems}
        self.lastwait = {}
        self.rd = {}
        self.tw = {}
        self.tr = {}
        self.gq = 0

    def op(self, eng, fn, waits=(), inc=None, n=1):
        w = []
        for m in _marks(waits):
            sname, v = m
            assert v <= self.cnt[sname], (sname, v, self.cnt[sname])
            if self.lastwait.get((eng, sname), -1) < v:
                w.append((sname, v))
                self.lastwait[(eng, sname)] = v
        if inc is not None:
            self.cnt[inc] += n
        post = self.cnt[inc] if inc is not None else None
        sems = self.sems

        def emit(e, _fn=fn, _w=tuple(w), _inc=inc, _n=n):
            for sname, v in _w:
                e.wait_ge(sems[sname], v)
            instr = _fn(e) if _fn is not None else None
            if _inc is not None:
                instr.then_inc(sems[_inc], _n)
        self.ops[eng].append(emit)
        return (inc, post) if inc is not None else None

    def wqw(self, key):
        return self.tr.pop(key, []) + ([self.tw[key]] if key in self.tw else [])

    def wrote(self, key, mark):
        if mark is not None:
            self.tw[key] = mark

    def rqw(self, key):
        return [self.tw[key]] if key in self.tw else []

    def read(self, key, mark):
        if mark is not None:
            self.tr.setdefault(key, []).append(mark)


def mm_group(sch, mms, banks, reads=(), waits=(), inc="t"):
    w = list(waits)
    for b in banks:
        w += sch.wqw(("ps", b))
    for k in reads:
        w += sch.rqw(k)
    n = len(mms)
    mark = None
    for i, (o, l, r) in enumerate(mms):
        mark = sch.op(
            "tensor",
            (lambda _o=o, _l=l, _r=r, _st=(i == 0), _sp=(i == n - 1):
             (lambda e: e.matmul(_o, _l, _r, start=_st, stop=_sp)))(),
            waits=(w if i == 0 else ()),
            inc=(inc if i == n - 1 else None))
    for b in banks:
        sch.wrote(("ps", b), mark)
    for k in reads:
        sch.read(k, mark)
    return mark


def ev(sch, eng, fn, rbanks=(), wkeys=(), rkeys=(), waits=(), inc=None, n=1):
    w = list(waits)
    for b in rbanks:
        w += sch.rqw(("ps", b))
    for k in wkeys:
        w += sch.wqw(k)
    for k in rkeys:
        w += sch.rqw(k)
    mark = sch.op(eng, fn, waits=w, inc=inc, n=n)
    for b in rbanks:
        sch.read(("ps", b), mark)
    for k in wkeys:
        sch.wrote(k, mark)
    for k in rkeys:
        sch.read(k, mark)
    return mark


def dma(sch, eng, fn, wkeys=(), rkeys=(), waits=()):
    if eng == "gpsimd":
        sname = "g%d" % sch.gq
        sch.gq = (sch.gq + 1) % 4
    else:
        sname = "y"
    w = list(waits) + [(sname, sch.cnt[sname])]
    return ev(sch, eng, fn, wkeys=wkeys, rkeys=rkeys, waits=w, inc=sname, n=16)


def build(taps=None):
    taps = taps or {}
    nc = bass.Bass("TRN2", target_bir_lowering=False)

    d_x = nc.dram_tensor("x", [NSEQ, CH, 128, S], F32, kind="ExternalInput").ap()
    d_wqk = nc.dram_tensor("wqk", [CH, 2, 128, 1024], F32, kind="ExternalInput").ap()
    d_wvT = nc.dram_tensor("wvT", [CH, 128, 1024], F32, kind="ExternalInput").ap()
    d_wpc = nc.dram_tensor("wpc", [CH, 128, 1024], F32, kind="ExternalInput").ap()
    d_wg = nc.dram_tensor("wg", [HHC, 128, 1024], I32, kind="ExternalInput").ap()
    d_wu = nc.dram_tensor("wu", [HHC, 128, 1024], I32, kind="ExternalInput").ap()
    d_wdc = nc.dram_tensor("wdc", [CH, 128, 3072], I32, kind="ExternalInput").ap()
    d_wb = [nc.dram_tensor(f"wb{i}", [CH, 128, 1024], F32, kind="ExternalInput").ap()
            for i in range(6)]
    d_misc = nc.dram_tensor("misc", [128, 58], F32, kind="ExternalInput").ap()
    d_cbf = nc.dram_tensor("cbf", [128, 256], F32, kind="ExternalInput").ap()
    d_y = nc.dram_tensor("y", [NSEQ, CH, 128, S], F32, kind="ExternalOutput").ap()
    d_taps = {name: nc.dram_tensor("tap_" + name, shape, dt, kind="ExternalOutput").ap()
              for name, (shape, dt) in taps.items()}

    with ExitStack() as ctx:
        XF = ctx.enter_context(nc.sbuf_tensor("XF", [128, CH * S], F32))
        XN = ctx.enter_context(nc.sbuf_tensor("XN", [128, CH * S], BF16))
        AR = ctx.enter_context(nc.sbuf_tensor("AR", [128, 24 * S], BF16))
        WB = ctx.enter_context(nc.sbuf_tensor("WB", [128, 4 * 1024], BF16))
        SQ = ctx.enter_context(nc.sbuf_tensor("SQ", [128, 1024], BF16))
        ROW = ctx.enter_context(nc.sbuf_tensor("ROW", [1, 1536], BF16))
        ROWF = ctx.enter_context(nc.sbuf_tensor("ROWF", [1, 512], F32))
        MISC = ctx.enter_context(nc.sbuf_tensor("MISC", [128, 72], F32))
        CBF = ctx.enter_context(nc.sbuf_tensor("CBF", [128, 256], BF16))
        BV2 = ctx.enter_context(nc.sbuf_tensor("BV2", [128, 8], BF16))
        PS = ctx.enter_context(nc.psum_tensor("PS", [128, 4096], F32))
        sems = {k: ctx.enter_context(nc.semaphore("sem_" + k))
                for k in ("t", "v", "s", "y", "g0", "g1", "g2", "g3")}
        sch = Sched(sems)

        def ar(c0, n):
            return AR[:, c0:c0 + n]

        def xf(c, t0, n):
            return XF[:, S * c + t0: S * c + t0 + n]

        def xn(c, t0, n):
            return XN[:, S * c + t0: S * c + t0 + n]

        def sl(i, off, n):
            return XN[:, 4096 * i + off: 4096 * i + off + n]

        def wbs(s_, off=0, n=1024):
            return WB[:, 1024 * s_ + off: 1024 * s_ + off + n]

        def bank(b, r0=0, r1=128, c0=0, n=512):
            return PS[r0:r1, 512 * b + c0: 512 * b + c0 + n]

        def mcol(c, n=1):
            return MISC[:, c:c + n]

        ones_col = CBF[:, 128:129]
        ones_row = CBF[0:1, 128:256]
        mask_ap = bass.AP(CBF, 0, [[256, 128], [0, 8], [1, 128]])

        G_B, V_B, M_B, PT_B, WVT_B, O_B = 0, S * 8, S * 16, S * 16, S * 20, S * 20

        def arslot(col0, ncols):
            s0, s1 = col0 // S, (col0 + ncols - 1) // S
            return [("ar", s_) for s_ in range(s0, s1 + 1)]

        def xnslot(col0, ncols):
            s0, s1 = col0 // 512, (col0 + ncols - 1) // 512
            return [("xnc", s_) for s_ in range(s0, s1 + 1)]

        def tap(name, src_ap):
            if name not in d_taps:
                return
            waits = [(s_, sch.cnt[s_]) for s_ in sch.cnt]
            dma(sch, "gpsimd",
                (lambda _s=src_ap, _d=d_taps[name]: (lambda e: e.dma_start(out=_d, in_=_s)))(),
                waits=waits)
            for e in ENGINES:
                sch.op(e, None, waits=[(s_, sch.cnt[s_]) for s_ in sch.cnt])

        mi = dma(sch, "sync", lambda e: e.dma_start(out=MISC[:, 0:58], in_=d_misc))
        cb = dma(sch, "gpsimd", lambda e: e.dma_start(out=CBF[:, :], in_=d_cbf))
        sch.rd["misc"] = mi
        sch.rd["cbf"] = cb
        bv2m = sch.op("vector", lambda e: e.tensor_copy(BV2[:, :], MISC[:, MC_BV2:MC_BV2 + 8]),
                      waits=[mi], inc="v")
        sch.rd["bv2"] = bv2m

        def ln_tile(src_t0, dst_fn, dst_keys_fn):
            casts = []
            for c in range(CH):
                m = ev(sch, "vector",
                       (lambda _c=c: (lambda e: e.tensor_copy(
                           dst_fn(_c), xf(_c, src_t0, TT))))(),
                       wkeys=dst_keys_fn(c), rkeys=[("xfc", c)], inc="v")
                casts.append(m)
            mm_group(sch, [(bank(6, 0, 1), ones_col, dst_fn(c)) for c in range(CH)],
                     banks=[6], waits=[casts[-1], sch.rd["cbf"]])
            s1m = ("t", sch.cnt["t"])
            mubf = ev(sch, "scalar", lambda e: e.activation(
                ROW[0:1, 0:512], bank(6, 0, 1), AF.Copy, scale=1.0 / 1024.0),
                rbanks=[6], wkeys=[("row", 0)], waits=[s1m], inc="s")
            mm_group(sch, [(bank(6), ones_row, ROW[0:1, 0:512])], banks=[6],
                     reads=[("row", 0)], waits=[mubf])
            bcm1 = ("t", sch.cnt["t"])
            tmarks = []
            for c in range(CH):
                m = ev(sch, "vector",
                       (lambda _c=c: (lambda e: e.scalar_tensor_tensor(
                           dst_fn(_c), xf(_c, src_t0, TT), 1.0, bank(6),
                           ALU.bypass, ALU.subtract)))(),
                       rbanks=[6], rkeys=[("xfc", c)], waits=[bcm1], inc="v")
                for k in dst_keys_fn(c):
                    sch.wrote(k, m)
                tmarks.append(m)
            for c in range(CH):
                sqk = ("sq", c % 2)
                sqh = SQ[:, 512 * (c % 2):512 * (c % 2) + 512]
                sqm = ev(sch, "scalar",
                         (lambda _c=c, _h=sqh: (lambda e: e.activation(
                             _h, dst_fn(_c), AF.Square)))(),
                         wkeys=[sqk], waits=[tmarks[c]], inc="s")
                w = [sqm] + (sch.wqw(("ps", 7)) if c == 0 else [])
                mm = sch.op("tensor",
                            (lambda _h=sqh, _st=(c == 0), _sp=(c == CH - 1):
                             (lambda e: e.matmul(bank(7, 0, 1), ones_col, _h,
                                                 start=_st, stop=_sp)))(),
                            waits=w, inc="t")
                sch.read(sqk, mm)
            s2m = ("t", sch.cnt["t"])
            sch.wrote(("ps", 7), s2m)
            lnv = ev(sch, "scalar", lambda e: e.activation(
                ROWF[0:1, :], bank(7, 0, 1), AF.Ln, scale=1.0 / 1024.0,
                bias=MISC[0:1, MC_EPS:MC_EPS + 1]),
                rbanks=[7], wkeys=[("rowf",)], waits=[s2m, sch.rd["misc"]], inc="s")
            rr = ev(sch, "scalar", lambda e: e.activation(
                ROW[0:1, 512:1024], ROWF[0:1, :], AF.Exp, scale=-0.5),
                rkeys=[("rowf",)], wkeys=[("row", 1)], waits=[lnv], inc="s")
            mm_group(sch, [(bank(7), ones_row, ROW[0:1, 512:1024])], banks=[7],
                     reads=[("row", 1)], waits=[rr])
            bcm2 = ("t", sch.cnt["t"])
            rsb = ev(sch, "vector", lambda e: e.tensor_copy(SQ[:, 0:512], bank(7)),
                     rbanks=[7], wkeys=[("sq", 0)], waits=[bcm2], inc="v")
            for c in range(CH):
                t2 = ev(sch, "vector",
                        (lambda _c=c: (lambda e: e.tensor_tensor(
                            dst_fn(_c), dst_fn(_c), SQ[:, 0:512], ALU.mult)))(),
                        rkeys=[("sq", 0)], waits=[rsb], inc="v")
                for k in dst_keys_fn(c):
                    sch.wrote(k, t2)

        def proj_fm(wtile_fn, wread_keys, src_fn, src_keys_fn, dst_fn, dst_keys_fn,
                    bias_col, banks, waits=()):
            for m in range(CH):
                b = banks[m % len(banks)]
                rds = list(wread_keys)
                for k in range(CH):
                    rds += src_keys_fn(k)
                gm = mm_group(sch, [(bank(b), wtile_fn(k, m), src_fn(k))
                                    for k in range(CH)],
                              banks=[b], reads=rds, waits=list(waits))
                ev(sch, "scalar",
                   (lambda _m=m, _b=b, _bc=bias_col: (lambda e: e.activation(
                       dst_fn(_m), bank(_b), AF.Identity, bias=mcol(_bc + _m))))(),
                   rbanks=[b], wkeys=dst_keys_fn(m),
                   waits=[gm, sch.rd["misc"]], inc="s")

        for s_i in range(NSEQ):
            xw = []
            for c in range(CH):
                xw += sch.wqw(("xfc", c))
            xld = dma(sch, "sync",
                      (lambda _s=s_i: (lambda e: e.dma_start(
                          out=bass.AP(XF, 0, [[CH * S, 128], [S, CH], [1, S]]),
                          in_=d_x[_s].rearrange("c p t -> p c t"))))(),
                      waits=xw)
            for c in range(CH):
                sch.wrote(("xfc", c), xld)

            groups = [(ec, eh) for ec in range(CH) for eh in range(2)]
            waves = [groups[0:6], groups[6:12], groups[12:16]]

            wqld = dma(sch, "gpsimd",
                       lambda e: e.dma_start(
                           out=bass.AP(AR, 0, [[24 * S, 128], [2048, CH], [1, 2048]]),
                           in_=d_wqk.rearrange("c w p e -> p c w e")),
                       wkeys=[("ar", s_) for s_ in range(8)])

            def emit_mwave(wave):
                bank_w = {gi: sch.wqw(("ps", gi)) for gi in range(len(wave))}
                for oc in range(CH):
                    mk = None
                    for gi, (ec, eh) in enumerate(wave):
                        mk = sch.op("tensor",
                                    (lambda _oc=oc, _ec=ec, _eh=eh, _b=gi,
                                     _st=(oc == 0), _sp=(oc == CH - 1):
                                     (lambda e: e.matmul(
                                         bank(_b),
                                         ar(2048 * _oc + 128 * _ec, 128),
                                         ar(2048 * _oc + 1024 + 512 * _eh, 512),
                                         start=_st, stop=_sp)))(),
                                    waits=(([wqld] if gi == 0 and oc == 0 else []) +
                                           (bank_w[gi] if oc == 0 else [])),
                                    inc=("t" if gi == len(wave) - 1 else None))
                for gi, (ec, eh) in enumerate(wave):
                    mkk = ("t", sch.cnt["t"])
                    sch.wrote(("ps", gi), mkk)
                    for s_ in range(8):
                        sch.read(("ar", s_), mkk)
                    ev(sch, "vector",
                       (lambda _ec=ec, _eh=eh, _b=gi: (lambda e: e.tensor_copy(
                           ar(M_B + 1024 * _ec + 512 * _eh, 512), bank(_b))))(),
                       rbanks=[gi], wkeys=arslot(M_B + 1024 * ec + 512 * eh, 512),
                       inc="v")

            def emit_ln1(g):
                ln_tile(TT * g,
                        (lambda _g=g: (lambda c: xn(c, TT * _g, TT)))(),
                        (lambda _g=g: (lambda c: xnslot(S * c + TT * _g, TT)))())

            def emit_vproj(tg):
                grp = [(tt, h2) for tt in range(4) for h2 in range(2)]
                for wave in (grp[:6], grp[6:]):
                    for gi, (tt, h2) in enumerate(wave):
                        b = gi
                        rds = [k for kk in range(CH) for k in xnslot(S * kk + TT * tg, TT)]
                        rds += arslot(WVT_B, 8192)
                        gm = mm_group(sch,
                                      [(bank(b), xn(k, TT * tg + 128 * tt, 128),
                                        ar(WVT_B + 1024 * k + 512 * h2, 512))
                                       for k in range(CH)],
                                      banks=[b], reads=rds, waits=[wvld])
                        ev(sch, "scalar",
                           (lambda _tg=tg, _tt=tt, _h2=h2, _b=b: (lambda e: e.activation(
                               ar(V_B + 1024 * (4 * _tg + _tt) + 512 * _h2, 512),
                               bank(_b), AF.Copy)))(),
                           rbanks=[b],
                           wkeys=arslot(V_B + 1024 * (4 * tg + tt) + 512 * h2, 512),
                           waits=[gm], inc="s")

            emit_mwave(waves[0])
            emit_ln1(0)
            emit_mwave(waves[1])
            emit_ln1(1)
            emit_mwave(waves[2])
            if s_i == 0:
                tap("m", AR[:, M_B:M_B + 8192])
            wvld = dma(sch, "gpsimd",
                       lambda e: e.dma_start(
                           out=bass.AP(AR, WVT_B, [[24 * S, 128], [1024, CH], [1, 1024]]),
                           in_=d_wvT.rearrange("c p e -> p c e")),
                       wkeys=arslot(WVT_B, 8192))
            emit_ln1(2)
            emit_vproj(0)
            emit_ln1(3)
            if s_i == 0:
                tap("xn1", XN[:, :])
            for tg in range(1, NG):
                emit_vproj(tg)
            if s_i == 0:
                tap("v", AR[:, V_B:V_B + 16384])
            def emit_negc(g):
                for c in range(CH):
                    sqk = ("sq", c % 2)
                    sqh = SQ[:, 512 * (c % 2):512 * (c % 2) + 512]
                    sqm = ev(sch, "scalar",
                             (lambda _c=c, _g=g, _h=sqh: (lambda e: e.activation(
                                 _h, ar(G_B + S * _c + TT * _g, TT), AF.Square)))(),
                             wkeys=[sqk], rkeys=arslot(G_B + S * c + TT * g, TT),
                             inc="s")
                    w = [sqm] + (sch.wqw(("ps", 6)) if c == 0 else [])
                    mm = sch.op("tensor",
                                (lambda _h=sqh, _st=(c == 0), _sp=(c == CH - 1):
                                 (lambda e: e.matmul(bank(6, 0, 1), ones_col, _h,
                                                     start=_st, stop=_sp)))(),
                                waits=w, inc="t")
                    sch.read(sqk, mm)
                gsm = ("t", sch.cnt["t"])
                sch.wrote(("ps", 6), gsm)
                l1 = ev(sch, "scalar", lambda e: e.activation(
                    ROWF[0:1, :], bank(6, 0, 1), AF.Ln,
                    bias=MISC[0:1, MC_EPS:MC_EPS + 1]),
                    rbanks=[6], wkeys=[("rowf",)],
                    waits=[gsm, sch.rd["misc"]], inc="s")
                l2 = ev(sch, "scalar", lambda e: e.activation(
                    bank(6, 0, 1), ROWF[0:1, :], AF.Exp, scale=0.5),
                    rbanks=[6], rkeys=[("rowf",)], waits=[l1], inc="s")
                sch.wrote(("ps", 6), l2)
                return ev(sch, "vector",
                          (lambda _g=g: (lambda e: e.tensor_scalar(
                              ROW[0:1, 512 * (_g % 2):512 * (_g % 2) + 512],
                              bank(6, 0, 1), -32.2, None, ALU.mult)))(),
                          rbanks=[6], wkeys=[("row", g % 2)], waits=[l2], inc="v")

            ngc_marks = {}
            for g in range(NG):
                for m in range(CH):
                    b = m % 6
                    rds = arslot(M_B, 8192) + \
                        [k for kk in range(CH) for k in xnslot(S * kk + TT * g, TT)]
                    gm = mm_group(sch,
                                  [(bank(b),
                                    ar(M_B + 1024 * k, 1024)[:, 128 * m:128 * m + 128],
                                    xn(k, TT * g, TT)) for k in range(CH)],
                                  banks=[b], reads=rds)
                    ev(sch, "scalar",
                       (lambda _g=g, _m=m, _b=b: (lambda e: e.activation(
                           ar(G_B + S * _m + TT * _g, TT), bank(_b), AF.Copy)))(),
                       rbanks=[b], wkeys=arslot(G_B + S * m + TT * g, TT),
                       waits=[gm], inc="s")
                if g < 2 and m == CH - 1:
                    ngc_marks[g] = emit_negc(g)

            if s_i == 0:
                tap("g", AR[:, 0:16384])
            for g in range(NG):
                ngc = ngc_marks[g]
                for kt in range(16):
                    b = kt % 2
                    rds = [k for kk in range(CH) for k in xnslot(S * kk, S)]
                    rds += [("ar", kk) for kk in range(CH)] + [("row", g % 2)]
                    mms = [(bank(b), xn(k, 128 * kt, 128),
                            ar(G_B + S * k + TT * g, TT)) for k in range(CH)]
                    mms.append((bank(b), ones_row,
                                ROW[0:1, 512 * (g % 2):512 * (g % 2) + 512]))
                    gm = mm_group(sch, mms, banks=[b], reads=rds, waits=[ngc])
                    ev(sch, "scalar",
                       (lambda _kt=kt, _b=b: (lambda e: e.activation(
                           ar(PT_B + 512 * _kt, 512), bank(_b), AF.Exp,
                           scale=A_SCALE)))(),
                       rbanks=[b], wkeys=arslot(PT_B + 512 * kt, 512),
                       waits=[gm], inc="s")
                expm = ("s", sch.cnt["s"])
                if g + 2 < NG:
                    ngc_marks[g + 2] = emit_negc(g + 2)
                if s_i == 0 and g == 0:
                    tap("pt0", AR[:, PT_B:PT_B + 8192])
                    tap("row0", ROW[0:1, :])
                for j in range(4):
                    ob = 2 if (j % 2 == 0) else 4
                    w = sch.wqw(("ps", ob)) + sch.wqw(("ps", ob + 1)) + \
                        sch.wqw(("ps", 1)) + [expm]
                    rds = arslot(PT_B, 8192) + arslot(V_B, 16384)
                    for k in rds:
                        w += sch.rqw(k)
                    for kt in range(16):
                        lhs = ar(PT_B + 512 * kt + 128 * j, 128)
                        trip = [(PS[:, 512 * ob:512 * ob + 512], lhs,
                                 ar(V_B + 1024 * kt, 512)),
                                (PS[:, 512 * ob + 512:512 * ob + 1024], lhs,
                                 ar(V_B + 1024 * kt + 512, 512)),
                                (PS[:, 512 + j:512 + j + 1], lhs, ones_col)]
                        for u, (o, l, r) in enumerate(trip):
                            sch.op("tensor",
                                   (lambda _o=o, _l=l, _r=r, _st=(kt == 0),
                                    _sp=(kt == 15):
                                    (lambda e: e.matmul(_o, _l, _r, start=_st,
                                                        stop=_sp)))(),
                                   waits=(w if (kt == 0 and u == 0) else []),
                                   inc=("t" if (kt == 15 and u == 2) else None))
                    pvm = ("t", sch.cnt["t"])
                    sch.wrote(("ps", ob), pvm)
                    sch.wrote(("ps", ob + 1), pvm)
                    sch.wrote(("ps", 1), pvm)
                    for k in rds:
                        sch.read(k, pvm)
                    rc = ev(sch, "vector",
                            (lambda _j=j: (lambda e: e.reciprocal(
                                mcol(MC_RS + _j % 2), PS[:, 512 + _j:512 + _j + 1])))(),
                            rbanks=[1], wkeys=[("rs", j % 2)], inc="v")
                    jj = 4 * g + j
                    r_i = jj % 8
                    ev(sch, "scalar",
                       (lambda _r=r_i, _ob=ob, _j=j: (lambda e: e.activation(
                           ar(O_B + 1024 * _r, 1024),
                           PS[:, 512 * _ob:512 * _ob + 1024], AF.Identity,
                           scale=mcol(MC_RS + _j % 2))))(),
                       rbanks=[ob, ob + 1], rkeys=[("rs", j % 2)],
                       wkeys=arslot(O_B + 1024 * r_i, 1024), waits=[rc], inc="s")

                if g in (1, 3):
                    par = 0 if g == 1 else 1
                    for ec in range(CH):
                        slot = ec % 4
                        dm = dma(sch, "gpsimd",
                                 (lambda _ec=ec, _sl=slot: (lambda e: e.dma_start(
                                     out=wbs(_sl), in_=d_wpc[_ec])))(),
                                 wkeys=[("wb", slot)])
                        ob = 2 if (ec % 2 == 0) else 4
                        w = sch.wqw(("ps", ob)) + sch.wqw(("ps", ob + 1)) + [dm]
                        rds = arslot(O_B, 8192) + [("wb", slot)]
                        for k in rds:
                            w += sch.rqw(k)
                        for rc_ in range(CH):
                            for ih in range(2):
                                sch.op("tensor",
                                       (lambda _sl=slot, _rc=rc_, _ih=ih, _ob=ob:
                                        (lambda e: e.matmul(
                                            PS[:, 512 * (_ob + _ih):512 * (_ob + _ih) + 512],
                                            wbs(_sl, 128 * _rc, 128),
                                            ar(O_B + 1024 * _rc + 512 * _ih, 512),
                                            start=(_rc == 0), stop=(_rc == CH - 1))))(),
                                       waits=(w if (rc_ == 0 and ih == 0) else []),
                                       inc=("t" if (rc_ == CH - 1 and ih == 1) else None))
                        opm = ("t", sch.cnt["t"])
                        sch.wrote(("ps", ob), opm)
                        sch.wrote(("ps", ob + 1), opm)
                        for k in rds:
                            sch.read(k, opm)
                        ev(sch, "vector",
                           (lambda _ec=ec, _ob=ob, _par=par: (lambda e: e.scalar_tensor_tensor(
                               bass.AP(XF, S * _ec + _par, [[CH * S, 128], [2, 1024]]),
                               PS[:, 512 * _ob:512 * _ob + 1024],
                               mcol(MC_BPROJ + _ec),
                               bass.AP(XF, S * _ec + _par, [[CH * S, 128], [2, 1024]]),
                               ALU.add, ALU.add)))(),
                           rbanks=[ob, ob + 1], wkeys=[("xfc", ec)],
                           waits=[opm, sch.rd["misc"]], inc="v")

            if s_i == 0:
                tap("o", AR[:, O_B:O_B + 8192])
                tap("xf1", XF[:, :])
            def ffn_layer(tag):
                def f_ln(g):
                    ln_tile(TT * g,
                            (lambda _g=g: (lambda c: xn(c, TT * _g, TT)))(),
                            (lambda _g=g: (lambda c: xnslot(S * c + TT * _g, TT)))())

                dm_marks = {}

                def f_dm(c):
                    slot = 2 * (c % 2)
                    dmg = dma(sch, "gpsimd",
                              (lambda _c=c, _sl=slot: (lambda e: e.dma_start(
                                  out=wbs(_sl), in_=d_wg[_c])))(),
                              wkeys=[("wb", slot)])
                    dmu = dma(sch, "gpsimd",
                              (lambda _c=c, _sl=slot: (lambda e: e.dma_start(
                                  out=wbs(_sl + 1), in_=d_wu[_c])))(),
                              wkeys=[("wb", slot + 1)])
                    dm_marks[c] = (slot, dmg, dmu)

                def f_unit(c, g):
                    if c not in dm_marks:
                        f_dm(c)
                    slot, dmg, dmu = dm_marks[c]
                    bg = 2 * (g % 2)
                    bu = bg + 1
                    rds = [("wb", slot)] +                         [k for kk in range(CH) for k in xnslot(S * kk + TT * g, TT)]
                    gmg = mm_group(sch,
                                   [(bank(bg), wbs(slot, 128 * k, 128),
                                     xn(k, TT * g, TT)) for k in range(CH)],
                                   banks=[bg], reads=rds, waits=[dmg])
                    gmu = mm_group(sch,
                                   [(bank(bu), wbs(slot + 1, 128 * k, 128),
                                     xn(k, TT * g, TT)) for k in range(CH)],
                                   banks=[bu], reads=[("wb", slot + 1)],
                                   waits=[dmu])
                    sq_i = (2 * c + g) % 2
                    sqh = SQ[:, 512 * sq_i:512 * sq_i + 512]
                    sil = ev(sch, "scalar",
                             (lambda _bg=bg, _h=sqh: (lambda e: e.activation(
                                 _h, bank(_bg), AF.Silu, scale=DEQ)))(),
                             rbanks=[bg], wkeys=[("sq", sq_i)],
                             waits=[gmg], inc="s")
                    ev(sch, "vector",
                       (lambda _bu=bu, _h=sqh, _c=c, _g=g:
                        (lambda e: e.scalar_tensor_tensor(
                            ar(S * _c + TT * _g, TT), bank(_bu), DEQ,
                            _h, ALU.mult, ALU.mult)))(),
                       rbanks=[bu], rkeys=[("sq", sq_i)],
                       wkeys=arslot(S * c + TT * g, TT),
                       waits=[sil, gmu], inc="v")

                f_ln(0)
                order = [(0, 0), "L1", (0, 1), (1, 0), (1, 1), "L2",
                         (0, 2), (1, 2), "L3", (0, 3), (1, 3)]
                order += [(c, g) for c in range(2, HHC) for g in range(NG)]
                for u in order:
                    if u == "L1":
                        f_ln(1)
                    elif u == "L2":
                        f_ln(2)
                    elif u == "L3":
                        f_ln(3)
                    else:
                        f_unit(*u)
                if s_i == 0:
                    tap(tag + "xn", XN[:, :])
                if s_i == 0:
                    tap(tag + "h", AR[:, :])
                for ec in range(CH):
                    pieces = []
                    for i in range(3):
                        p_i = 3 * ec + i
                        slot = p_i % 4
                        dm = dma(sch, "gpsimd",
                                 (lambda _ec=ec, _i=i, _sl=slot: (lambda e: e.dma_start(
                                     out=wbs(_sl),
                                     in_=d_wdc[_ec][:, 1024 * _i:1024 * _i + 1024])))(),
                                 wkeys=[("wb", slot)])
                        pieces.append((slot, dm))
                    bb = 4 * (ec % 2)
                    w0 = sch.wqw(("ps", bb)) + sch.wqw(("ps", bb + 1)) + \
                        sch.wqw(("ps", bb + 2)) + sch.wqw(("ps", bb + 3))
                    rds = [("wb", pieces[i][0]) for i in range(3)] + \
                        [("ar", s_) for s_ in range(24)]
                    for c in range(HHC):
                        slot, dm = pieces[c // 8]
                        wchunk = []
                        if c % 8 == 0:
                            wchunk.append(dm)
                        wchunk += sch.rqw(("ar", (S * c) // S))
                        if c == 0:
                            wchunk += w0
                        for g in range(NG):
                            sch.op("tensor",
                                   (lambda _sl=slot, _c=c, _g=g, _b=bb + g:
                                    (lambda e: e.matmul(
                                        bank(_b), wbs(_sl, 128 * (_c % 8), 128),
                                        ar(S * _c + TT * _g, TT),
                                        start=(_c == 0), stop=(_c == HHC - 1))))(),
                                   waits=(wchunk if g == 0 else []),
                                   inc=("t" if (c == HHC - 1 and g == NG - 1) else None))
                    dwm = ("t", sch.cnt["t"])
                    for b_ in range(4):
                        sch.wrote(("ps", bb + b_), dwm)
                    for k in rds:
                        sch.read(k, dwm)
                    for g in range(NG):
                        ev(sch, "vector",
                           (lambda _ec=ec, _g=g, _b=bb + g: (lambda e: e.scalar_tensor_tensor(
                               xf(_ec, TT * _g, TT), bank(_b), DEQ,
                               xf(_ec, TT * _g, TT), ALU.mult, ALU.add)))(),
                           rbanks=[bb + g], wkeys=[("xfc", ec)],
                           waits=[dwm], inc="v")

            ffn_layer("f1")
            if s_i == 0:
                tap("xf2", XF[:, :])

            wbld = []
            for i in range(6):
                m = dma(sch, "gpsimd",
                        (lambda _i=i: (lambda e: e.dma_start(
                            out=bass.AP(AR, S * 4 * _i,
                                        [[24 * S, 128], [1024, CH], [1, 1024]]),
                            in_=d_wb[_i].rearrange("c p e -> p c e"))))(),
                        wkeys=[("ar", s_) for s_ in range(4 * i, 4 * i + 4)])
                wbld.append(m)
            for blk in range(NG):
                t0 = TT * blk
                ln_tile(t0, (lambda c: sl(0, 512 * c, 512)),
                        lambda c: xnslot(512 * c, 512))
                proj_fm(lambda k, m: ar(1024 * k + 128 * m, 128),
                        [("ar", kk // 2) for kk in range(CH)],
                        lambda k: sl(0, 512 * k, 512),
                        lambda k: xnslot(512 * k, 512),
                        lambda m: sl(1, 512 * m, 512),
                        lambda m: xnslot(4096 + 512 * m, 512),
                        MC_BQK, banks=(0, 1), waits=[wbld[0]])
                proj_fm(lambda k, m: ar(S * 4 + 1024 * k + 128 * m, 128),
                        [("ar", 4 + kk // 2) for kk in range(CH)],
                        lambda k: sl(0, 512 * k, 512),
                        lambda k: xnslot(512 * k, 512),
                        lambda m: sl(2, 512 * m, 512),
                        lambda m: xnslot(8192 + 512 * m, 512),
                        MC_BV1, banks=(2, 3), waits=[wbld[1]])
                proj_fm(lambda k, m: ar(S * 8 + 1024 * k + 128 * m, 128),
                        [("ar", 8 + kk // 2) for kk in range(CH)],
                        lambda k: sl(1, 512 * k, 512),
                        lambda k: xnslot(4096 + 512 * k, 512),
                        lambda m: sl(3, 512 * m, 512),
                        lambda m: xnslot(12288 + 512 * m, 512),
                        MC_BQ2, banks=(0, 1), waits=[wbld[2]])
                proj_fm(lambda k, m: ar(S * 12 + 1024 * k + 128 * m, 128),
                        [("ar", 12 + kk // 2) for kk in range(CH)],
                        lambda k: sl(1, 512 * k, 512),
                        lambda k: xnslot(4096 + 512 * k, 512),
                        lambda m: sl(0, 512 * m, 512),
                        lambda m: xnslot(512 * m, 512),
                        MC_BK2, banks=(2, 3), waits=[wbld[3]])
                if s_i == 0 and blk == 0:
                    tap("bq", sl(3, 0, 4096))
                    tap("bk", sl(0, 0, 4096))
                    tap("bqkin", sl(1, 0, 4096))
                    tap("bvin", sl(2, 0, 4096))
                for tt in range(4):
                    for h2 in range(2):
                        b = (4 * tt + 2 * h2) % 4
                        rds = [("ar", 16 + kk // 2) for kk in range(CH)]
                        for k in range(CH):
                            rds += xnslot(8192 + 512 * k, 512)
                        gm = mm_group(sch,
                                      [(bank(b), sl(2, 512 * k + 128 * tt, 128),
                                        ar(S * 16 + 1024 * k + 512 * h2, 512))
                                       for k in range(CH)],
                                      banks=[b], reads=rds, waits=[wbld[4]])
                        ev(sch, "scalar",
                           (lambda _tt=tt, _h2=h2, _b=b: (lambda e: e.activation(
                               sl(1, 1024 * _tt + 512 * _h2, 512), bank(_b),
                               AF.Copy)))(),
                           rbanks=[b],
                           wkeys=xnslot(4096 + 1024 * tt + 512 * h2, 512),
                           waits=[gm], inc="s")
                if s_i == 0 and blk == 0:
                    tap("bvtm", sl(1, 0, 4096))
                if blk == 0:
                    w = sch.wqw(("ps", 7))
                    for m in range(CH):
                        gm = mm_group(sch,
                                      [(PS[:, 3584 + m:3584 + m + 1],
                                        ar(S * 20 + 1024 * k + 128 * m, 128),
                                        BV2[:, k:k + 1]) for k in range(CH)],
                                      banks=[], reads=[("ar", 20 + m // 2)],
                                      waits=(w if m == 0 else []) + [wbld[5], sch.rd["bv2"]],
                                      inc="t")
                    bcw = ("t", sch.cnt["t"])
                    sch.wrote(("ps", 7), bcw)
                    for m in range(CH):
                        ev(sch, "vector",
                           (lambda _m=m: (lambda e: e.scalar_tensor_tensor(
                               mcol(MC_BCOMB + _m), PS[:, 3584 + _m:3584 + _m + 1], 1.0,
                               MISC[:, MC_BOUT + _m:MC_BOUT + _m + 1], ALU.mult, ALU.add)))(),
                           rbanks=([7] if m == 0 else []),
                           waits=[bcw, sch.rd["misc"]], inc="v")
                    bcomb = ("v", sch.cnt["v"])
                    sch.read(("ps", 7), bcomb)

                def b_scores(jj):
                    pslot = 3 * (jj % 2)
                    w = sch.wqw(("ps", 0)) + sch.wqw(("ps", 1))
                    rds = xnslot(0, 4096) + xnslot(12288, 4096)
                    for h in range(CH):
                        sch.op("tensor",
                               (lambda _h=h, _jj=jj: (lambda e: e.matmul(
                                   PS[:, 128 * _h:128 * _h + 128],
                                   sl(0, 512 * _h + 128 * _jj, 128),
                                   sl(3, 512 * _h + 128 * _jj, 128),
                                   start=True, stop=True)))(),
                               waits=(w + [sch.rqw(k)[0] for k in rds if sch.rqw(k)]
                                      if h == 0 else []),
                               inc=("t" if h == CH - 1 else None))
                    sw = ("t", sch.cnt["t"])
                    sch.wrote(("ps", 0), sw)
                    sch.wrote(("ps", 1), sw)
                    for k in rds:
                        sch.read(k, sw)
                    exm = ev(sch, "scalar",
                             (lambda _ps=pslot: (lambda e: e.activation(
                                 wbs(_ps, 0, 1024), PS[:, 0:1024], AF.Exp,
                                 scale=B_SCALE, bias=MISC[:, MC_NEG8:MC_NEG8 + 1])))(),
                             rbanks=[0, 1], wkeys=[("wb", pslot)],
                             waits=[sw, sch.rd["misc"]], inc="s")
                    if s_i == 0 and blk == 0 and jj == 0:
                        tap("bpraw", wbs(pslot, 0, 1024))
                    return exm

                def b_tail(jj, exm):
                    pslot = 3 * (jj % 2)
                    pmm = ev(sch, "vector",
                             (lambda _ps=pslot: (lambda e: e.scalar_tensor_tensor(
                                 wbs(1, 0, 1024), wbs(_ps, 0, 1024), 0.0,
                                 mask_ap, ALU.bypass, ALU.mult)))(),
                             wkeys=[("wb", 1)], rkeys=[("wb", pslot)],
                             waits=[exm, sch.rd["cbf"]], inc="v")
                    if s_i == 0 and blk == 0 and jj == 0:
                        tap("bpm", wbs(1, 0, 1024))
                    w2 = sch.wqw(("ps", 2)) + sch.wqw(("ps", 3))
                    for h in range(CH):
                        sch.op("tensor",
                               (lambda _h=h: (lambda e: e.matmul(
                                   PS[0:1, 1024 + 128 * _h:1024 + 128 * _h + 128],
                                   ones_col, wbs(1, 128 * _h, 128),
                                   start=True, stop=True)))(),
                               waits=((w2 + [pmm]) if h == 0 else []),
                               inc=("t" if h == CH - 1 else None))
                    smw = ("t", sch.cnt["t"])
                    sch.wrote(("ps", 2), smw)
                    sch.wrote(("ps", 3), smw)
                    sch.read(("wb", 1), smw)
                    rc0 = ev(sch, "vector",
                             lambda e: e.reciprocal(PS[0:1, 2048:3072],
                                                    PS[0:1, 1024:2048]),
                             rbanks=[2, 3], wkeys=[("psrow45",)], waits=[smw] +
                             sch.wqw(("ps", 4)) + sch.wqw(("ps", 5)), inc="v")
                    sch.wrote(("ps", 4), rc0)
                    sch.wrote(("ps", 5), rc0)
                    rcm = ev(sch, "vector",
                             lambda e: e.tensor_copy(WB[0:1, 2048:3072],
                                                     PS[0:1, 2048:3072]),
                             rbanks=[4, 5], wkeys=[("wb", 2)], waits=[rc0], inc="v")
                    mm_group(sch, [(PS[:, 1024:1536], ones_row, WB[0:1, 2048:2560])],
                             banks=[2], reads=[("wb", 2)], waits=[rcm])
                    mm_group(sch, [(PS[:, 1536:2048], ones_row, WB[0:1, 2560:3072])],
                             banks=[3], reads=[("wb", 2)])
                    cbm = ("t", sch.cnt["t"])
                    ptm = ev(sch, "vector",
                             (lambda _ps=pslot: (lambda e: e.tensor_tensor(
                                 wbs(_ps, 0, 1024), wbs(1, 0, 1024),
                                 PS[:, 1024:2048], ALU.mult)))(),
                             rbanks=[2, 3], wkeys=[("wb", pslot)], rkeys=[("wb", 1)],
                             waits=[cbm], inc="v")
                    if s_i == 0 and blk == 0 and jj == 0:
                        tap("brc", WB[0:1, 2048:3072])
                        tap("bpt", wbs(pslot, 0, 1024))
                    w3 = sch.wqw(("ps", 4)) + sch.wqw(("ps", 5))
                    rds3 = xnslot(4096 + 1024 * jj, 1024)
                    for h in range(CH):
                        sch.op("tensor",
                               (lambda _h=h, _jj=jj, _ps=pslot: (lambda e: e.matmul(
                                   PS[:, 2048 + 128 * _h:2048 + 128 * _h + 128],
                                   sl(1, 1024 * _jj + 128 * _h, 128),
                                   wbs(_ps, 128 * _h, 128),
                                   start=True, stop=True)))(),
                               waits=((w3 + [ptm] +
                                       [sch.rqw(k)[0] for k in rds3 if sch.rqw(k)])
                                      if h == 0 else []),
                               inc=("t" if h == CH - 1 else None))
                    ovm = ("t", sch.cnt["t"])
                    sch.wrote(("ps", 4), ovm)
                    sch.wrote(("ps", 5), ovm)
                    sch.read(("wb", pslot), ovm)
                    for k in rds3:
                        sch.read(k, ovm)
                    ev(sch, "scalar",
                       (lambda _jj=jj: (lambda e: e.activation(
                           bass.AP(XN, 4096 * 2 + 128 * _jj,
                                   [[CH * S, 128], [512, 8], [1, 128]]),
                           PS[:, 2048:3072].rearrange("p (h x) -> p h x", x=128),
                           AF.Copy)))(),
                       rbanks=[4, 5], wkeys=xnslot(8192, 4096),
                       waits=[ovm], inc="s")

                exms = {}
                for jj in range(4):
                    exms[jj] = b_scores(jj)
                    if jj >= 1:
                        b_tail(jj - 1, exms[jj - 1])
                b_tail(3, exms[3])
                if s_i == 0 and blk == 0:
                    tap("bo", sl(2, 0, 4096))
                for m in range(CH):
                    b = m % 2
                    rds = [("ar", 20 + kk // 2) for kk in range(CH)]
                    for k in range(CH):
                        rds += xnslot(8192 + 512 * k, 512)
                    gm = mm_group(sch,
                                  [(bank(b), ar(S * 20 + 1024 * k + 128 * m, 128),
                                    sl(2, 512 * k, 512)) for k in range(CH)],
                                  banks=[b], reads=rds, waits=[wbld[5]])
                    ev(sch, "vector",
                       (lambda _m=m, _b=b, _t0=t0: (lambda e: e.scalar_tensor_tensor(
                           xf(_m, _t0, TT), bank(_b), mcol(MC_BCOMB + _m),
                           xf(_m, _t0, TT), ALU.add, ALU.add)))(),
                       rbanks=[b], wkeys=[("xfc", m)],
                       waits=[gm, bcomb], inc="v")

            if s_i == 0:
                tap("xf3", XF[:, :])
            ffn_layer("f2")

            for c in range(CH):
                yst = dma(sch, "sync",
                          (lambda _s=s_i, _c=c: (lambda e: e.dma_start(
                              out=d_y[_s][_c], in_=xf(_c, 0, S))))(),
                          waits=sch.rqw(("xfc", c)))
                sch.read(("xfc", c), yst)

        sch.op("sync", None, waits=[("y", sch.cnt["y"])])

        block = ctx.enter_context(nc.Block())

        @block.tensor
        def _(e):
            for f in sch.ops["tensor"]:
                f(e)

        @block.vector
        def _(e):
            for f in sch.ops["vector"]:
                f(e)

        @block.scalar
        def _(e):
            for f in sch.ops["scalar"]:
                f(e)

        @block.gpsimd
        def _(e):
            for f in sch.ops["gpsimd"]:
                f(e)

        @block.sync
        def _(e):
            for f in sch.ops["sync"]:
                f(e)

    return nc


def host_inputs(L0_attn_a_w_qkv, L0_attn_a_w_proj, L0_attn_a_b_proj,
                L0_attn_b_proj_qk_w, L0_attn_b_proj_qk_b, L0_attn_b_proj_v_w,
                L0_attn_b_proj_v_b, L0_attn_b_in_proj_w, L0_attn_b_in_proj_b,
                L0_attn_b_out_proj_w, L0_attn_b_out_proj_b,
                L0_ffn_w_gate, L0_ffn_w_up, L0_ffn_w_down):
    f = lambda a: np.ascontiguousarray(np.asarray(a, dtype=np.float32))
    ii = lambda a: np.ascontiguousarray(np.asarray(a, dtype=np.int32))
    wqkv = f(L0_attn_a_w_qkv)
    wq_img = wqkv[0:1024].reshape(8, 128, 1024)
    wk_img = wqkv[1024:2048].reshape(8, 128, 1024)
    wqk = np.ascontiguousarray(np.stack([wq_img, wk_img], axis=1))
    wvT = np.ascontiguousarray(wqkv[2048:3072].T).reshape(8, 128, 1024)
    wpT = np.ascontiguousarray(f(L0_attn_a_w_proj).T)
    wpc = np.ascontiguousarray(
        wpT.reshape(8, 128, 8, 128).transpose(2, 1, 0, 3)).reshape(8, 128, 1024)
    wg = np.ascontiguousarray(
        ii(L0_ffn_w_gate).reshape(24, 128, 8, 128).transpose(0, 3, 2, 1)).reshape(24, 128, 1024)
    wu = np.ascontiguousarray(
        ii(L0_ffn_w_up).reshape(24, 128, 8, 128).transpose(0, 3, 2, 1)).reshape(24, 128, 1024)
    wdc = np.ascontiguousarray(
        ii(L0_ffn_w_down).reshape(8, 128, 24, 128).transpose(0, 3, 2, 1)).reshape(8, 128, 3072)
    inw = f(L0_attn_b_in_proj_w)
    wb_list = []
    for W in (f(L0_attn_b_proj_qk_w), f(L0_attn_b_proj_v_w), inw[0:1024],
              inw[1024:2048], inw[2048:3072], f(L0_attn_b_out_proj_w)):
        wb_list.append(np.ascontiguousarray(np.ascontiguousarray(W.T)).reshape(8, 128, 1024))
    misc = np.zeros((128, 58), np.float32)
    inb = f(L0_attn_b_in_proj_b)
    for c0, b in ((MC_BPROJ, f(L0_attn_a_b_proj)), (MC_BQK, f(L0_attn_b_proj_qk_b)),
                  (MC_BV1, f(L0_attn_b_proj_v_b)), (MC_BQ2, inb[0:1024]),
                  (MC_BK2, inb[1024:2048]), (MC_BV2, inb[2048:3072]),
                  (MC_BOUT, f(L0_attn_b_out_proj_b))):
        misc[:, c0:c0 + 8] = b.reshape(8, 128).T
    misc[:, MC_EPS] = 1e-5
    misc[:, MC_NEG8] = -8.0
    cbf = np.zeros((128, 256), np.float32)
    p = np.arange(128)
    m01 = ((p[:, None] // 4 == p[None, :] // 4) & (p[:, None] != p[None, :]))
    cbf[:, 0:128] = m01.astype(np.float32)
    cbf[:, 128:256] = 1.0
    return dict(wqk=wqk, wvT=wvT, wpc=wpc, wg=wg, wu=wu, wdc=wdc,
                wb0=wb_list[0], wb1=wb_list[1], wb2=wb_list[2], wb3=wb_list[3],
                wb4=wb_list[4], wb5=wb_list[5], misc=misc, cbf=cbf)


def kernel(L0_attn_a_x, L0_attn_a_w_qkv, L0_attn_a_w_proj, L0_attn_a_b_proj,
           L0_attn_b_proj_qk_w, L0_attn_b_proj_qk_b, L0_attn_b_proj_v_w,
           L0_attn_b_proj_v_b, L0_attn_b_in_proj_w, L0_attn_b_in_proj_b,
           L0_attn_b_out_proj_w, L0_attn_b_out_proj_b,
           L0_ffn_w_gate, L0_ffn_w_up, L0_ffn_w_down):
    x = np.ascontiguousarray(np.asarray(L0_attn_a_x, dtype=np.float32))
    shared = host_inputs(L0_attn_a_w_qkv, L0_attn_a_w_proj, L0_attn_a_b_proj,
                         L0_attn_b_proj_qk_w, L0_attn_b_proj_qk_b,
                         L0_attn_b_proj_v_w, L0_attn_b_proj_v_b,
                         L0_attn_b_in_proj_w, L0_attn_b_in_proj_b,
                         L0_attn_b_out_proj_w, L0_attn_b_out_proj_b,
                         L0_ffn_w_gate, L0_ffn_w_up, L0_ffn_w_down)
    nc = build()
    in_maps = []
    for c in range(8):
        xi = np.ascontiguousarray(
            x[2 * c:2 * c + 2].transpose(0, 2, 1)).reshape(NSEQ, CH, 128, S)
        m = dict(shared)
        m["x"] = xi
        in_maps.append(m)
    res = run_bass_kernel_spmd(nc, in_maps, core_ids=list(range(8)))
    out = np.empty((16, S, D), np.float32)
    for c in range(8):
        y = np.asarray(res.results[c]["y"])
        out[2 * c:2 * c + 2] = y.reshape(NSEQ, D, S).transpose(0, 2, 1)
    return out


# version 10
# speedup vs baseline: 1.0557x; 1.0017x over previous
import numpy as np
from contextlib import ExitStack

import concourse.bass as bass
import concourse.mybir as mybir
from concourse.bass_utils import run_bass_kernel_spmd

F32 = mybir.dt.float32
BF16 = mybir.dt.bfloat16
I32 = mybir.dt.int32
AF = mybir.ActivationFunctionType
ALU = mybir.AluOpType

D = 1024
S = 2048
NSEQ = 2
CH = 8
TT = 512
NG = S // TT
HHC = 24
A_SCALE = 0.125
B_SCALE = float(1.0 / np.sqrt(128.0))
DEQ = float(1.0 / 127.0)
ENGINES = ("tensor", "vector", "scalar", "gpsimd", "sync")

MC_BPROJ, MC_BQK, MC_BV1, MC_BQ2, MC_BK2, MC_BV2, MC_BOUT = 0, 8, 16, 24, 32, 40, 48
MC_EPS, MC_NEG8, MC_BCOMB, MC_RS = 56, 57, 58, 66
MC_BQ2F, MC_BK2F, MC_BVF = 72, 80, 88


def _marks(ws):
    return [m for m in ws if m is not None and m[0] is not None]


class Sched:
    def __init__(self, sems):
        self.sems = sems
        self.ops = {e: [] for e in ENGINES}
        self.cnt = {k: 0 for k in sems}
        self.lastwait = {}
        self.rd = {}
        self.tw = {}
        self.tr = {}
        self.gq = 0

    def op(self, eng, fn, waits=(), inc=None, n=1):
        w = []
        for m in _marks(waits):
            sname, v = m
            assert v <= self.cnt[sname], (sname, v, self.cnt[sname])
            if self.lastwait.get((eng, sname), -1) < v:
                w.append((sname, v))
                self.lastwait[(eng, sname)] = v
        if inc is not None:
            self.cnt[inc] += n
        post = self.cnt[inc] if inc is not None else None
        sems = self.sems

        def emit(e, _fn=fn, _w=tuple(w), _inc=inc, _n=n):
            for sname, v in _w:
                e.wait_ge(sems[sname], v)
            instr = _fn(e) if _fn is not None else None
            if _inc is not None:
                instr.then_inc(sems[_inc], _n)
        self.ops[eng].append(emit)
        return (inc, post) if inc is not None else None

    def wqw(self, key):
        return self.tr.pop(key, []) + ([self.tw[key]] if key in self.tw else [])

    def wrote(self, key, mark):
        if mark is not None:
            self.tw[key] = mark

    def rqw(self, key):
        return [self.tw[key]] if key in self.tw else []

    def read(self, key, mark):
        if mark is not None:
            self.tr.setdefault(key, []).append(mark)


def mm_group(sch, mms, banks, reads=(), waits=(), inc="t"):
    w = list(waits)
    for b in banks:
        w += sch.wqw(("ps", b))
    for k in reads:
        w += sch.rqw(k)
    n = len(mms)
    mark = None
    for i, (o, l, r) in enumerate(mms):
        mark = sch.op(
            "tensor",
            (lambda _o=o, _l=l, _r=r, _st=(i == 0), _sp=(i == n - 1):
             (lambda e: e.matmul(_o, _l, _r, start=_st, stop=_sp)))(),
            waits=(w if i == 0 else ()),
            inc=(inc if i == n - 1 else None))
    for b in banks:
        sch.wrote(("ps", b), mark)
    for k in reads:
        sch.read(k, mark)
    return mark


def ev(sch, eng, fn, rbanks=(), wkeys=(), rkeys=(), waits=(), inc=None, n=1):
    w = list(waits)
    for b in rbanks:
        w += sch.rqw(("ps", b))
    for k in wkeys:
        w += sch.wqw(k)
    for k in rkeys:
        w += sch.rqw(k)
    mark = sch.op(eng, fn, waits=w, inc=inc, n=n)
    for b in rbanks:
        sch.read(("ps", b), mark)
    for k in wkeys:
        sch.wrote(k, mark)
    for k in rkeys:
        sch.read(k, mark)
    return mark


def dma(sch, eng, fn, wkeys=(), rkeys=(), waits=()):
    if eng == "gpsimd":
        sname = "g%d" % sch.gq
        sch.gq = (sch.gq + 1) % 4
    else:
        sname = "y"
    w = list(waits) + [(sname, sch.cnt[sname])]
    return ev(sch, eng, fn, wkeys=wkeys, rkeys=rkeys, waits=w, inc=sname, n=16)


def build(taps=None):
    taps = taps or {}
    nc = bass.Bass("TRN2", target_bir_lowering=False)

    d_x = nc.dram_tensor("x", [NSEQ, CH, 128, S], F32, kind="ExternalInput").ap()
    d_wqk = nc.dram_tensor("wqk", [CH, 2, 128, 1024], F32, kind="ExternalInput").ap()
    d_wvT = nc.dram_tensor("wvT", [CH, 128, 1024], F32, kind="ExternalInput").ap()
    d_wpc = nc.dram_tensor("wpc", [CH, 128, 1024], F32, kind="ExternalInput").ap()
    d_wg = nc.dram_tensor("wg", [HHC, 128, 1024], I32, kind="ExternalInput").ap()
    d_wu = nc.dram_tensor("wu", [HHC, 128, 1024], I32, kind="ExternalInput").ap()
    d_wdc = nc.dram_tensor("wdc", [CH, 128, 3072], I32, kind="ExternalInput").ap()
    d_wb = [nc.dram_tensor(f"wb{i}", [CH, 128, 1024], F32, kind="ExternalInput").ap()
            for i in range(6)]
    d_wbn0 = nc.dram_tensor("wbn0", [CH, 128, 1024], F32, kind="ExternalInput").ap()
    d_wbn1 = nc.dram_tensor("wbn1", [CH, 128, 1024], F32, kind="ExternalInput").ap()
    d_fw = nc.dram_tensor("fw", [3, CH, 128, 1024], BF16).ap()
    d_fb = nc.dram_tensor("fb", [128, 40], F32).ap()
    d_misc = nc.dram_tensor("misc", [128, 58], F32, kind="ExternalInput").ap()
    d_cbf = nc.dram_tensor("cbf", [128, 256], F32, kind="ExternalInput").ap()
    d_y = nc.dram_tensor("y", [NSEQ, CH, 128, S], F32, kind="ExternalOutput").ap()
    d_taps = {name: nc.dram_tensor("tap_" + name, shape, dt, kind="ExternalOutput").ap()
              for name, (shape, dt) in taps.items()}

    with ExitStack() as ctx:
        XF = ctx.enter_context(nc.sbuf_tensor("XF", [128, CH * S], F32))
        XN = ctx.enter_context(nc.sbuf_tensor("XN", [128, CH * S], BF16))
        AR = ctx.enter_context(nc.sbuf_tensor("AR", [128, 24 * S], BF16))
        WB = ctx.enter_context(nc.sbuf_tensor("WB", [128, 4 * 1024], BF16))
        SQ = ctx.enter_context(nc.sbuf_tensor("SQ", [128, 1024], BF16))
        ROW = ctx.enter_context(nc.sbuf_tensor("ROW", [1, 1024], BF16))
        ROWF = ctx.enter_context(nc.sbuf_tensor("ROWF", [1, 512], F32))
        MISC = ctx.enter_context(nc.sbuf_tensor("MISC", [128, 96], F32))
        CBF = ctx.enter_context(nc.sbuf_tensor("CBF", [128, 256], BF16))
        BV2 = ctx.enter_context(nc.sbuf_tensor("BV2", [128, 8], BF16))
        BB = ctx.enter_context(nc.sbuf_tensor("BB", [128, 24], BF16))
        PS = ctx.enter_context(nc.psum_tensor("PS", [128, 4096], F32))
        sems = {k: ctx.enter_context(nc.semaphore("sem_" + k))
                for k in ("t", "v", "s", "y", "g0", "g1", "g2", "g3")}
        sch = Sched(sems)

        def ar(c0, n):
            return AR[:, c0:c0 + n]

        def xf(c, t0, n):
            return XF[:, S * c + t0: S * c + t0 + n]

        def xn(c, t0, n):
            return XN[:, S * c + t0: S * c + t0 + n]

        def sl(i, off, n):
            return XN[:, 4096 * i + off: 4096 * i + off + n]

        def wbs(s_, off=0, n=1024):
            return WB[:, 1024 * s_ + off: 1024 * s_ + off + n]

        def bank(b, r0=0, r1=128, c0=0, n=512):
            return PS[r0:r1, 512 * b + c0: 512 * b + c0 + n]

        def mcol(c, n=1):
            return MISC[:, c:c + n]

        ones_col = CBF[:, 128:129]
        ones_row = CBF[0:1, 128:256]
        mask_ap = bass.AP(CBF, 0, [[256, 128], [0, 8], [1, 128]])

        G_B, V_B, M_B, PT_B, WVT_B, O_B = 0, S * 8, S * 16, S * 16, S * 20, S * 20

        def arslot(col0, ncols):
            s0, s1 = col0 // S, (col0 + ncols - 1) // S
            return [("ar", s_) for s_ in range(s0, s1 + 1)]

        def xnslot(col0, ncols):
            s0, s1 = col0 // 512, (col0 + ncols - 1) // 512
            return [("xnc", s_) for s_ in range(s0, s1 + 1)]

        def xfkeys(c, t0, n):
            return [("xfc", c, t_) for t_ in range(t0 // 512, (t0 + n - 1) // 512 + 1)]

        def tap(name, src_ap):
            if name not in d_taps:
                return
            waits = [(s_, sch.cnt[s_]) for s_ in sch.cnt]
            dma(sch, "gpsimd",
                (lambda _s=src_ap, _d=d_taps[name]: (lambda e: e.dma_start(out=_d, in_=_s)))(),
                waits=waits)
            for e in ENGINES:
                sch.op(e, None, waits=[(s_, sch.cnt[s_]) for s_ in sch.cnt])

        mi = dma(sch, "sync", lambda e: e.dma_start(out=MISC[:, 0:58], in_=d_misc))
        cb = dma(sch, "gpsimd", lambda e: e.dma_start(out=CBF[:, :], in_=d_cbf))
        sch.rd["misc"] = mi
        sch.rd["cbf"] = cb
        sch.op("vector", lambda e: e.memset(MISC[:, 68:72], 0.0), inc="v")
        bv2m = sch.op("vector", lambda e: e.tensor_copy(BV2[:, :], MISC[:, MC_BV2:MC_BV2 + 8]),
                      waits=[mi], inc="v")
        sch.rd["bv2"] = bv2m

        def ln_tile(src_t0, dst_fn, dst_keys_fn):
            casts = []
            for c in range(CH):
                m = ev(sch, "vector",
                       (lambda _c=c: (lambda e: e.tensor_copy(
                           dst_fn(_c), xf(_c, src_t0, TT))))(),
                       wkeys=dst_keys_fn(c), rkeys=xfkeys(c, src_t0, TT), inc="v")
                casts.append(m)
            mm_group(sch, [(bank(6, 0, 1), ones_col, dst_fn(c)) for c in range(CH)],
                     banks=[6], waits=[casts[-1], sch.rd["cbf"]])
            s1m = ("t", sch.cnt["t"])
            mubf = ev(sch, "scalar", lambda e: e.activation(
                ROW[0:1, 0:512], bank(6, 0, 1), AF.Copy, scale=1.0 / 1024.0),
                rbanks=[6], wkeys=[("row", 0)], waits=[s1m], inc="s")
            mm_group(sch, [(bank(6), ones_row, ROW[0:1, 0:512])], banks=[6],
                     reads=[("row", 0)], waits=[mubf])
            bcm1 = ("t", sch.cnt["t"])
            tmarks = []
            for c in range(CH):
                m = ev(sch, "vector",
                       (lambda _c=c: (lambda e: e.scalar_tensor_tensor(
                           dst_fn(_c), xf(_c, src_t0, TT), 1.0, bank(6),
                           ALU.bypass, ALU.subtract)))(),
                       rbanks=[6], rkeys=xfkeys(c, src_t0, TT), waits=[bcm1], inc="v")
                for k in dst_keys_fn(c):
                    sch.wrote(k, m)
                tmarks.append(m)
            for c in range(CH):
                sqk = ("sq", c % 2)
                sqh = SQ[:, 512 * (c % 2):512 * (c % 2) + 512]
                sqm = ev(sch, "scalar",
                         (lambda _c=c, _h=sqh: (lambda e: e.activation(
                             _h, dst_fn(_c), AF.Square)))(),
                         wkeys=[sqk], waits=[tmarks[c]], inc="s")
                w = [sqm] + (sch.wqw(("ps", 7)) if c == 0 else [])
                mm = sch.op("tensor",
                            (lambda _h=sqh, _st=(c == 0), _sp=(c == CH - 1):
                             (lambda e: e.matmul(bank(7, 0, 1), ones_col, _h,
                                                 start=_st, stop=_sp)))(),
                            waits=w, inc="t")
                sch.read(sqk, mm)
            s2m = ("t", sch.cnt["t"])
            sch.wrote(("ps", 7), s2m)
            lnv = ev(sch, "scalar", lambda e: e.activation(
                ROWF[0:1, :], bank(7, 0, 1), AF.Ln, scale=1.0 / 1024.0,
                bias=MISC[0:1, MC_EPS:MC_EPS + 1]),
                rbanks=[7], wkeys=[("rowf",)], waits=[s2m, sch.rd["misc"]], inc="s")
            rr = ev(sch, "scalar", lambda e: e.activation(
                ROW[0:1, 512:1024], ROWF[0:1, :], AF.Exp, scale=-0.5),
                rkeys=[("rowf",)], wkeys=[("row", 1)], waits=[lnv], inc="s")
            mm_group(sch, [(bank(7), ones_row, ROW[0:1, 512:1024])], banks=[7],
                     reads=[("row", 1)], waits=[rr])
            bcm2 = ("t", sch.cnt["t"])
            rsb = ev(sch, "vector", lambda e: e.tensor_copy(SQ[:, 0:512], bank(7)),
                     rbanks=[7], wkeys=[("sq", 0)], waits=[bcm2], inc="v")
            for c in range(CH):
                t2 = ev(sch, "vector",
                        (lambda _c=c: (lambda e: e.tensor_tensor(
                            dst_fn(_c), dst_fn(_c), SQ[:, 0:512], ALU.mult)))(),
                        rkeys=[("sq", 0)], waits=[rsb], inc="v")
                for k in dst_keys_fn(c):
                    sch.wrote(k, t2)

        def proj_fm(wtile_fn, wread_keys, src_fn, src_keys_fn, dst_fn, dst_keys_fn,
                    bias_col, banks, waits=()):
            for m in range(CH):
                b = banks[m % len(banks)]
                rds = list(wread_keys)
                for k in range(CH):
                    rds += src_keys_fn(k)
                gm = mm_group(sch, [(bank(b), wtile_fn(k, m), src_fn(k))
                                    for k in range(CH)],
                              banks=[b], reads=rds, waits=list(waits))
                ev(sch, "scalar",
                   (lambda _m=m, _b=b, _bc=bias_col: (lambda e: e.activation(
                       dst_fn(_m), bank(_b), AF.Identity, bias=mcol(_bc + _m))))(),
                   rbanks=[b], wkeys=dst_keys_fn(m),
                   waits=[gm, sch.rd["misc"]], inc="s")

        for s_i in range(NSEQ):
            xw = []
            for c in range(CH):
                for t_ in range(NG):
                    xw += sch.wqw(("xfc", c, t_))
            xld = dma(sch, "sync",
                      (lambda _s=s_i: (lambda e: e.dma_start(
                          out=bass.AP(XF, 0, [[CH * S, 128], [S, CH], [1, S]]),
                          in_=d_x[_s].rearrange("c p t -> p c t"))))(),
                      waits=xw)
            for c in range(CH):
                for t_ in range(NG):
                    sch.wrote(("xfc", c, t_), xld)

            groups = [(ec, eh) for ec in range(CH) for eh in range(2)]
            waves = [groups[0:6], groups[6:12], groups[12:16]]

            wqld = dma(sch, "gpsimd",
                       lambda e: e.dma_start(
                           out=bass.AP(AR, 0, [[24 * S, 128], [2048, CH], [1, 2048]]),
                           in_=d_wqk.rearrange("c w p e -> p c w e")),
                       wkeys=[("ar", s_) for s_ in range(8)])

            def emit_mwave(wave):
                bank_w = {gi: sch.wqw(("ps", gi)) for gi in range(len(wave))}
                for oc in range(CH):
                    mk = None
                    for gi, (ec, eh) in enumerate(wave):
                        mk = sch.op("tensor",
                                    (lambda _oc=oc, _ec=ec, _eh=eh, _b=gi,
                                     _st=(oc == 0), _sp=(oc == CH - 1):
                                     (lambda e: e.matmul(
                                         bank(_b),
                                         ar(2048 * _oc + 128 * _ec, 128),
                                         ar(2048 * _oc + 1024 + 512 * _eh, 512),
                                         start=_st, stop=_sp)))(),
                                    waits=(([wqld] if gi == 0 and oc == 0 else []) +
                                           (bank_w[gi] if oc == 0 else [])),
                                    inc=("t" if gi == len(wave) - 1 else None))
                for gi, (ec, eh) in enumerate(wave):
                    mkk = ("t", sch.cnt["t"])
                    sch.wrote(("ps", gi), mkk)
                    for s_ in range(8):
                        sch.read(("ar", s_), mkk)
                    ev(sch, "vector",
                       (lambda _ec=ec, _eh=eh, _b=gi: (lambda e: e.tensor_copy(
                           ar(M_B + 1024 * _ec + 512 * _eh, 512), bank(_b))))(),
                       rbanks=[gi], wkeys=arslot(M_B + 1024 * ec + 512 * eh, 512),
                       inc="v")

            def emit_ln1(g):
                ln_tile(TT * g,
                        (lambda _g=g: (lambda c: xn(c, TT * _g, TT)))(),
                        (lambda _g=g: (lambda c: xnslot(S * c + TT * _g, TT)))())

            def emit_vproj(tg):
                grp = [(tt, h2) for tt in range(4) for h2 in range(2)]
                for wave in (grp[:6], grp[6:]):
                    for gi, (tt, h2) in enumerate(wave):
                        b = gi
                        rds = [k for kk in range(CH) for k in xnslot(S * kk + TT * tg, TT)]
                        rds += arslot(WVT_B, 8192)
                        gm = mm_group(sch,
                                      [(bank(b), xn(k, TT * tg + 128 * tt, 128),
                                        ar(WVT_B + 1024 * k + 512 * h2, 512))
                                       for k in range(CH)],
                                      banks=[b], reads=rds, waits=[wvld])
                        ev(sch, "scalar",
                           (lambda _tg=tg, _tt=tt, _h2=h2, _b=b: (lambda e: e.activation(
                               ar(V_B + 1024 * (4 * _tg + _tt) + 512 * _h2, 512),
                               bank(_b), AF.Copy)))(),
                           rbanks=[b],
                           wkeys=arslot(V_B + 1024 * (4 * tg + tt) + 512 * h2, 512),
                           waits=[gm], inc="s")

            emit_mwave(waves[0])
            emit_ln1(0)
            emit_mwave(waves[1])
            emit_ln1(1)
            emit_mwave(waves[2])
            if s_i == 0:
                tap("m", AR[:, M_B:M_B + 8192])
            wvld = dma(sch, "gpsimd",
                       lambda e: e.dma_start(
                           out=bass.AP(AR, WVT_B, [[24 * S, 128], [1024, CH], [1, 1024]]),
                           in_=d_wvT.rearrange("c p e -> p c e")),
                       wkeys=arslot(WVT_B, 8192))
            emit_ln1(2)
            emit_vproj(0)
            emit_ln1(3)
            if s_i == 0:
                tap("xn1", XN[:, :])
            for tg in range(1, NG):
                emit_vproj(tg)
            if s_i == 0:
                tap("v", AR[:, V_B:V_B + 16384])
            def emit_negc(g):
                for c in range(CH):
                    sqk = ("sq", c % 2)
                    sqh = SQ[:, 512 * (c % 2):512 * (c % 2) + 512]
                    sqm = ev(sch, "scalar",
                             (lambda _c=c, _g=g, _h=sqh: (lambda e: e.activation(
                                 _h, ar(G_B + S * _c + TT * _g, TT), AF.Square)))(),
                             wkeys=[sqk], rkeys=arslot(G_B + S * c + TT * g, TT),
                             inc="s")
                    w = [sqm] + (sch.wqw(("ps", 6)) if c == 0 else [])
                    mm = sch.op("tensor",
                                (lambda _h=sqh, _st=(c == 0), _sp=(c == CH - 1):
                                 (lambda e: e.matmul(bank(6, 0, 1), ones_col, _h,
                                                     start=_st, stop=_sp)))(),
                                waits=w, inc="t")
                    sch.read(sqk, mm)
                gsm = ("t", sch.cnt["t"])
                sch.wrote(("ps", 6), gsm)
                l1 = ev(sch, "scalar", lambda e: e.activation(
                    ROWF[0:1, :], bank(6, 0, 1), AF.Ln,
                    bias=MISC[0:1, MC_EPS:MC_EPS + 1]),
                    rbanks=[6], wkeys=[("rowf",)],
                    waits=[gsm, sch.rd["misc"]], inc="s")
                l2 = ev(sch, "scalar", lambda e: e.activation(
                    bank(6, 0, 1), ROWF[0:1, :], AF.Exp, scale=0.5),
                    rbanks=[6], rkeys=[("rowf",)], waits=[l1], inc="s")
                sch.wrote(("ps", 6), l2)
                return ev(sch, "vector",
                          (lambda _g=g: (lambda e: e.tensor_scalar(
                              ROW[0:1, 512 * (_g % 2):512 * (_g % 2) + 512],
                              bank(6, 0, 1), -32.2, None, ALU.mult)))(),
                          rbanks=[6], wkeys=[("row", g % 2)], waits=[l2], inc="v")

            ngc_marks = {}
            for g in range(NG):
                for m in range(CH):
                    b = m % 6
                    rds = arslot(M_B, 8192) + \
                        [k for kk in range(CH) for k in xnslot(S * kk + TT * g, TT)]
                    gm = mm_group(sch,
                                  [(bank(b),
                                    ar(M_B + 1024 * k, 1024)[:, 128 * m:128 * m + 128],
                                    xn(k, TT * g, TT)) for k in range(CH)],
                                  banks=[b], reads=rds)
                    ev(sch, "scalar",
                       (lambda _g=g, _m=m, _b=b: (lambda e: e.activation(
                           ar(G_B + S * _m + TT * _g, TT), bank(_b), AF.Copy)))(),
                       rbanks=[b], wkeys=arslot(G_B + S * m + TT * g, TT),
                       waits=[gm], inc="s")
                if g < 2 and m == CH - 1:
                    ngc_marks[g] = emit_negc(g)

            if s_i == 0:
                tap("g", AR[:, 0:16384])
            for g in range(NG):
                ngc = ngc_marks[g]
                for kt in range(16):
                    b = kt % 2
                    rds = [k for kk in range(CH) for k in xnslot(S * kk, S)]
                    rds += [("ar", kk) for kk in range(CH)] + [("row", g % 2)]
                    mms = [(bank(b), xn(k, 128 * kt, 128),
                            ar(G_B + S * k + TT * g, TT)) for k in range(CH)]
                    mms.append((bank(b), ones_row,
                                ROW[0:1, 512 * (g % 2):512 * (g % 2) + 512]))
                    gm = mm_group(sch, mms, banks=[b], reads=rds, waits=[ngc])
                    ev(sch, "scalar",
                       (lambda _kt=kt, _b=b: (lambda e: e.activation(
                           ar(PT_B + 512 * _kt, 512), bank(_b), AF.Exp,
                           scale=A_SCALE)))(),
                       rbanks=[b], wkeys=arslot(PT_B + 512 * kt, 512),
                       waits=[gm], inc="s")
                expm = ("s", sch.cnt["s"])
                if g + 2 < NG:
                    ngc_marks[g + 2] = emit_negc(g + 2)
                if s_i == 0 and g == 0:
                    tap("pt0", AR[:, PT_B:PT_B + 8192])
                    tap("row0", ROW[0:1, :])
                for j in range(4):
                    ob = 2 if (j % 2 == 0) else 4
                    w = sch.wqw(("ps", ob)) + sch.wqw(("ps", ob + 1)) + \
                        sch.wqw(("ps", 1)) + [expm]
                    rds = arslot(PT_B, 8192) + arslot(V_B, 16384)
                    for k in rds:
                        w += sch.rqw(k)
                    for kt in range(16):
                        lhs = ar(PT_B + 512 * kt + 128 * j, 128)
                        trip = [(PS[:, 512 * ob:512 * ob + 512], lhs,
                                 ar(V_B + 1024 * kt, 512)),
                                (PS[:, 512 * ob + 512:512 * ob + 1024], lhs,
                                 ar(V_B + 1024 * kt + 512, 512)),
                                (PS[:, 512 + j:512 + j + 1], lhs, ones_col)]
                        for u, (o, l, r) in enumerate(trip):
                            sch.op("tensor",
                                   (lambda _o=o, _l=l, _r=r, _st=(kt == 0),
                                    _sp=(kt == 15):
                                    (lambda e: e.matmul(_o, _l, _r, start=_st,
                                                        stop=_sp)))(),
                                   waits=(w if (kt == 0 and u == 0) else []),
                                   inc=("t" if (kt == 15 and u == 2) else None))
                    pvm = ("t", sch.cnt["t"])
                    sch.wrote(("ps", ob), pvm)
                    sch.wrote(("ps", ob + 1), pvm)
                    sch.wrote(("ps", 1), pvm)
                    for k in rds:
                        sch.read(k, pvm)
                    rc = ev(sch, "vector",
                            (lambda _j=j: (lambda e: e.reciprocal(
                                mcol(MC_RS + _j % 2), PS[:, 512 + _j:512 + _j + 1])))(),
                            rbanks=[1], wkeys=[("rs", j % 2)], inc="v")
                    jj = 4 * g + j
                    r_i = jj % 8
                    ev(sch, "scalar",
                       (lambda _r=r_i, _ob=ob, _j=j: (lambda e: e.activation(
                           ar(O_B + 1024 * _r, 1024),
                           PS[:, 512 * _ob:512 * _ob + 1024], AF.Identity,
                           scale=mcol(MC_RS + _j % 2))))(),
                       rbanks=[ob, ob + 1], rkeys=[("rs", j % 2)],
                       wkeys=arslot(O_B + 1024 * r_i, 1024), waits=[rc], inc="s")

                if g in (1, 3):
                    par = 0 if g == 1 else 1
                    for ec in range(CH):
                        slot = ec % 4
                        dm = dma(sch, "gpsimd",
                                 (lambda _ec=ec, _sl=slot: (lambda e: e.dma_start(
                                     out=wbs(_sl), in_=d_wpc[_ec])))(),
                                 wkeys=[("wb", slot)])
                        ob = 2 if (ec % 2 == 0) else 4
                        w = sch.wqw(("ps", ob)) + sch.wqw(("ps", ob + 1)) + [dm]
                        rds = arslot(O_B, 8192) + [("wb", slot)]
                        for k in rds:
                            w += sch.rqw(k)
                        for rc_ in range(CH):
                            for ih in range(2):
                                sch.op("tensor",
                                       (lambda _sl=slot, _rc=rc_, _ih=ih, _ob=ob:
                                        (lambda e: e.matmul(
                                            PS[:, 512 * (_ob + _ih):512 * (_ob + _ih) + 512],
                                            wbs(_sl, 128 * _rc, 128),
                                            ar(O_B + 1024 * _rc + 512 * _ih, 512),
                                            start=(_rc == 0), stop=(_rc == CH - 1))))(),
                                       waits=(w if (rc_ == 0 and ih == 0) else []),
                                       inc=("t" if (rc_ == CH - 1 and ih == 1) else None))
                        opm = ("t", sch.cnt["t"])
                        sch.wrote(("ps", ob), opm)
                        sch.wrote(("ps", ob + 1), opm)
                        for k in rds:
                            sch.read(k, opm)
                        ev(sch, "vector",
                           (lambda _ec=ec, _ob=ob, _par=par: (lambda e: e.scalar_tensor_tensor(
                               bass.AP(XF, S * _ec + _par, [[CH * S, 128], [2, 1024]]),
                               PS[:, 512 * _ob:512 * _ob + 1024],
                               mcol(MC_BPROJ + _ec),
                               bass.AP(XF, S * _ec + _par, [[CH * S, 128], [2, 1024]]),
                               ALU.add, ALU.add)))(),
                           rbanks=[ob, ob + 1],
                           wkeys=[("xfc", ec, t_) for t_ in range(NG)],
                           waits=[opm, sch.rd["misc"]], inc="v")

            if s_i == 0:
                tap("o", AR[:, O_B:O_B + 8192])
                tap("xf1", XF[:, :])
            def ffn_layer(tag):
                def f_ln(g):
                    ln_tile(TT * g,
                            (lambda _g=g: (lambda c: xn(c, TT * _g, TT)))(),
                            (lambda _g=g: (lambda c: xnslot(S * c + TT * _g, TT)))())

                dm_marks = {}

                def f_dm(c):
                    slot = 2 * (c % 2)
                    dmg = dma(sch, "gpsimd",
                              (lambda _c=c, _sl=slot: (lambda e: e.dma_start(
                                  out=wbs(_sl), in_=d_wg[_c])))(),
                              wkeys=[("wb", slot)])
                    dmu = dma(sch, "gpsimd",
                              (lambda _c=c, _sl=slot: (lambda e: e.dma_start(
                                  out=wbs(_sl + 1), in_=d_wu[_c])))(),
                              wkeys=[("wb", slot + 1)])
                    dm_marks[c] = (slot, dmg, dmu)

                def f_unit(c, g):
                    if c not in dm_marks:
                        f_dm(c)
                    slot, dmg, dmu = dm_marks[c]
                    bg = 2 * (g % 2)
                    bu = bg + 1
                    rds = [("wb", slot)] +                         [k for kk in range(CH) for k in xnslot(S * kk + TT * g, TT)]
                    gmg = mm_group(sch,
                                   [(bank(bg), wbs(slot, 128 * k, 128),
                                     xn(k, TT * g, TT)) for k in range(CH)],
                                   banks=[bg], reads=rds, waits=[dmg])
                    gmu = mm_group(sch,
                                   [(bank(bu), wbs(slot + 1, 128 * k, 128),
                                     xn(k, TT * g, TT)) for k in range(CH)],
                                   banks=[bu], reads=[("wb", slot + 1)],
                                   waits=[dmu])
                    sq_i = (2 * c + g) % 2
                    sqh = SQ[:, 512 * sq_i:512 * sq_i + 512]
                    sil = ev(sch, "scalar",
                             (lambda _bg=bg, _h=sqh: (lambda e: e.activation(
                                 _h, bank(_bg), AF.Silu, scale=DEQ)))(),
                             rbanks=[bg], wkeys=[("sq", sq_i)],
                             waits=[gmg], inc="s")
                    ev(sch, "vector",
                       (lambda _bu=bu, _h=sqh, _c=c, _g=g:
                        (lambda e: e.scalar_tensor_tensor(
                            ar(S * _c + TT * _g, TT), bank(_bu), DEQ,
                            _h, ALU.mult, ALU.mult)))(),
                       rbanks=[bu], rkeys=[("sq", sq_i)],
                       wkeys=arslot(S * c + TT * g, TT),
                       waits=[sil, gmu], inc="v")

                f_ln(0)
                order = [(0, 0), "L1", (0, 1), (1, 0), (1, 1), "L2",
                         (0, 2), (1, 2), "L3", (0, 3), (1, 3)]
                order += [(c, g) for c in range(2, HHC) for g in range(NG)]
                for u in order:
                    if u == "L1":
                        f_ln(1)
                    elif u == "L2":
                        f_ln(2)
                    elif u == "L3":
                        f_ln(3)
                    else:
                        f_unit(*u)
                if s_i == 0:
                    tap(tag + "xn", XN[:, :])
                if s_i == 0:
                    tap(tag + "h", AR[:, :])
                for ec in range(CH):
                    pieces = []
                    for i in range(3):
                        p_i = 3 * ec + i
                        slot = p_i % 4
                        dm = dma(sch, "gpsimd",
                                 (lambda _ec=ec, _i=i, _sl=slot: (lambda e: e.dma_start(
                                     out=wbs(_sl),
                                     in_=d_wdc[_ec][:, 1024 * _i:1024 * _i + 1024])))(),
                                 wkeys=[("wb", slot)])
                        pieces.append((slot, dm))
                    bb = 4 * (ec % 2)
                    w0 = sch.wqw(("ps", bb)) + sch.wqw(("ps", bb + 1)) + \
                        sch.wqw(("ps", bb + 2)) + sch.wqw(("ps", bb + 3))
                    rds = [("wb", pieces[i][0]) for i in range(3)] + \
                        [("ar", s_) for s_ in range(24)]
                    for c in range(HHC):
                        slot, dm = pieces[c // 8]
                        wchunk = []
                        if c % 8 == 0:
                            wchunk.append(dm)
                        wchunk += sch.rqw(("ar", (S * c) // S))
                        if c == 0:
                            wchunk += w0
                        for g in range(NG):
                            sch.op("tensor",
                                   (lambda _sl=slot, _c=c, _g=g, _b=bb + g:
                                    (lambda e: e.matmul(
                                        bank(_b), wbs(_sl, 128 * (_c % 8), 128),
                                        ar(S * _c + TT * _g, TT),
                                        start=(_c == 0), stop=(_c == HHC - 1))))(),
                                   waits=(wchunk if g == 0 else []),
                                   inc=("t" if (c == HHC - 1 and g == NG - 1) else None))
                    dwm = ("t", sch.cnt["t"])
                    for b_ in range(4):
                        sch.wrote(("ps", bb + b_), dwm)
                    for k in rds:
                        sch.read(k, dwm)
                    for g in range(NG):
                        ev(sch, "vector",
                           (lambda _ec=ec, _g=g, _b=bb + g: (lambda e: e.scalar_tensor_tensor(
                               xf(_ec, TT * _g, TT), bank(_b), DEQ,
                               xf(_ec, TT * _g, TT), ALU.mult, ALU.add)))(),
                           rbanks=[bb + g], wkeys=[("xfc", ec, g)],
                           waits=[dwm], inc="v")

            ffn_layer("f1")
            if s_i == 0:
                tap("xf2", XF[:, :])

            FW_Q, FW_K, FW_V, FW_O = 16384, 24576, 32768, 40960
            if s_i == 0:
                ld_qkn = dma(sch, "gpsimd",
                             lambda e: e.dma_start(
                                 out=bass.AP(AR, 0, [[24 * S, 128], [1024, CH], [1, 1024]]),
                                 in_=d_wbn0.rearrange("c p e -> p c e")),
                             wkeys=[("ar", s_) for s_ in range(4)])
                ld_vn = dma(sch, "gpsimd",
                            lambda e: e.dma_start(
                                out=bass.AP(AR, 8192, [[24 * S, 128], [1024, CH], [1, 1024]]),
                                in_=d_wbn1.rearrange("c p e -> p c e")),
                            wkeys=[("ar", s_) for s_ in range(4, 8)])
                ld_q2 = dma(sch, "gpsimd",
                            lambda e: e.dma_start(
                                out=bass.AP(AR, FW_Q, [[24 * S, 128], [1024, CH], [1, 1024]]),
                                in_=d_wb[2].rearrange("c p e -> p c e")),
                            wkeys=[("ar", s_) for s_ in range(8, 12)])
                ld_k2 = dma(sch, "gpsimd",
                            lambda e: e.dma_start(
                                out=bass.AP(AR, FW_K, [[24 * S, 128], [1024, CH], [1, 1024]]),
                                in_=d_wb[3].rearrange("c p e -> p c e")),
                            wkeys=[("ar", s_) for s_ in range(12, 16)])
                ld_v2 = dma(sch, "gpsimd",
                            lambda e: e.dma_start(
                                out=bass.AP(AR, FW_V, [[24 * S, 128], [1024, CH], [1, 1024]]),
                                in_=d_wb[4].rearrange("c p e -> p c e")),
                            wkeys=[("ar", s_) for s_ in range(16, 20)])
                ld_o = dma(sch, "gpsimd",
                           lambda e: e.dma_start(
                               out=bass.AP(AR, FW_O, [[24 * S, 128], [1024, CH], [1, 1024]]),
                               in_=d_wb[5].rearrange("c p e -> p c e")),
                           wkeys=[("ar", s_) for s_ in range(20, 24)])

                bbm = ev(sch, "vector",
                         lambda e: e.tensor_copy(BB[:, 0:8], MISC[:, MC_BQK:MC_BQK + 8]),
                         waits=[sch.rd["misc"]], inc="v")
                bbm = ev(sch, "vector",
                         lambda e: e.tensor_copy(BB[:, 8:16], MISC[:, MC_BV1:MC_BV1 + 8]),
                         waits=[bbm], inc="v")

                def fused_bias(wbase, bb_col, add_col, dst_col, ldm, bbw):
                    w7 = sch.wqw(("ps", 7))
                    for m in range(CH):
                        for hc in range(CH):
                            sch.op("tensor",
                                   (lambda _m=m, _hc=hc, _wb=wbase, _bc=bb_col,
                                    _st=(hc == 0), _sp=(hc == CH - 1):
                                    (lambda e: e.matmul(
                                        PS[:, 3584 + _m:3584 + _m + 1],
                                        ar(_wb + 1024 * _hc + 128 * _m, 128),
                                        BB[:, _bc + _hc:_bc + _hc + 1],
                                        start=_st, stop=_sp)))(),
                                   waits=([ldm, bbw] + w7 if (m == 0 and hc == 0) else []),
                                   inc=("t" if hc == CH - 1 else None))
                    mk = ("t", sch.cnt["t"])
                    sch.wrote(("ps", 7), mk)
                    for m in range(CH):
                        ev(sch, "vector",
                           (lambda _m=m, _a=add_col, _d=dst_col:
                            (lambda e: e.scalar_tensor_tensor(
                                mcol(_d + _m), PS[:, 3584 + _m:3584 + _m + 1], 1.0,
                                MISC[:, _a + _m:_a + _m + 1], ALU.mult, ALU.add)))(),
                           rbanks=([7] if m == 0 else []),
                           waits=[mk, sch.rd["misc"]], inc="v")
                    mk2 = ("v", sch.cnt["v"])
                    sch.read(("ps", 7), mk2)
                    return mk2

                fused_bias(FW_Q, 0, MC_BQ2, MC_BQ2F, ld_q2, bbm)
                fused_bias(FW_K, 0, MC_BK2, MC_BK2F, ld_k2, bbm)
                bvm = fused_bias(FW_V, 8, MC_BV2, MC_BVF, ld_v2, bbm)
                bbm2 = ev(sch, "vector",
                          lambda e: e.tensor_copy(BB[:, 16:24], MISC[:, MC_BVF:MC_BVF + 8]),
                          waits=[bvm], inc="v")
                fused_bias(FW_O, 16, MC_BOUT, MC_BCOMB, ld_o, bbm2)

                def fused_w(lh_base, rh_base, xn_base, lh_ld, rh_ld):
                    for m in range(CH):
                        for nh in range(2):
                            b = (2 * m + nh) % 6
                            rds = arslot(lh_base, 8192) + arslot(rh_base, 8192)
                            gm = mm_group(
                                sch,
                                [(bank(b), ar(lh_base + 1024 * hc + 128 * m, 128),
                                  ar(rh_base + 1024 * hc + 512 * nh, 512))
                                 for hc in range(CH)],
                                banks=[b], reads=rds, waits=[lh_ld, rh_ld])
                            ev(sch, "scalar",
                               (lambda _m=m, _nh=nh, _b=b, _x=xn_base:
                                (lambda e: e.activation(
                                    XN[:, _x + 1024 * _m + 512 * _nh:
                                       _x + 1024 * _m + 512 * _nh + 512],
                                    bank(_b), AF.Copy)))(),
                               rbanks=[b],
                               wkeys=xnslot(xn_base + 1024 * m + 512 * nh, 512),
                               waits=[gm], inc="s")

                fused_w(0, FW_Q, 0, ld_qkn, ld_q2)
                fused_w(0, FW_K, 8192, ld_qkn, ld_k2)
                cpq = ev(sch, "vector",
                         lambda e: e.tensor_copy(AR[:, FW_Q:FW_Q + 8192],
                                                 XN[:, 0:8192]),
                         wkeys=[("ar", s_) for s_ in range(8, 12)],
                         rkeys=xnslot(0, 8192), inc="v")
                cpk = ev(sch, "vector",
                         lambda e: e.tensor_copy(AR[:, FW_K:FW_K + 8192],
                                                 XN[:, 8192:16384]),
                         wkeys=[("ar", s_) for s_ in range(12, 16)],
                         rkeys=xnslot(8192, 8192), inc="v")
                fused_w(8192, FW_V, 0, ld_vn, ld_v2)
                cpv = ev(sch, "vector",
                         lambda e: e.tensor_copy(AR[:, FW_V:FW_V + 8192],
                                                 XN[:, 0:8192]),
                         wkeys=[("ar", s_) for s_ in range(16, 20)],
                         rkeys=xnslot(0, 8192), inc="v")
                dma(sch, "gpsimd",
                    lambda e: e.dma_start(
                        out=d_fw.rearrange("i c p e -> p i c e"),
                        in_=bass.AP(AR, FW_Q, [[24 * S, 128], [8192, 3], [1, 8192]])),
                    waits=[cpq, cpk, cpv])
                dma(sch, "sync",
                    lambda e: e.dma_start(out=d_fb, in_=MISC[:, 56:96]),
                    waits=[("v", sch.cnt["v"])])
                sch.rd["misc"] = ("v", sch.cnt["v"])
            else:
                m1 = dma(sch, "gpsimd",
                         lambda e: e.dma_start(
                             out=bass.AP(AR, FW_Q, [[24 * S, 128], [8192, 3], [1, 8192]]),
                             in_=d_fw.rearrange("i c p e -> p i c e")),
                         wkeys=[("ar", s_) for s_ in range(8, 20)])
                m2 = dma(sch, "gpsimd",
                         lambda e: e.dma_start(
                             out=bass.AP(AR, FW_O, [[24 * S, 128], [1024, CH], [1, 1024]]),
                             in_=d_wb[5].rearrange("c p e -> p c e")),
                         wkeys=[("ar", s_) for s_ in range(20, 24)])
                m3 = dma(sch, "sync",
                         lambda e: e.dma_start(out=MISC[:, 56:96], in_=d_fb),
                         waits=[("v", sch.cnt["v"]), ("s", sch.cnt["s"])])
                sch.rd["misc"] = m3
                for s_ in range(8, 20):
                    sch.wrote(("ar", s_), m1)
                sch.wrote(("ar", 20), m2)

            def blk_base(b):
                if b % 2 == 0:
                    return XN, 0
                return AR, 0

            def breg(b, which, off, n):
                T_, c0 = blk_base(b)
                base = c0 + {"xnb": 0, "q": 4096, "k": 8192, "v": 12288,
                             "o": 0}[which] + off
                return T_[:, base:base + n]

            def bkeys(b, which, off, n):
                base = {"xnb": 0, "q": 4096, "k": 8192, "v": 12288, "o": 0}[which] + off
                if b % 2 == 0:
                    return xnslot(base, n)
                return arslot(base, n)

            def proj_unit(b, which, wbase, bias_col, m):
                bk = m % 2
                rds = [("ar", (wbase + 1024 * kk) // S) for kk in range(CH)]
                for kk in range(CH):
                    rds += bkeys(b, "xnb", 512 * kk, 512)
                gm = mm_group(sch,
                              [(bank(bk), ar(wbase + 1024 * kk + 128 * m, 128),
                                breg(b, "xnb", 512 * kk, 512)) for kk in range(CH)],
                              banks=[bk], reads=rds)
                if bias_col is None:
                    ev(sch, "scalar",
                       (lambda _b=b, _w=which, _m=m, _bk=bk: (lambda e: e.activation(
                           breg(_b, _w, 512 * _m, 512), bank(_bk), AF.Copy)))(),
                       rbanks=[bk], wkeys=bkeys(b, which, 512 * m, 512),
                       waits=[gm], inc="s")
                else:
                    ev(sch, "scalar",
                       (lambda _b=b, _w=which, _m=m, _bk=bk, _bc=bias_col:
                        (lambda e: e.activation(
                            breg(_b, _w, 512 * _m, 512), bank(_bk), AF.Identity,
                            bias=mcol(_bc + _m))))(),
                       rbanks=[bk], wkeys=bkeys(b, which, 512 * m, 512),
                       waits=[gm, sch.rd["misc"]], inc="s")

            def vtm_unit(b, tt, h2):
                bk = (2 * tt + h2) % 2
                rds = [("ar", (FW_V + 1024 * kk) // S) for kk in range(CH)]
                for kk in range(CH):
                    rds += bkeys(b, "xnb", 512 * kk, 512)
                gm = mm_group(sch,
                              [(bank(bk),
                                breg(b, "xnb", 512 * kk + 128 * tt, 128),
                                ar(FW_V + 1024 * kk + 512 * h2, 512))
                               for kk in range(CH)],
                              banks=[bk], reads=rds)
                ev(sch, "scalar",
                   (lambda _b=b, _tt=tt, _h2=h2, _bk=bk: (lambda e: e.activation(
                       breg(_b, "v", 1024 * _tt + 512 * _h2, 512), bank(_bk),
                       AF.Copy)))(),
                   rbanks=[bk], wkeys=bkeys(b, "v", 1024 * tt + 512 * h2, 512),
                   waits=[gm], inc="s")

            def build_ph1(b):
                units = [lambda b=b: ln_tile(
                    TT * b, (lambda c: breg(b, "xnb", 512 * c, 512)),
                    (lambda c: bkeys(b, "xnb", 512 * c, 512)))]
                for m in range(CH):
                    units.append(lambda b=b, m=m: proj_unit(b, "q", FW_Q, MC_BQ2F, m))
                for m in range(CH):
                    units.append(lambda b=b, m=m: proj_unit(b, "k", FW_K, MC_BK2F, m))
                for tt in range(4):
                    for h2 in range(2):
                        units.append(lambda b=b, tt=tt, h2=h2: vtm_unit(b, tt, h2))
                return units

            def b_scores(b, jj):
                pslot = 3 * (jj % 2)
                w = sch.wqw(("ps", 2)) + sch.wqw(("ps", 3))
                rds = bkeys(b, "k", 0, 4096) + bkeys(b, "q", 0, 4096)
                for h in range(CH):
                    sch.op("tensor",
                           (lambda _b=b, _h=h, _jj=jj: (lambda e: e.matmul(
                               PS[:, 1024 + 128 * _h:1024 + 128 * _h + 128],
                               breg(_b, "k", 512 * _h + 128 * _jj, 128),
                               breg(_b, "q", 512 * _h + 128 * _jj, 128),
                               start=True, stop=True)))(),
                           waits=(w + [sch.rqw(k_)[0] for k_ in rds if sch.rqw(k_)]
                                  if h == 0 else []),
                           inc=("t" if h == CH - 1 else None))
                sw = ("t", sch.cnt["t"])
                sch.wrote(("ps", 2), sw)
                sch.wrote(("ps", 3), sw)
                for k_ in rds:
                    sch.read(k_, sw)
                exm = ev(sch, "scalar",
                         (lambda _ps=pslot: (lambda e: e.activation(
                             wbs(_ps, 0, 1024), PS[:, 1024:2048], AF.Exp,
                             scale=B_SCALE, bias=MISC[:, MC_NEG8:MC_NEG8 + 1])))(),
                         rbanks=[2, 3], wkeys=[("wb", pslot)],
                         waits=[sw, sch.rd["misc"]], inc="s")
                return exm

            def b_tail(b, jj, exm):
                pslot = 3 * (jj % 2)
                pmm = ev(sch, "vector",
                         (lambda _ps=pslot: (lambda e: e.scalar_tensor_tensor(
                             wbs(1, 0, 1024), wbs(_ps, 0, 1024), 0.0,
                             mask_ap, ALU.bypass, ALU.mult)))(),
                         wkeys=[("wb", 1)], rkeys=[("wb", pslot)],
                         waits=[exm, sch.rd["cbf"]], inc="v")
                w2 = sch.wqw(("ps", 2)) + sch.wqw(("ps", 3))
                for h in range(CH):
                    sch.op("tensor",
                           (lambda _h=h: (lambda e: e.matmul(
                               PS[0:1, 1024 + 128 * _h:1024 + 128 * _h + 128],
                               ones_col, wbs(1, 128 * _h, 128),
                               start=True, stop=True)))(),
                           waits=((w2 + [pmm]) if h == 0 else []),
                           inc=("t" if h == CH - 1 else None))
                smw = ("t", sch.cnt["t"])
                sch.wrote(("ps", 2), smw)
                sch.wrote(("ps", 3), smw)
                sch.read(("wb", 1), smw)
                rc0 = ev(sch, "vector",
                         lambda e: e.reciprocal(PS[0:1, 2048:3072],
                                                PS[0:1, 1024:2048]),
                         rbanks=[2, 3], wkeys=[("psrow45",)], waits=[smw] +
                         sch.wqw(("ps", 4)) + sch.wqw(("ps", 5)), inc="v")
                sch.wrote(("ps", 4), rc0)
                sch.wrote(("ps", 5), rc0)
                rcm = ev(sch, "scalar",
                         lambda e: e.activation(WB[0:1, 2048:3072],
                                                PS[0:1, 2048:3072], AF.Copy),
                         rbanks=[4, 5], wkeys=[("wb", 2)], waits=[rc0], inc="s")
                mm_group(sch, [(PS[:, 1024:1536], ones_row, WB[0:1, 2048:2560])],
                         banks=[2], reads=[("wb", 2)], waits=[rcm])
                mm_group(sch, [(PS[:, 1536:2048], ones_row, WB[0:1, 2560:3072])],
                         banks=[3], reads=[("wb", 2)])
                cbm = ("t", sch.cnt["t"])
                ptm = ev(sch, "vector",
                         (lambda _ps=pslot: (lambda e: e.tensor_tensor(
                             wbs(_ps, 0, 1024), wbs(1, 0, 1024),
                             PS[:, 1024:2048], ALU.mult)))(),
                         rbanks=[2, 3], wkeys=[("wb", pslot)], rkeys=[("wb", 1)],
                         waits=[cbm], inc="v")
                w3 = sch.wqw(("ps", 4)) + sch.wqw(("ps", 5))
                rds3 = bkeys(b, "v", 1024 * jj, 1024)
                for h in range(CH):
                    sch.op("tensor",
                           (lambda _b=b, _h=h, _jj=jj, _ps=pslot: (lambda e: e.matmul(
                               PS[:, 2048 + 128 * _h:2048 + 128 * _h + 128],
                               breg(_b, "v", 1024 * _jj + 128 * _h, 128),
                               wbs(_ps, 128 * _h, 128),
                               start=True, stop=True)))(),
                           waits=((w3 + [ptm] +
                                   [sch.rqw(k_)[0] for k_ in rds3 if sch.rqw(k_)])
                                  if h == 0 else []),
                           inc=("t" if h == CH - 1 else None))
                ovm = ("t", sch.cnt["t"])
                sch.wrote(("ps", 4), ovm)
                sch.wrote(("ps", 5), ovm)
                sch.read(("wb", pslot), ovm)
                for k_ in rds3:
                    sch.read(k_, ovm)
                T_, c0 = blk_base(b)
                pitch = (CH * S) if (b % 2 == 0) else (24 * S)
                ev(sch, "scalar",
                   (lambda _T=T_, _o=c0 + 128 * jj, _p=pitch: (lambda e: e.activation(
                       bass.AP(_T, _o, [[_p, 128], [512, 8], [1, 128]]),
                       PS[:, 2048:3072].rearrange("p (h x) -> p h x", x=128),
                       AF.Copy)))(),
                   rbanks=[4, 5], wkeys=bkeys(b, "o", 0, 4096),
                   waits=[ovm], inc="s")

            def outproj_unit(b, m):
                bk = 4 + m % 2
                t0 = TT * b
                rds = [("ar", (FW_O + 1024 * kk) // S) for kk in range(CH)]
                for kk in range(CH):
                    rds += bkeys(b, "o", 512 * kk, 512)
                gm = mm_group(sch,
                              [(bank(bk), ar(FW_O + 1024 * kk + 128 * m, 128),
                                breg(b, "o", 512 * kk, 512)) for kk in range(CH)],
                              banks=[bk], reads=rds)
                ev(sch, "vector",
                   (lambda _m=m, _bk=bk, _t0=t0: (lambda e: e.scalar_tensor_tensor(
                       xf(_m, _t0, TT), bank(_bk), mcol(MC_BCOMB + _m),
                       xf(_m, _t0, TT), ALU.add, ALU.add)))(),
                   rbanks=[bk], wkeys=[("xfc", m, t0 // 512)],
                   waits=[gm, sch.rd["misc"]], inc="v")

            def build_ph2(b):
                units = []
                exms = {}

                def u_s(jj):
                    exms[jj] = b_scores(b, jj)

                def u_t(jj):
                    b_tail(b, jj, exms[jj])

                units.append(lambda: u_s(0))
                units.append(lambda: (u_s(1), u_t(0)))
                units.append(lambda: (u_s(2), u_t(1)))
                units.append(lambda: (u_s(3), u_t(2)))
                units.append(lambda: u_t(3))
                for m in range(CH):
                    units.append(lambda b=b, m=m: outproj_unit(b, m))
                return units

            def weave(l1, l2):
                n1, n2 = len(l1), len(l2)
                i1 = 0
                for i2 in range(n2):
                    tgt = (i2 + 1) * n1 // n2
                    while i1 < tgt:
                        l1[i1]()
                        i1 += 1
                    l2[i2]()
                while i1 < n1:
                    l1[i1]()
                    i1 += 1

            for b in range(NG + 1):
                l1 = build_ph1(b) if b < NG else []
                l2 = build_ph2(b - 1) if b >= 1 else []
                weave(l1, l2)

            if s_i == 0:
                tap("xf3", XF[:, :])
            ffn_layer("f2")

            for c in range(CH):
                yw2 = []
                for t_ in range(NG):
                    yw2 += sch.rqw(("xfc", c, t_))
                yst = dma(sch, "sync",
                          (lambda _s=s_i, _c=c: (lambda e: e.dma_start(
                              out=d_y[_s][_c], in_=xf(_c, 0, S))))(),
                          waits=yw2)
                for t_ in range(NG):
                    sch.read(("xfc", c, t_), yst)

        sch.op("sync", None, waits=[("y", sch.cnt["y"])])

        block = ctx.enter_context(nc.Block())

        @block.tensor
        def _(e):
            for f in sch.ops["tensor"]:
                f(e)

        @block.vector
        def _(e):
            for f in sch.ops["vector"]:
                f(e)

        @block.scalar
        def _(e):
            for f in sch.ops["scalar"]:
                f(e)

        @block.gpsimd
        def _(e):
            for f in sch.ops["gpsimd"]:
                f(e)

        @block.sync
        def _(e):
            for f in sch.ops["sync"]:
                f(e)

    return nc


def host_inputs(L0_attn_a_w_qkv, L0_attn_a_w_proj, L0_attn_a_b_proj,
                L0_attn_b_proj_qk_w, L0_attn_b_proj_qk_b, L0_attn_b_proj_v_w,
                L0_attn_b_proj_v_b, L0_attn_b_in_proj_w, L0_attn_b_in_proj_b,
                L0_attn_b_out_proj_w, L0_attn_b_out_proj_b,
                L0_ffn_w_gate, L0_ffn_w_up, L0_ffn_w_down):
    f = lambda a: np.ascontiguousarray(np.asarray(a, dtype=np.float32))
    ii = lambda a: np.ascontiguousarray(np.asarray(a, dtype=np.int32))
    wqkv = f(L0_attn_a_w_qkv)
    wq_img = wqkv[0:1024].reshape(8, 128, 1024)
    wk_img = wqkv[1024:2048].reshape(8, 128, 1024)
    wqk = np.ascontiguousarray(np.stack([wq_img, wk_img], axis=1))
    wvT = np.ascontiguousarray(wqkv[2048:3072].T).reshape(8, 128, 1024)
    wpT = np.ascontiguousarray(f(L0_attn_a_w_proj).T)
    wpc = np.ascontiguousarray(
        wpT.reshape(8, 128, 8, 128).transpose(2, 1, 0, 3)).reshape(8, 128, 1024)
    wg = np.ascontiguousarray(
        ii(L0_ffn_w_gate).reshape(24, 128, 8, 128).transpose(0, 3, 2, 1)).reshape(24, 128, 1024)
    wu = np.ascontiguousarray(
        ii(L0_ffn_w_up).reshape(24, 128, 8, 128).transpose(0, 3, 2, 1)).reshape(24, 128, 1024)
    wdc = np.ascontiguousarray(
        ii(L0_ffn_w_down).reshape(8, 128, 24, 128).transpose(0, 3, 2, 1)).reshape(8, 128, 3072)
    inw = f(L0_attn_b_in_proj_w)
    wb_list = []
    for W in (f(L0_attn_b_proj_qk_w), f(L0_attn_b_proj_v_w), inw[0:1024],
              inw[1024:2048], inw[2048:3072], f(L0_attn_b_out_proj_w)):
        wb_list.append(np.ascontiguousarray(np.ascontiguousarray(W.T)).reshape(8, 128, 1024))
    misc = np.zeros((128, 58), np.float32)
    inb = f(L0_attn_b_in_proj_b)
    for c0, b in ((MC_BPROJ, f(L0_attn_a_b_proj)), (MC_BQK, f(L0_attn_b_proj_qk_b)),
                  (MC_BV1, f(L0_attn_b_proj_v_b)), (MC_BQ2, inb[0:1024]),
                  (MC_BK2, inb[1024:2048]), (MC_BV2, inb[2048:3072]),
                  (MC_BOUT, f(L0_attn_b_out_proj_b))):
        misc[:, c0:c0 + 8] = b.reshape(8, 128).T
    misc[:, MC_EPS] = 1e-5
    misc[:, MC_NEG8] = -8.0
    cbf = np.zeros((128, 256), np.float32)
    p = np.arange(128)
    m01 = ((p[:, None] // 4 == p[None, :] // 4) & (p[:, None] != p[None, :]))
    cbf[:, 0:128] = m01.astype(np.float32)
    cbf[:, 128:256] = 1.0
    wbn0 = np.ascontiguousarray(f(L0_attn_b_proj_qk_w)).reshape(8, 128, 1024)
    wbn1 = np.ascontiguousarray(f(L0_attn_b_proj_v_w)).reshape(8, 128, 1024)
    return dict(wqk=wqk, wvT=wvT, wpc=wpc, wg=wg, wu=wu, wdc=wdc,
                wb0=wb_list[0], wb1=wb_list[1], wb2=wb_list[2], wb3=wb_list[3],
                wb4=wb_list[4], wb5=wb_list[5], wbn0=wbn0, wbn1=wbn1,
                misc=misc, cbf=cbf)


def kernel(L0_attn_a_x, L0_attn_a_w_qkv, L0_attn_a_w_proj, L0_attn_a_b_proj,
           L0_attn_b_proj_qk_w, L0_attn_b_proj_qk_b, L0_attn_b_proj_v_w,
           L0_attn_b_proj_v_b, L0_attn_b_in_proj_w, L0_attn_b_in_proj_b,
           L0_attn_b_out_proj_w, L0_attn_b_out_proj_b,
           L0_ffn_w_gate, L0_ffn_w_up, L0_ffn_w_down):
    x = np.ascontiguousarray(np.asarray(L0_attn_a_x, dtype=np.float32))
    shared = host_inputs(L0_attn_a_w_qkv, L0_attn_a_w_proj, L0_attn_a_b_proj,
                         L0_attn_b_proj_qk_w, L0_attn_b_proj_qk_b,
                         L0_attn_b_proj_v_w, L0_attn_b_proj_v_b,
                         L0_attn_b_in_proj_w, L0_attn_b_in_proj_b,
                         L0_attn_b_out_proj_w, L0_attn_b_out_proj_b,
                         L0_ffn_w_gate, L0_ffn_w_up, L0_ffn_w_down)
    nc = build()
    in_maps = []
    for c in range(8):
        xi = np.ascontiguousarray(
            x[2 * c:2 * c + 2].transpose(0, 2, 1)).reshape(NSEQ, CH, 128, S)
        m = dict(shared)
        m["x"] = xi
        in_maps.append(m)
    res = run_bass_kernel_spmd(nc, in_maps, core_ids=list(range(8)))
    out = np.empty((16, S, D), np.float32)
    for c in range(8):
        y = np.asarray(res.results[c]["y"])
        out[2 * c:2 * c + 2] = y.reshape(NSEQ, D, S).transpose(0, 2, 1)
    return out


# version 11
# speedup vs baseline: 1.1094x; 1.0508x over previous
import numpy as np
from contextlib import ExitStack

import concourse.bass as bass
import concourse.mybir as mybir
from concourse.bass_utils import run_bass_kernel_spmd

F32 = mybir.dt.float32
BF16 = mybir.dt.bfloat16
I32 = mybir.dt.int32
AF = mybir.ActivationFunctionType
ALU = mybir.AluOpType

D = 1024
S = 2048
NSEQ = 2
CH = 8
TT = 512
NG = S // TT
HHC = 24
A_SCALE = 0.125
B_SCALE = float(1.0 / np.sqrt(128.0))
DEQ = float(1.0 / 127.0)
ENGINES = ("tensor", "vector", "scalar", "gpsimd", "sync")

MC_BPROJ, MC_BQK, MC_BV1, MC_BQ2, MC_BK2, MC_BV2, MC_BOUT = 0, 8, 16, 24, 32, 40, 48
MC_EPS, MC_NEG8, MC_BCOMB, MC_RS = 56, 57, 58, 66
MC_BQ2F, MC_BK2F, MC_BVF = 72, 80, 88


def _marks(ws):
    return [m for m in ws if m is not None and m[0] is not None]


class Sched:
    def __init__(self, sems):
        self.sems = sems
        self.ops = {e: [] for e in ENGINES}
        self.cnt = {k: 0 for k in sems}
        self.lastwait = {}
        self.rd = {}
        self.tw = {}
        self.tr = {}
        self.gq = 0

    def op(self, eng, fn, waits=(), inc=None, n=1):
        w = []
        for m in _marks(waits):
            sname, v = m
            assert v <= self.cnt[sname], (sname, v, self.cnt[sname])
            if self.lastwait.get((eng, sname), -1) < v:
                w.append((sname, v))
                self.lastwait[(eng, sname)] = v
        if inc is not None:
            self.cnt[inc] += n
        post = self.cnt[inc] if inc is not None else None
        sems = self.sems

        def emit(e, _fn=fn, _w=tuple(w), _inc=inc, _n=n):
            for sname, v in _w:
                e.wait_ge(sems[sname], v)
            instr = _fn(e) if _fn is not None else None
            if _inc is not None:
                instr.then_inc(sems[_inc], _n)
        self.ops[eng].append(emit)
        return (inc, post) if inc is not None else None

    def wqw(self, key):
        return self.tr.pop(key, []) + ([self.tw[key]] if key in self.tw else [])

    def wrote(self, key, mark):
        if mark is not None:
            self.tw[key] = mark

    def rqw(self, key):
        return [self.tw[key]] if key in self.tw else []

    def read(self, key, mark):
        if mark is not None:
            self.tr.setdefault(key, []).append(mark)


def mm_group(sch, mms, banks, reads=(), waits=(), inc="t"):
    w = list(waits)
    for b in banks:
        w += sch.wqw(("ps", b))
    for k in reads:
        w += sch.rqw(k)
    n = len(mms)
    mark = None
    for i, (o, l, r) in enumerate(mms):
        mark = sch.op(
            "tensor",
            (lambda _o=o, _l=l, _r=r, _st=(i == 0), _sp=(i == n - 1):
             (lambda e: e.matmul(_o, _l, _r, start=_st, stop=_sp)))(),
            waits=(w if i == 0 else ()),
            inc=(inc if i == n - 1 else None))
    for b in banks:
        sch.wrote(("ps", b), mark)
    for k in reads:
        sch.read(k, mark)
    return mark


def ev(sch, eng, fn, rbanks=(), wkeys=(), rkeys=(), waits=(), inc=None, n=1):
    w = list(waits)
    for b in rbanks:
        w += sch.rqw(("ps", b))
    for k in wkeys:
        w += sch.wqw(k)
    for k in rkeys:
        w += sch.rqw(k)
    mark = sch.op(eng, fn, waits=w, inc=inc, n=n)
    for b in rbanks:
        sch.read(("ps", b), mark)
    for k in wkeys:
        sch.wrote(k, mark)
    for k in rkeys:
        sch.read(k, mark)
    return mark


def dma(sch, eng, fn, wkeys=(), rkeys=(), waits=()):
    if eng == "gpsimd":
        sname = "g%d" % sch.gq
        sch.gq = (sch.gq + 1) % 4
    else:
        sname = "y"
    w = list(waits) + [(sname, sch.cnt[sname])]
    return ev(sch, eng, fn, wkeys=wkeys, rkeys=rkeys, waits=w, inc=sname, n=16)


def build(taps=None):
    taps = taps or {}
    nc = bass.Bass("TRN2", target_bir_lowering=False)

    d_x = nc.dram_tensor("x", [NSEQ, CH, 128, S], F32, kind="ExternalInput").ap()
    d_wqk = nc.dram_tensor("wqk", [CH, 2, 128, 1024], F32, kind="ExternalInput").ap()
    d_wvT = nc.dram_tensor("wvT", [CH, 128, 1024], F32, kind="ExternalInput").ap()
    d_wpc = nc.dram_tensor("wpc", [CH, 128, 1024], F32, kind="ExternalInput").ap()
    d_wg = nc.dram_tensor("wg", [HHC, 128, 1024], I32, kind="ExternalInput").ap()
    d_wu = nc.dram_tensor("wu", [HHC, 128, 1024], I32, kind="ExternalInput").ap()
    d_wdc = nc.dram_tensor("wdc", [CH, 128, 3072], I32, kind="ExternalInput").ap()
    d_wb = [nc.dram_tensor(f"wb{i}", [CH, 128, 1024], F32, kind="ExternalInput").ap()
            for i in range(6)]
    d_wbn0 = nc.dram_tensor("wbn0", [CH, 128, 1024], F32, kind="ExternalInput").ap()
    d_wbn1 = nc.dram_tensor("wbn1", [CH, 128, 1024], F32, kind="ExternalInput").ap()
    d_fw = nc.dram_tensor("fw", [3, CH, 128, 1024], BF16).ap()
    d_fb = nc.dram_tensor("fb", [128, 40], F32).ap()
    d_fm = nc.dram_tensor("fm", [128, 8192], BF16).ap()
    d_misc = nc.dram_tensor("misc", [128, 58], F32, kind="ExternalInput").ap()
    d_cbf = nc.dram_tensor("cbf", [128, 256], F32, kind="ExternalInput").ap()
    d_y = nc.dram_tensor("y", [NSEQ, CH, 128, S], F32, kind="ExternalOutput").ap()
    d_taps = {name: nc.dram_tensor("tap_" + name, shape, dt, kind="ExternalOutput").ap()
              for name, (shape, dt) in taps.items()}

    with ExitStack() as ctx:
        XF = ctx.enter_context(nc.sbuf_tensor("XF", [128, CH * S], F32))
        XN = ctx.enter_context(nc.sbuf_tensor("XN", [128, CH * S], BF16))
        AR = ctx.enter_context(nc.sbuf_tensor("AR", [128, 24 * S], BF16))
        WB = ctx.enter_context(nc.sbuf_tensor("WB", [128, 4 * 1024], BF16))
        SQ = ctx.enter_context(nc.sbuf_tensor("SQ", [128, 1024], BF16))
        ROW = ctx.enter_context(nc.sbuf_tensor("ROW", [1, 1024], BF16))
        ROWF = ctx.enter_context(nc.sbuf_tensor("ROWF", [1, 512], F32))
        MISC = ctx.enter_context(nc.sbuf_tensor("MISC", [128, 96], F32))
        CBF = ctx.enter_context(nc.sbuf_tensor("CBF", [128, 256], BF16))
        BV2 = ctx.enter_context(nc.sbuf_tensor("BV2", [128, 8], BF16))
        BB = ctx.enter_context(nc.sbuf_tensor("BB", [128, 24], BF16))
        PS = ctx.enter_context(nc.psum_tensor("PS", [128, 4096], F32))
        sems = {k: ctx.enter_context(nc.semaphore("sem_" + k))
                for k in ("t", "v", "s", "y", "g0", "g1", "g2", "g3")}
        sch = Sched(sems)

        def ar(c0, n):
            return AR[:, c0:c0 + n]

        def xf(c, t0, n):
            return XF[:, S * c + t0: S * c + t0 + n]

        def xn(c, t0, n):
            return XN[:, S * c + t0: S * c + t0 + n]

        def sl(i, off, n):
            return XN[:, 4096 * i + off: 4096 * i + off + n]

        def wbs(s_, off=0, n=1024):
            return WB[:, 1024 * s_ + off: 1024 * s_ + off + n]

        def bank(b, r0=0, r1=128, c0=0, n=512):
            return PS[r0:r1, 512 * b + c0: 512 * b + c0 + n]

        def mcol(c, n=1):
            return MISC[:, c:c + n]

        ones_col = CBF[:, 128:129]
        ones_row = CBF[0:1, 128:256]
        mask_ap = bass.AP(CBF, 0, [[256, 128], [0, 8], [1, 128]])

        G_B, V_B, M_B, PT_B, WVT_B, O_B = 0, S * 8, S * 16, S * 16, S * 20, S * 20

        def arslot(col0, ncols):
            s0, s1 = col0 // S, (col0 + ncols - 1) // S
            return [("ar", s_) for s_ in range(s0, s1 + 1)]

        def xnslot(col0, ncols):
            s0, s1 = col0 // 512, (col0 + ncols - 1) // 512
            return [("xnc", s_) for s_ in range(s0, s1 + 1)]

        def xfkeys(c, t0, n):
            return [("xfc", c, t_) for t_ in range(t0 // 512, (t0 + n - 1) // 512 + 1)]

        def tap(name, src_ap):
            if name not in d_taps:
                return
            waits = [(s_, sch.cnt[s_]) for s_ in sch.cnt]
            dma(sch, "gpsimd",
                (lambda _s=src_ap, _d=d_taps[name]: (lambda e: e.dma_start(out=_d, in_=_s)))(),
                waits=waits)
            for e in ENGINES:
                sch.op(e, None, waits=[(s_, sch.cnt[s_]) for s_ in sch.cnt])

        mi = dma(sch, "sync", lambda e: e.dma_start(out=MISC[:, 0:58], in_=d_misc))
        cb = dma(sch, "gpsimd", lambda e: e.dma_start(out=CBF[:, :], in_=d_cbf))
        sch.rd["misc"] = mi
        sch.rd["cbf"] = cb
        sch.op("vector", lambda e: e.memset(MISC[:, 68:72], 0.0), inc="v")
        bv2m = sch.op("vector", lambda e: e.tensor_copy(BV2[:, :], MISC[:, MC_BV2:MC_BV2 + 8]),
                      waits=[mi], inc="v")
        sch.rd["bv2"] = bv2m

        def ln_tile(src_t0, dst_fn, dst_keys_fn):
            casts = []
            for c in range(CH):
                m = ev(sch, "vector",
                       (lambda _c=c: (lambda e: e.tensor_copy(
                           dst_fn(_c), xf(_c, src_t0, TT))))(),
                       wkeys=dst_keys_fn(c), rkeys=xfkeys(c, src_t0, TT), inc="v")
                casts.append(m)
            mm_group(sch, [(bank(6, 0, 1), ones_col, dst_fn(c)) for c in range(CH)],
                     banks=[6], waits=[casts[-1], sch.rd["cbf"]])
            s1m = ("t", sch.cnt["t"])
            mubf = ev(sch, "scalar", lambda e: e.activation(
                ROW[0:1, 0:512], bank(6, 0, 1), AF.Copy, scale=1.0 / 1024.0),
                rbanks=[6], wkeys=[("row", 0)], waits=[s1m], inc="s")
            mm_group(sch, [(bank(6), ones_row, ROW[0:1, 0:512])], banks=[6],
                     reads=[("row", 0)], waits=[mubf])
            bcm1 = ("t", sch.cnt["t"])
            tmarks = []
            for c in range(CH):
                m = ev(sch, "vector",
                       (lambda _c=c: (lambda e: e.scalar_tensor_tensor(
                           dst_fn(_c), xf(_c, src_t0, TT), 1.0, bank(6),
                           ALU.bypass, ALU.subtract)))(),
                       rbanks=[6], rkeys=xfkeys(c, src_t0, TT), waits=[bcm1], inc="v")
                for k in dst_keys_fn(c):
                    sch.wrote(k, m)
                tmarks.append(m)
            for c in range(CH):
                sqk = ("sq", c % 2)
                sqh = SQ[:, 512 * (c % 2):512 * (c % 2) + 512]
                sqm = ev(sch, "scalar",
                         (lambda _c=c, _h=sqh: (lambda e: e.activation(
                             _h, dst_fn(_c), AF.Square)))(),
                         wkeys=[sqk], waits=[tmarks[c]], inc="s")
                w = [sqm] + (sch.wqw(("ps", 7)) if c == 0 else [])
                mm = sch.op("tensor",
                            (lambda _h=sqh, _st=(c == 0), _sp=(c == CH - 1):
                             (lambda e: e.matmul(bank(7, 0, 1), ones_col, _h,
                                                 start=_st, stop=_sp)))(),
                            waits=w, inc="t")
                sch.read(sqk, mm)
            s2m = ("t", sch.cnt["t"])
            sch.wrote(("ps", 7), s2m)
            lnv = ev(sch, "scalar", lambda e: e.activation(
                ROWF[0:1, :], bank(7, 0, 1), AF.Ln, scale=1.0 / 1024.0,
                bias=MISC[0:1, MC_EPS:MC_EPS + 1]),
                rbanks=[7], wkeys=[("rowf",)], waits=[s2m, sch.rd["misc"]], inc="s")
            rr = ev(sch, "scalar", lambda e: e.activation(
                ROW[0:1, 512:1024], ROWF[0:1, :], AF.Exp, scale=-0.5),
                rkeys=[("rowf",)], wkeys=[("row", 1)], waits=[lnv], inc="s")
            mm_group(sch, [(bank(7), ones_row, ROW[0:1, 512:1024])], banks=[7],
                     reads=[("row", 1)], waits=[rr])
            bcm2 = ("t", sch.cnt["t"])
            rsb = ev(sch, "vector", lambda e: e.tensor_copy(SQ[:, 0:512], bank(7)),
                     rbanks=[7], wkeys=[("sq", 0)], waits=[bcm2], inc="v")
            for c in range(CH):
                t2 = ev(sch, "vector",
                        (lambda _c=c: (lambda e: e.tensor_tensor(
                            dst_fn(_c), dst_fn(_c), SQ[:, 0:512], ALU.mult)))(),
                        rkeys=[("sq", 0)], waits=[rsb], inc="v")
                for k in dst_keys_fn(c):
                    sch.wrote(k, t2)

        def proj_fm(wtile_fn, wread_keys, src_fn, src_keys_fn, dst_fn, dst_keys_fn,
                    bias_col, banks, waits=()):
            for m in range(CH):
                b = banks[m % len(banks)]
                rds = list(wread_keys)
                for k in range(CH):
                    rds += src_keys_fn(k)
                gm = mm_group(sch, [(bank(b), wtile_fn(k, m), src_fn(k))
                                    for k in range(CH)],
                              banks=[b], reads=rds, waits=list(waits))
                ev(sch, "scalar",
                   (lambda _m=m, _b=b, _bc=bias_col: (lambda e: e.activation(
                       dst_fn(_m), bank(_b), AF.Identity, bias=mcol(_bc + _m))))(),
                   rbanks=[b], wkeys=dst_keys_fn(m),
                   waits=[gm, sch.rd["misc"]], inc="s")

        for s_i in range(NSEQ):
            xw = []
            for c in range(CH):
                for t_ in range(NG):
                    xw += sch.wqw(("xfc", c, t_))
            xld = dma(sch, "sync",
                      (lambda _s=s_i: (lambda e: e.dma_start(
                          out=bass.AP(XF, 0, [[CH * S, 128], [S, CH], [1, S]]),
                          in_=d_x[_s].rearrange("c p t -> p c t"))))(),
                      waits=xw)
            for c in range(CH):
                for t_ in range(NG):
                    sch.wrote(("xfc", c, t_), xld)

            groups = [(ec, eh) for ec in range(CH) for eh in range(2)]
            waves = [groups[0:6], groups[6:12], groups[12:16]]

            def emit_mwave(wave):
                bank_w = {gi: sch.wqw(("ps", gi)) for gi in range(len(wave))}
                for oc in range(CH):
                    mk = None
                    for gi, (ec, eh) in enumerate(wave):
                        mk = sch.op("tensor",
                                    (lambda _oc=oc, _ec=ec, _eh=eh, _b=gi,
                                     _st=(oc == 0), _sp=(oc == CH - 1):
                                     (lambda e: e.matmul(
                                         bank(_b),
                                         ar(2048 * _oc + 128 * _ec, 128),
                                         ar(2048 * _oc + 1024 + 512 * _eh, 512),
                                         start=_st, stop=_sp)))(),
                                    waits=(([wqld_a if oc < 4 else wqld_b]
                                            if gi == 0 and oc in (0, 4) else []) +
                                           (bank_w[gi] if oc == 0 else [])),
                                    inc=("t" if gi == len(wave) - 1 else None))
                for gi, (ec, eh) in enumerate(wave):
                    mkk = ("t", sch.cnt["t"])
                    sch.wrote(("ps", gi), mkk)
                    for s_ in range(8):
                        sch.read(("ar", s_), mkk)
                    ev(sch, "vector",
                       (lambda _ec=ec, _eh=eh, _b=gi: (lambda e: e.tensor_copy(
                           ar(M_B + 1024 * _ec + 512 * _eh, 512), bank(_b))))(),
                       rbanks=[gi], wkeys=arslot(M_B + 1024 * ec + 512 * eh, 512),
                       inc="v")

            def emit_ln1(g):
                ln_tile(TT * g,
                        (lambda _g=g: (lambda c: xn(c, TT * _g, TT)))(),
                        (lambda _g=g: (lambda c: xnslot(S * c + TT * _g, TT)))())

            def emit_vproj(tg):
                grp = [(tt, h2) for tt in range(4) for h2 in range(2)]
                for wave in (grp[:6], grp[6:]):
                    for gi, (tt, h2) in enumerate(wave):
                        b = gi
                        rds = [k for kk in range(CH) for k in xnslot(S * kk + TT * tg, TT)]
                        rds += arslot(WVT_B, 8192)
                        gm = mm_group(sch,
                                      [(bank(b), xn(k, TT * tg + 128 * tt, 128),
                                        ar(WVT_B + 1024 * k + 512 * h2, 512))
                                       for k in range(CH)],
                                      banks=[b], reads=rds, waits=[wvld])
                        ev(sch, "scalar",
                           (lambda _tg=tg, _tt=tt, _h2=h2, _b=b: (lambda e: e.activation(
                               ar(V_B + 1024 * (4 * _tg + _tt) + 512 * _h2, 512),
                               bank(_b), AF.Copy)))(),
                           rbanks=[b],
                           wkeys=arslot(V_B + 1024 * (4 * tg + tt) + 512 * h2, 512),
                           waits=[gm], inc="s")

            if s_i == 0:
                wqld_a = dma(sch, "gpsimd",
                             lambda e: e.dma_start(
                                 out=bass.AP(AR, 0, [[24 * S, 128], [2048, 4], [1, 2048]]),
                                 in_=d_wqk[0:4].rearrange("c w p e -> p c w e")),
                             wkeys=[("ar", s_) for s_ in range(4)])
                wqld_b = dma(sch, "gpsimd",
                             lambda e: e.dma_start(
                                 out=bass.AP(AR, 8192, [[24 * S, 128], [2048, 4], [1, 2048]]),
                                 in_=d_wqk[4:8].rearrange("c w p e -> p c w e")),
                             wkeys=[("ar", s_) for s_ in range(4, 8)])
                emit_mwave(waves[0])
                emit_ln1(0)
                emit_mwave(waves[1])
                emit_ln1(1)
                emit_mwave(waves[2])
                dma(sch, "gpsimd",
                    lambda e: e.dma_start(out=d_fm, in_=AR[:, M_B:M_B + 8192]),
                    rkeys=arslot(M_B, 8192), waits=[("v", sch.cnt["v"])])
                tap("m", AR[:, M_B:M_B + 8192])
            else:
                dma(sch, "gpsimd",
                    lambda e: e.dma_start(out=AR[:, M_B:M_B + 8192], in_=d_fm),
                    wkeys=arslot(M_B, 8192))
                emit_ln1(0)
                emit_ln1(1)
            wvld = dma(sch, "gpsimd",
                       lambda e: e.dma_start(
                           out=bass.AP(AR, WVT_B, [[24 * S, 128], [1024, CH], [1, 1024]]),
                           in_=d_wvT.rearrange("c p e -> p c e")),
                       wkeys=arslot(WVT_B, 8192))
            emit_ln1(2)
            emit_vproj(0)
            emit_ln1(3)
            if s_i == 0:
                tap("xn1", XN[:, :])
            for tg in range(1, NG):
                emit_vproj(tg)
            if s_i == 0:
                tap("v", AR[:, V_B:V_B + 16384])
            def emit_negc(g):
                for c in range(CH):
                    sqk = ("sq", c % 2)
                    sqh = SQ[:, 512 * (c % 2):512 * (c % 2) + 512]
                    sqm = ev(sch, "scalar",
                             (lambda _c=c, _g=g, _h=sqh: (lambda e: e.activation(
                                 _h, ar(G_B + S * _c + TT * _g, TT), AF.Square)))(),
                             wkeys=[sqk], rkeys=arslot(G_B + S * c + TT * g, TT),
                             inc="s")
                    w = [sqm] + (sch.wqw(("ps", 6)) if c == 0 else [])
                    mm = sch.op("tensor",
                                (lambda _h=sqh, _st=(c == 0), _sp=(c == CH - 1):
                                 (lambda e: e.matmul(bank(6, 0, 1), ones_col, _h,
                                                     start=_st, stop=_sp)))(),
                                waits=w, inc="t")
                    sch.read(sqk, mm)
                gsm = ("t", sch.cnt["t"])
                sch.wrote(("ps", 6), gsm)
                l1 = ev(sch, "scalar", lambda e: e.activation(
                    ROWF[0:1, :], bank(6, 0, 1), AF.Ln,
                    bias=MISC[0:1, MC_EPS:MC_EPS + 1]),
                    rbanks=[6], wkeys=[("rowf",)],
                    waits=[gsm, sch.rd["misc"]], inc="s")
                l2 = ev(sch, "scalar", lambda e: e.activation(
                    bank(6, 0, 1), ROWF[0:1, :], AF.Exp, scale=0.5),
                    rbanks=[6], rkeys=[("rowf",)], waits=[l1], inc="s")
                sch.wrote(("ps", 6), l2)
                return ev(sch, "vector",
                          (lambda _g=g: (lambda e: e.tensor_scalar(
                              ROW[0:1, 512 * (_g % 2):512 * (_g % 2) + 512],
                              bank(6, 0, 1), -32.2, None, ALU.mult)))(),
                          rbanks=[6], wkeys=[("row", g % 2)], waits=[l2], inc="v")

            ngc_marks = {}
            for g in range(NG):
                for m in range(CH):
                    b = m % 6
                    rds = arslot(M_B, 8192) + \
                        [k for kk in range(CH) for k in xnslot(S * kk + TT * g, TT)]
                    gm = mm_group(sch,
                                  [(bank(b),
                                    ar(M_B + 1024 * k, 1024)[:, 128 * m:128 * m + 128],
                                    xn(k, TT * g, TT)) for k in range(CH)],
                                  banks=[b], reads=rds)
                    ev(sch, "scalar",
                       (lambda _g=g, _m=m, _b=b: (lambda e: e.activation(
                           ar(G_B + S * _m + TT * _g, TT), bank(_b), AF.Copy)))(),
                       rbanks=[b], wkeys=arslot(G_B + S * m + TT * g, TT),
                       waits=[gm], inc="s")
                if g < 2 and m == CH - 1:
                    ngc_marks[g] = emit_negc(g)

            if s_i == 0:
                tap("g", AR[:, 0:16384])
            for g in range(NG):
                ngc = ngc_marks[g]
                for kt in range(16):
                    b = kt % 2
                    rds = [k for kk in range(CH) for k in xnslot(S * kk, S)]
                    rds += [("ar", kk) for kk in range(CH)] + [("row", g % 2)]
                    mms = [(bank(b), xn(k, 128 * kt, 128),
                            ar(G_B + S * k + TT * g, TT)) for k in range(CH)]
                    mms.append((bank(b), ones_row,
                                ROW[0:1, 512 * (g % 2):512 * (g % 2) + 512]))
                    gm = mm_group(sch, mms, banks=[b], reads=rds, waits=[ngc])
                    ev(sch, "scalar",
                       (lambda _kt=kt, _b=b: (lambda e: e.activation(
                           ar(PT_B + 512 * _kt, 512), bank(_b), AF.Exp,
                           scale=A_SCALE)))(),
                       rbanks=[b], wkeys=arslot(PT_B + 512 * kt, 512),
                       waits=[gm], inc="s")
                expm = ("s", sch.cnt["s"])
                if g + 2 < NG:
                    ngc_marks[g + 2] = emit_negc(g + 2)
                if s_i == 0 and g == 0:
                    tap("pt0", AR[:, PT_B:PT_B + 8192])
                    tap("row0", ROW[0:1, :])
                for j in range(4):
                    ob = 2 if (j % 2 == 0) else 4
                    w = sch.wqw(("ps", ob)) + sch.wqw(("ps", ob + 1)) + \
                        sch.wqw(("ps", 1)) + [expm]
                    rds = arslot(PT_B, 8192) + arslot(V_B, 16384)
                    for k in rds:
                        w += sch.rqw(k)
                    for kt in range(16):
                        lhs = ar(PT_B + 512 * kt + 128 * j, 128)
                        trip = [(PS[:, 512 * ob:512 * ob + 512], lhs,
                                 ar(V_B + 1024 * kt, 512)),
                                (PS[:, 512 * ob + 512:512 * ob + 1024], lhs,
                                 ar(V_B + 1024 * kt + 512, 512)),
                                (PS[:, 512 + j:512 + j + 1], lhs, ones_col)]
                        for u, (o, l, r) in enumerate(trip):
                            sch.op("tensor",
                                   (lambda _o=o, _l=l, _r=r, _st=(kt == 0),
                                    _sp=(kt == 15):
                                    (lambda e: e.matmul(_o, _l, _r, start=_st,
                                                        stop=_sp)))(),
                                   waits=(w if (kt == 0 and u == 0) else []),
                                   inc=("t" if (kt == 15 and u == 2) else None))
                    pvm = ("t", sch.cnt["t"])
                    sch.wrote(("ps", ob), pvm)
                    sch.wrote(("ps", ob + 1), pvm)
                    sch.wrote(("ps", 1), pvm)
                    for k in rds:
                        sch.read(k, pvm)
                    rc = ev(sch, "vector",
                            (lambda _j=j: (lambda e: e.reciprocal(
                                mcol(MC_RS + _j % 2), PS[:, 512 + _j:512 + _j + 1])))(),
                            rbanks=[1], wkeys=[("rs", j % 2)], inc="v")
                    jj = 4 * g + j
                    r_i = jj % 8
                    ev(sch, "scalar",
                       (lambda _r=r_i, _ob=ob, _j=j: (lambda e: e.activation(
                           ar(O_B + 1024 * _r, 1024),
                           PS[:, 512 * _ob:512 * _ob + 1024], AF.Identity,
                           scale=mcol(MC_RS + _j % 2))))(),
                       rbanks=[ob, ob + 1], rkeys=[("rs", j % 2)],
                       wkeys=arslot(O_B + 1024 * r_i, 1024), waits=[rc], inc="s")

                if g in (1, 3):
                    par = 0 if g == 1 else 1
                    for ec in range(CH):
                        slot = ec % 4
                        dm = dma(sch, "gpsimd",
                                 (lambda _ec=ec, _sl=slot: (lambda e: e.dma_start(
                                     out=wbs(_sl), in_=d_wpc[_ec])))(),
                                 wkeys=[("wb", slot)])
                        ob = 2 if (ec % 2 == 0) else 4
                        w = sch.wqw(("ps", ob)) + sch.wqw(("ps", ob + 1)) + [dm]
                        rds = arslot(O_B, 8192) + [("wb", slot)]
                        for k in rds:
                            w += sch.rqw(k)
                        for rc_ in range(CH):
                            for ih in range(2):
                                sch.op("tensor",
                                       (lambda _sl=slot, _rc=rc_, _ih=ih, _ob=ob:
                                        (lambda e: e.matmul(
                                            PS[:, 512 * (_ob + _ih):512 * (_ob + _ih) + 512],
                                            wbs(_sl, 128 * _rc, 128),
                                            ar(O_B + 1024 * _rc + 512 * _ih, 512),
                                            start=(_rc == 0), stop=(_rc == CH - 1))))(),
                                       waits=(w if (rc_ == 0 and ih == 0) else []),
                                       inc=("t" if (rc_ == CH - 1 and ih == 1) else None))
                        opm = ("t", sch.cnt["t"])
                        sch.wrote(("ps", ob), opm)
                        sch.wrote(("ps", ob + 1), opm)
                        for k in rds:
                            sch.read(k, opm)
                        ev(sch, "vector",
                           (lambda _ec=ec, _ob=ob, _par=par: (lambda e: e.scalar_tensor_tensor(
                               bass.AP(XF, S * _ec + _par, [[CH * S, 128], [2, 1024]]),
                               PS[:, 512 * _ob:512 * _ob + 1024],
                               mcol(MC_BPROJ + _ec),
                               bass.AP(XF, S * _ec + _par, [[CH * S, 128], [2, 1024]]),
                               ALU.add, ALU.add)))(),
                           rbanks=[ob, ob + 1],
                           wkeys=[("xfc", ec, t_) for t_ in range(NG)],
                           waits=[opm, sch.rd["misc"]], inc="v")

            if s_i == 0:
                tap("o", AR[:, O_B:O_B + 8192])
                tap("xf1", XF[:, :])
            def ffn_layer(tag):
                def f_ln(g):
                    ln_tile(TT * g,
                            (lambda _g=g: (lambda c: xn(c, TT * _g, TT)))(),
                            (lambda _g=g: (lambda c: xnslot(S * c + TT * _g, TT)))())

                dm_marks = {}

                def f_dm(c):
                    slot = 2 * (c % 2)
                    dmg = dma(sch, "gpsimd",
                              (lambda _c=c, _sl=slot: (lambda e: e.dma_start(
                                  out=wbs(_sl), in_=d_wg[_c])))(),
                              wkeys=[("wb", slot)])
                    dmu = dma(sch, "gpsimd",
                              (lambda _c=c, _sl=slot: (lambda e: e.dma_start(
                                  out=wbs(_sl + 1), in_=d_wu[_c])))(),
                              wkeys=[("wb", slot + 1)])
                    dm_marks[c] = (slot, dmg, dmu)

                def f_unit(c, g):
                    if c not in dm_marks:
                        f_dm(c)
                    slot, dmg, dmu = dm_marks[c]
                    bg = 2 * (g % 2)
                    bu = bg + 1
                    rds = [("wb", slot)] +                         [k for kk in range(CH) for k in xnslot(S * kk + TT * g, TT)]
                    gmg = mm_group(sch,
                                   [(bank(bg), wbs(slot, 128 * k, 128),
                                     xn(k, TT * g, TT)) for k in range(CH)],
                                   banks=[bg], reads=rds, waits=[dmg])
                    gmu = mm_group(sch,
                                   [(bank(bu), wbs(slot + 1, 128 * k, 128),
                                     xn(k, TT * g, TT)) for k in range(CH)],
                                   banks=[bu], reads=[("wb", slot + 1)],
                                   waits=[dmu])
                    sq_i = (2 * c + g) % 2
                    sqh = SQ[:, 512 * sq_i:512 * sq_i + 512]
                    sil = ev(sch, "scalar",
                             (lambda _bg=bg, _h=sqh: (lambda e: e.activation(
                                 _h, bank(_bg), AF.Silu, scale=DEQ)))(),
                             rbanks=[bg], wkeys=[("sq", sq_i)],
                             waits=[gmg], inc="s")
                    ev(sch, "vector",
                       (lambda _bu=bu, _h=sqh, _c=c, _g=g:
                        (lambda e: e.scalar_tensor_tensor(
                            ar(S * _c + TT * _g, TT), bank(_bu), DEQ,
                            _h, ALU.mult, ALU.mult)))(),
                       rbanks=[bu], rkeys=[("sq", sq_i)],
                       wkeys=arslot(S * c + TT * g, TT),
                       waits=[sil, gmu], inc="v")

                f_ln(0)
                order = [(0, 0), "L1", (0, 1), (1, 0), (1, 1), "L2",
                         (0, 2), (1, 2), "L3", (0, 3), (1, 3)]
                order += [(c, g) for c in range(2, HHC) for g in range(NG)]
                for u in order:
                    if u == "L1":
                        f_ln(1)
                    elif u == "L2":
                        f_ln(2)
                    elif u == "L3":
                        f_ln(3)
                    else:
                        f_unit(*u)
                if s_i == 0:
                    tap(tag + "xn", XN[:, :])
                if s_i == 0:
                    tap(tag + "h", AR[:, :])
                for ec in range(CH):
                    pieces = []
                    for i in range(3):
                        p_i = 3 * ec + i
                        slot = p_i % 4
                        dm = dma(sch, "gpsimd",
                                 (lambda _ec=ec, _i=i, _sl=slot: (lambda e: e.dma_start(
                                     out=wbs(_sl),
                                     in_=d_wdc[_ec][:, 1024 * _i:1024 * _i + 1024])))(),
                                 wkeys=[("wb", slot)])
                        pieces.append((slot, dm))
                    bb = 4 * (ec % 2)
                    w0 = sch.wqw(("ps", bb)) + sch.wqw(("ps", bb + 1)) + \
                        sch.wqw(("ps", bb + 2)) + sch.wqw(("ps", bb + 3))
                    rds = [("wb", pieces[i][0]) for i in range(3)] + \
                        [("ar", s_) for s_ in range(24)]
                    for c in range(HHC):
                        slot, dm = pieces[c // 8]
                        wchunk = []
                        if c % 8 == 0:
                            wchunk.append(dm)
                        wchunk += sch.rqw(("ar", (S * c) // S))
                        if c == 0:
                            wchunk += w0
                        for g in range(NG):
                            sch.op("tensor",
                                   (lambda _sl=slot, _c=c, _g=g, _b=bb + g:
                                    (lambda e: e.matmul(
                                        bank(_b), wbs(_sl, 128 * (_c % 8), 128),
                                        ar(S * _c + TT * _g, TT),
                                        start=(_c == 0), stop=(_c == HHC - 1))))(),
                                   waits=(wchunk if g == 0 else []),
                                   inc=("t" if (c == HHC - 1 and g == NG - 1) else None))
                    dwm = ("t", sch.cnt["t"])
                    for b_ in range(4):
                        sch.wrote(("ps", bb + b_), dwm)
                    for k in rds:
                        sch.read(k, dwm)
                    for g in range(NG):
                        ev(sch, "vector",
                           (lambda _ec=ec, _g=g, _b=bb + g: (lambda e: e.scalar_tensor_tensor(
                               xf(_ec, TT * _g, TT), bank(_b), DEQ,
                               xf(_ec, TT * _g, TT), ALU.mult, ALU.add)))(),
                           rbanks=[bb + g], wkeys=[("xfc", ec, g)],
                           waits=[dwm], inc="v")

            ffn_layer("f1")
            if s_i == 0:
                tap("xf2", XF[:, :])

            FW_Q, FW_K, FW_V, FW_O = 16384, 24576, 32768, 40960
            if s_i == 0:
                ld_qkn = dma(sch, "gpsimd",
                             lambda e: e.dma_start(
                                 out=bass.AP(AR, 0, [[24 * S, 128], [1024, CH], [1, 1024]]),
                                 in_=d_wbn0.rearrange("c p e -> p c e")),
                             wkeys=[("ar", s_) for s_ in range(4)])
                ld_vn = dma(sch, "gpsimd",
                            lambda e: e.dma_start(
                                out=bass.AP(AR, 8192, [[24 * S, 128], [1024, CH], [1, 1024]]),
                                in_=d_wbn1.rearrange("c p e -> p c e")),
                            wkeys=[("ar", s_) for s_ in range(4, 8)])
                ld_q2 = dma(sch, "gpsimd",
                            lambda e: e.dma_start(
                                out=bass.AP(AR, FW_Q, [[24 * S, 128], [1024, CH], [1, 1024]]),
                                in_=d_wb[2].rearrange("c p e -> p c e")),
                            wkeys=[("ar", s_) for s_ in range(8, 12)])
                ld_k2 = dma(sch, "gpsimd",
                            lambda e: e.dma_start(
                                out=bass.AP(AR, FW_K, [[24 * S, 128], [1024, CH], [1, 1024]]),
                                in_=d_wb[3].rearrange("c p e -> p c e")),
                            wkeys=[("ar", s_) for s_ in range(12, 16)])
                ld_v2 = dma(sch, "gpsimd",
                            lambda e: e.dma_start(
                                out=bass.AP(AR, FW_V, [[24 * S, 128], [1024, CH], [1, 1024]]),
                                in_=d_wb[4].rearrange("c p e -> p c e")),
                            wkeys=[("ar", s_) for s_ in range(16, 20)])
                ld_o = dma(sch, "gpsimd",
                           lambda e: e.dma_start(
                               out=bass.AP(AR, FW_O, [[24 * S, 128], [1024, CH], [1, 1024]]),
                               in_=d_wb[5].rearrange("c p e -> p c e")),
                           wkeys=[("ar", s_) for s_ in range(20, 24)])

                bbm = ev(sch, "vector",
                         lambda e: e.tensor_copy(BB[:, 0:8], MISC[:, MC_BQK:MC_BQK + 8]),
                         waits=[sch.rd["misc"]], inc="v")
                bbm = ev(sch, "vector",
                         lambda e: e.tensor_copy(BB[:, 8:16], MISC[:, MC_BV1:MC_BV1 + 8]),
                         waits=[bbm], inc="v")

                def fused_bias(wbase, bb_col, add_col, dst_col, ldm, bbw):
                    w7 = sch.wqw(("ps", 7))
                    for m in range(CH):
                        for hc in range(CH):
                            sch.op("tensor",
                                   (lambda _m=m, _hc=hc, _wb=wbase, _bc=bb_col,
                                    _st=(hc == 0), _sp=(hc == CH - 1):
                                    (lambda e: e.matmul(
                                        PS[:, 3584 + _m:3584 + _m + 1],
                                        ar(_wb + 1024 * _hc + 128 * _m, 128),
                                        BB[:, _bc + _hc:_bc + _hc + 1],
                                        start=_st, stop=_sp)))(),
                                   waits=([ldm, bbw] + w7 if (m == 0 and hc == 0) else []),
                                   inc=("t" if hc == CH - 1 else None))
                    mk = ("t", sch.cnt["t"])
                    sch.wrote(("ps", 7), mk)
                    for m in range(CH):
                        ev(sch, "vector",
                           (lambda _m=m, _a=add_col, _d=dst_col:
                            (lambda e: e.scalar_tensor_tensor(
                                mcol(_d + _m), PS[:, 3584 + _m:3584 + _m + 1], 1.0,
                                MISC[:, _a + _m:_a + _m + 1], ALU.mult, ALU.add)))(),
                           rbanks=([7] if m == 0 else []),
                           waits=[mk, sch.rd["misc"]], inc="v")
                    mk2 = ("v", sch.cnt["v"])
                    sch.read(("ps", 7), mk2)
                    return mk2

                fused_bias(FW_Q, 0, MC_BQ2, MC_BQ2F, ld_q2, bbm)
                fused_bias(FW_K, 0, MC_BK2, MC_BK2F, ld_k2, bbm)
                bvm = fused_bias(FW_V, 8, MC_BV2, MC_BVF, ld_v2, bbm)
                bbm2 = ev(sch, "vector",
                          lambda e: e.tensor_copy(BB[:, 16:24], MISC[:, MC_BVF:MC_BVF + 8]),
                          waits=[bvm], inc="v")
                fused_bias(FW_O, 16, MC_BOUT, MC_BCOMB, ld_o, bbm2)

                def fused_w(lh_base, rh_base, xn_base, lh_ld, rh_ld):
                    for m in range(CH):
                        for nh in range(2):
                            b = (2 * m + nh) % 6
                            rds = arslot(lh_base, 8192) + arslot(rh_base, 8192)
                            gm = mm_group(
                                sch,
                                [(bank(b), ar(lh_base + 1024 * hc + 128 * m, 128),
                                  ar(rh_base + 1024 * hc + 512 * nh, 512))
                                 for hc in range(CH)],
                                banks=[b], reads=rds, waits=[lh_ld, rh_ld])
                            ev(sch, "scalar",
                               (lambda _m=m, _nh=nh, _b=b, _x=xn_base:
                                (lambda e: e.activation(
                                    XN[:, _x + 1024 * _m + 512 * _nh:
                                       _x + 1024 * _m + 512 * _nh + 512],
                                    bank(_b), AF.Copy)))(),
                               rbanks=[b],
                               wkeys=xnslot(xn_base + 1024 * m + 512 * nh, 512),
                               waits=[gm], inc="s")

                fused_w(0, FW_Q, 0, ld_qkn, ld_q2)
                fused_w(0, FW_K, 8192, ld_qkn, ld_k2)
                cpq = ev(sch, "vector",
                         lambda e: e.tensor_copy(AR[:, FW_Q:FW_Q + 8192],
                                                 XN[:, 0:8192]),
                         wkeys=[("ar", s_) for s_ in range(8, 12)],
                         rkeys=xnslot(0, 8192), inc="v")
                cpk = ev(sch, "vector",
                         lambda e: e.tensor_copy(AR[:, FW_K:FW_K + 8192],
                                                 XN[:, 8192:16384]),
                         wkeys=[("ar", s_) for s_ in range(12, 16)],
                         rkeys=xnslot(8192, 8192), inc="v")
                fused_w(8192, FW_V, 0, ld_vn, ld_v2)
                cpv = ev(sch, "vector",
                         lambda e: e.tensor_copy(AR[:, FW_V:FW_V + 8192],
                                                 XN[:, 0:8192]),
                         wkeys=[("ar", s_) for s_ in range(16, 20)],
                         rkeys=xnslot(0, 8192), inc="v")
                dma(sch, "gpsimd",
                    lambda e: e.dma_start(
                        out=d_fw.rearrange("i c p e -> p i c e"),
                        in_=bass.AP(AR, FW_Q, [[24 * S, 128], [8192, 3], [1, 8192]])),
                    waits=[cpq, cpk, cpv])
                dma(sch, "sync",
                    lambda e: e.dma_start(out=d_fb, in_=MISC[:, 56:96]),
                    waits=[("v", sch.cnt["v"])])
                sch.rd["misc"] = ("v", sch.cnt["v"])
            else:
                m1 = dma(sch, "gpsimd",
                         lambda e: e.dma_start(
                             out=bass.AP(AR, FW_Q, [[24 * S, 128], [8192, 3], [1, 8192]]),
                             in_=d_fw.rearrange("i c p e -> p i c e")),
                         wkeys=[("ar", s_) for s_ in range(8, 20)])
                m2 = dma(sch, "gpsimd",
                         lambda e: e.dma_start(
                             out=bass.AP(AR, FW_O, [[24 * S, 128], [1024, CH], [1, 1024]]),
                             in_=d_wb[5].rearrange("c p e -> p c e")),
                         wkeys=[("ar", s_) for s_ in range(20, 24)])
                m3 = dma(sch, "sync",
                         lambda e: e.dma_start(out=MISC[:, 56:96], in_=d_fb),
                         waits=[("v", sch.cnt["v"]), ("s", sch.cnt["s"])])
                sch.rd["misc"] = m3
                for s_ in range(8, 20):
                    sch.wrote(("ar", s_), m1)
                sch.wrote(("ar", 20), m2)

            def blk_base(b):
                if b % 2 == 0:
                    return XN, 0
                return AR, 0

            def breg(b, which, off, n):
                T_, c0 = blk_base(b)
                base = c0 + {"xnb": 0, "q": 4096, "k": 8192, "v": 12288,
                             "o": 0}[which] + off
                return T_[:, base:base + n]

            def bkeys(b, which, off, n):
                base = {"xnb": 0, "q": 4096, "k": 8192, "v": 12288, "o": 0}[which] + off
                if b % 2 == 0:
                    return xnslot(base, n)
                return arslot(base, n)

            def proj_unit(b, which, wbase, bias_col, m):
                bk = m % 2
                rds = [("ar", (wbase + 1024 * kk) // S) for kk in range(CH)]
                for kk in range(CH):
                    rds += bkeys(b, "xnb", 512 * kk, 512)
                gm = mm_group(sch,
                              [(bank(bk), ar(wbase + 1024 * kk + 128 * m, 128),
                                breg(b, "xnb", 512 * kk, 512)) for kk in range(CH)],
                              banks=[bk], reads=rds)
                if bias_col is None:
                    ev(sch, "scalar",
                       (lambda _b=b, _w=which, _m=m, _bk=bk: (lambda e: e.activation(
                           breg(_b, _w, 512 * _m, 512), bank(_bk), AF.Copy)))(),
                       rbanks=[bk], wkeys=bkeys(b, which, 512 * m, 512),
                       waits=[gm], inc="s")
                else:
                    ev(sch, "scalar",
                       (lambda _b=b, _w=which, _m=m, _bk=bk, _bc=bias_col:
                        (lambda e: e.activation(
                            breg(_b, _w, 512 * _m, 512), bank(_bk), AF.Identity,
                            bias=mcol(_bc + _m))))(),
                       rbanks=[bk], wkeys=bkeys(b, which, 512 * m, 512),
                       waits=[gm, sch.rd["misc"]], inc="s")

            def vtm_unit(b, tt, h2):
                bk = (2 * tt + h2) % 2
                rds = [("ar", (FW_V + 1024 * kk) // S) for kk in range(CH)]
                for kk in range(CH):
                    rds += bkeys(b, "xnb", 512 * kk, 512)
                gm = mm_group(sch,
                              [(bank(bk),
                                breg(b, "xnb", 512 * kk + 128 * tt, 128),
                                ar(FW_V + 1024 * kk + 512 * h2, 512))
                               for kk in range(CH)],
                              banks=[bk], reads=rds)
                ev(sch, "scalar",
                   (lambda _b=b, _tt=tt, _h2=h2, _bk=bk: (lambda e: e.activation(
                       breg(_b, "v", 1024 * _tt + 512 * _h2, 512), bank(_bk),
                       AF.Copy)))(),
                   rbanks=[bk], wkeys=bkeys(b, "v", 1024 * tt + 512 * h2, 512),
                   waits=[gm], inc="s")

            def build_ph1(b):
                units = [lambda b=b: ln_tile(
                    TT * b, (lambda c: breg(b, "xnb", 512 * c, 512)),
                    (lambda c: bkeys(b, "xnb", 512 * c, 512)))]
                for m in range(CH):
                    units.append(lambda b=b, m=m: proj_unit(b, "q", FW_Q, MC_BQ2F, m))
                for m in range(CH):
                    units.append(lambda b=b, m=m: proj_unit(b, "k", FW_K, MC_BK2F, m))
                for tt in range(4):
                    for h2 in range(2):
                        units.append(lambda b=b, tt=tt, h2=h2: vtm_unit(b, tt, h2))
                return units

            def b_scores(b, jj):
                pslot = 3 * (jj % 2)
                w = sch.wqw(("ps", 2)) + sch.wqw(("ps", 3))
                rds = bkeys(b, "k", 0, 4096) + bkeys(b, "q", 0, 4096)
                for h in range(CH):
                    sch.op("tensor",
                           (lambda _b=b, _h=h, _jj=jj: (lambda e: e.matmul(
                               PS[:, 1024 + 128 * _h:1024 + 128 * _h + 128],
                               breg(_b, "k", 512 * _h + 128 * _jj, 128),
                               breg(_b, "q", 512 * _h + 128 * _jj, 128),
                               start=True, stop=True)))(),
                           waits=(w + [sch.rqw(k_)[0] for k_ in rds if sch.rqw(k_)]
                                  if h == 0 else []),
                           inc=("t" if h == CH - 1 else None))
                sw = ("t", sch.cnt["t"])
                sch.wrote(("ps", 2), sw)
                sch.wrote(("ps", 3), sw)
                for k_ in rds:
                    sch.read(k_, sw)
                exm = ev(sch, "scalar",
                         (lambda _ps=pslot: (lambda e: e.activation(
                             wbs(_ps, 0, 1024), PS[:, 1024:2048], AF.Exp,
                             scale=B_SCALE, bias=MISC[:, MC_NEG8:MC_NEG8 + 1])))(),
                         rbanks=[2, 3], wkeys=[("wb", pslot)],
                         waits=[sw, sch.rd["misc"]], inc="s")
                return exm

            def b_tail(b, jj, exm):
                pslot = 3 * (jj % 2)
                pmm = ev(sch, "vector",
                         (lambda _ps=pslot: (lambda e: e.scalar_tensor_tensor(
                             wbs(1, 0, 1024), wbs(_ps, 0, 1024), 0.0,
                             mask_ap, ALU.bypass, ALU.mult)))(),
                         wkeys=[("wb", 1)], rkeys=[("wb", pslot)],
                         waits=[exm, sch.rd["cbf"]], inc="v")
                w2 = sch.wqw(("ps", 2)) + sch.wqw(("ps", 3))
                for h in range(CH):
                    sch.op("tensor",
                           (lambda _h=h: (lambda e: e.matmul(
                               PS[0:1, 1024 + 128 * _h:1024 + 128 * _h + 128],
                               ones_col, wbs(1, 128 * _h, 128),
                               start=True, stop=True)))(),
                           waits=((w2 + [pmm]) if h == 0 else []),
                           inc=("t" if h == CH - 1 else None))
                smw = ("t", sch.cnt["t"])
                sch.wrote(("ps", 2), smw)
                sch.wrote(("ps", 3), smw)
                sch.read(("wb", 1), smw)
                rc0 = ev(sch, "scalar",
                         lambda e: e.activation(PS[0:1, 2048:3072],
                                                PS[0:1, 1024:2048], AF.Ln),
                         rbanks=[2, 3], wkeys=[("psrow45",)], waits=[smw] +
                         sch.wqw(("ps", 4)) + sch.wqw(("ps", 5)), inc="s")
                sch.wrote(("ps", 4), rc0)
                sch.wrote(("ps", 5), rc0)
                rcm = ev(sch, "scalar",
                         lambda e: e.activation(WB[0:1, 2048:3072],
                                                PS[0:1, 2048:3072], AF.Exp,
                                                scale=-1.0),
                         rbanks=[4, 5], wkeys=[("wb", 2)], waits=[rc0], inc="s")
                mm_group(sch, [(PS[:, 1024:1536], ones_row, WB[0:1, 2048:2560])],
                         banks=[2], reads=[("wb", 2)], waits=[rcm])
                mm_group(sch, [(PS[:, 1536:2048], ones_row, WB[0:1, 2560:3072])],
                         banks=[3], reads=[("wb", 2)])
                cbm = ("t", sch.cnt["t"])
                ptm = ev(sch, "vector",
                         (lambda _ps=pslot: (lambda e: e.tensor_tensor(
                             wbs(_ps, 0, 1024), wbs(1, 0, 1024),
                             PS[:, 1024:2048], ALU.mult)))(),
                         rbanks=[2, 3], wkeys=[("wb", pslot)], rkeys=[("wb", 1)],
                         waits=[cbm], inc="v")
                w3 = sch.wqw(("ps", 4)) + sch.wqw(("ps", 5))
                rds3 = bkeys(b, "v", 1024 * jj, 1024)
                for h in range(CH):
                    sch.op("tensor",
                           (lambda _b=b, _h=h, _jj=jj, _ps=pslot: (lambda e: e.matmul(
                               PS[:, 2048 + 128 * _h:2048 + 128 * _h + 128],
                               breg(_b, "v", 1024 * _jj + 128 * _h, 128),
                               wbs(_ps, 128 * _h, 128),
                               start=True, stop=True)))(),
                           waits=((w3 + [ptm] +
                                   [sch.rqw(k_)[0] for k_ in rds3 if sch.rqw(k_)])
                                  if h == 0 else []),
                           inc=("t" if h == CH - 1 else None))
                ovm = ("t", sch.cnt["t"])
                sch.wrote(("ps", 4), ovm)
                sch.wrote(("ps", 5), ovm)
                sch.read(("wb", pslot), ovm)
                for k_ in rds3:
                    sch.read(k_, ovm)
                T_, c0 = blk_base(b)
                pitch = (CH * S) if (b % 2 == 0) else (24 * S)
                ev(sch, "scalar",
                   (lambda _T=T_, _o=c0 + 128 * jj, _p=pitch: (lambda e: e.activation(
                       bass.AP(_T, _o, [[_p, 128], [512, 8], [1, 128]]),
                       PS[:, 2048:3072].rearrange("p (h x) -> p h x", x=128),
                       AF.Copy)))(),
                   rbanks=[4, 5], wkeys=bkeys(b, "o", 0, 4096),
                   waits=[ovm], inc="s")

            def outproj_unit(b, m):
                bk = 4 + m % 2
                t0 = TT * b
                rds = [("ar", (FW_O + 1024 * kk) // S) for kk in range(CH)]
                for kk in range(CH):
                    rds += bkeys(b, "o", 512 * kk, 512)
                gm = mm_group(sch,
                              [(bank(bk), ar(FW_O + 1024 * kk + 128 * m, 128),
                                breg(b, "o", 512 * kk, 512)) for kk in range(CH)],
                              banks=[bk], reads=rds)
                ev(sch, "vector",
                   (lambda _m=m, _bk=bk, _t0=t0: (lambda e: e.scalar_tensor_tensor(
                       xf(_m, _t0, TT), bank(_bk), mcol(MC_BCOMB + _m),
                       xf(_m, _t0, TT), ALU.add, ALU.add)))(),
                   rbanks=[bk], wkeys=[("xfc", m, t0 // 512)],
                   waits=[gm, sch.rd["misc"]], inc="v")

            def build_ph2(b):
                units = []
                exms = {}

                def u_s(jj):
                    exms[jj] = b_scores(b, jj)

                def u_t(jj):
                    b_tail(b, jj, exms[jj])

                units.append(lambda: u_s(0))
                units.append(lambda: (u_s(1), u_t(0)))
                units.append(lambda: (u_s(2), u_t(1)))
                units.append(lambda: (u_s(3), u_t(2)))
                units.append(lambda: u_t(3))
                for m in range(CH):
                    units.append(lambda b=b, m=m: outproj_unit(b, m))
                return units

            def weave(l1, l2):
                n1, n2 = len(l1), len(l2)
                i1 = 0
                for i2 in range(n2):
                    tgt = (i2 + 1) * n1 // n2
                    while i1 < tgt:
                        l1[i1]()
                        i1 += 1
                    l2[i2]()
                while i1 < n1:
                    l1[i1]()
                    i1 += 1

            for b in range(NG + 1):
                l1 = build_ph1(b) if b < NG else []
                l2 = build_ph2(b - 1) if b >= 1 else []
                weave(l1, l2)

            if s_i == 0:
                tap("xf3", XF[:, :])
            ffn_layer("f2")

            for c in range(CH):
                yw2 = []
                for t_ in range(NG):
                    yw2 += sch.rqw(("xfc", c, t_))
                yst = dma(sch, "sync",
                          (lambda _s=s_i, _c=c: (lambda e: e.dma_start(
                              out=d_y[_s][_c], in_=xf(_c, 0, S))))(),
                          waits=yw2)
                for t_ in range(NG):
                    sch.read(("xfc", c, t_), yst)

        sch.op("sync", None, waits=[("y", sch.cnt["y"])])

        block = ctx.enter_context(nc.Block())

        @block.tensor
        def _(e):
            for f in sch.ops["tensor"]:
                f(e)

        @block.vector
        def _(e):
            for f in sch.ops["vector"]:
                f(e)

        @block.scalar
        def _(e):
            for f in sch.ops["scalar"]:
                f(e)

        @block.gpsimd
        def _(e):
            for f in sch.ops["gpsimd"]:
                f(e)

        @block.sync
        def _(e):
            for f in sch.ops["sync"]:
                f(e)

    return nc


def host_inputs(L0_attn_a_w_qkv, L0_attn_a_w_proj, L0_attn_a_b_proj,
                L0_attn_b_proj_qk_w, L0_attn_b_proj_qk_b, L0_attn_b_proj_v_w,
                L0_attn_b_proj_v_b, L0_attn_b_in_proj_w, L0_attn_b_in_proj_b,
                L0_attn_b_out_proj_w, L0_attn_b_out_proj_b,
                L0_ffn_w_gate, L0_ffn_w_up, L0_ffn_w_down):
    f = lambda a: np.ascontiguousarray(np.asarray(a, dtype=np.float32))
    ii = lambda a: np.ascontiguousarray(np.asarray(a, dtype=np.int32))
    wqkv = f(L0_attn_a_w_qkv)
    wq_img = wqkv[0:1024].reshape(8, 128, 1024)
    wk_img = wqkv[1024:2048].reshape(8, 128, 1024)
    wqk = np.ascontiguousarray(np.stack([wq_img, wk_img], axis=1))
    wvT = np.ascontiguousarray(wqkv[2048:3072].T).reshape(8, 128, 1024)
    wpT = np.ascontiguousarray(f(L0_attn_a_w_proj).T)
    wpc = np.ascontiguousarray(
        wpT.reshape(8, 128, 8, 128).transpose(2, 1, 0, 3)).reshape(8, 128, 1024)
    wg = np.ascontiguousarray(
        ii(L0_ffn_w_gate).reshape(24, 128, 8, 128).transpose(0, 3, 2, 1)).reshape(24, 128, 1024)
    wu = np.ascontiguousarray(
        ii(L0_ffn_w_up).reshape(24, 128, 8, 128).transpose(0, 3, 2, 1)).reshape(24, 128, 1024)
    wdc = np.ascontiguousarray(
        ii(L0_ffn_w_down).reshape(8, 128, 24, 128).transpose(0, 3, 2, 1)).reshape(8, 128, 3072)
    inw = f(L0_attn_b_in_proj_w)
    wb_list = []
    for W in (f(L0_attn_b_proj_qk_w), f(L0_attn_b_proj_v_w), inw[0:1024],
              inw[1024:2048], inw[2048:3072], f(L0_attn_b_out_proj_w)):
        wb_list.append(np.ascontiguousarray(np.ascontiguousarray(W.T)).reshape(8, 128, 1024))
    misc = np.zeros((128, 58), np.float32)
    inb = f(L0_attn_b_in_proj_b)
    for c0, b in ((MC_BPROJ, f(L0_attn_a_b_proj)), (MC_BQK, f(L0_attn_b_proj_qk_b)),
                  (MC_BV1, f(L0_attn_b_proj_v_b)), (MC_BQ2, inb[0:1024]),
                  (MC_BK2, inb[1024:2048]), (MC_BV2, inb[2048:3072]),
                  (MC_BOUT, f(L0_attn_b_out_proj_b))):
        misc[:, c0:c0 + 8] = b.reshape(8, 128).T
    misc[:, MC_EPS] = 1e-5
    misc[:, MC_NEG8] = -8.0
    cbf = np.zeros((128, 256), np.float32)
    p = np.arange(128)
    m01 = ((p[:, None] // 4 == p[None, :] // 4) & (p[:, None] != p[None, :]))
    cbf[:, 0:128] = m01.astype(np.float32)
    cbf[:, 128:256] = 1.0
    wbn0 = np.ascontiguousarray(f(L0_attn_b_proj_qk_w)).reshape(8, 128, 1024)
    wbn1 = np.ascontiguousarray(f(L0_attn_b_proj_v_w)).reshape(8, 128, 1024)
    return dict(wqk=wqk, wvT=wvT, wpc=wpc, wg=wg, wu=wu, wdc=wdc,
                wb0=wb_list[0], wb1=wb_list[1], wb2=wb_list[2], wb3=wb_list[3],
                wb4=wb_list[4], wb5=wb_list[5], wbn0=wbn0, wbn1=wbn1,
                misc=misc, cbf=cbf)


def kernel(L0_attn_a_x, L0_attn_a_w_qkv, L0_attn_a_w_proj, L0_attn_a_b_proj,
           L0_attn_b_proj_qk_w, L0_attn_b_proj_qk_b, L0_attn_b_proj_v_w,
           L0_attn_b_proj_v_b, L0_attn_b_in_proj_w, L0_attn_b_in_proj_b,
           L0_attn_b_out_proj_w, L0_attn_b_out_proj_b,
           L0_ffn_w_gate, L0_ffn_w_up, L0_ffn_w_down):
    x = np.ascontiguousarray(np.asarray(L0_attn_a_x, dtype=np.float32))
    shared = host_inputs(L0_attn_a_w_qkv, L0_attn_a_w_proj, L0_attn_a_b_proj,
                         L0_attn_b_proj_qk_w, L0_attn_b_proj_qk_b,
                         L0_attn_b_proj_v_w, L0_attn_b_proj_v_b,
                         L0_attn_b_in_proj_w, L0_attn_b_in_proj_b,
                         L0_attn_b_out_proj_w, L0_attn_b_out_proj_b,
                         L0_ffn_w_gate, L0_ffn_w_up, L0_ffn_w_down)
    nc = build()
    in_maps = []
    for c in range(8):
        xi = np.ascontiguousarray(
            x[2 * c:2 * c + 2].transpose(0, 2, 1)).reshape(NSEQ, CH, 128, S)
        m = dict(shared)
        m["x"] = xi
        in_maps.append(m)
    res = run_bass_kernel_spmd(nc, in_maps, core_ids=list(range(8)))
    out = np.empty((16, S, D), np.float32)
    for c in range(8):
        y = np.asarray(res.results[c]["y"])
        out[2 * c:2 * c + 2] = y.reshape(NSEQ, D, S).transpose(0, 2, 1)
    return out


# version 12
# speedup vs baseline: 1.1218x; 1.0112x over previous
import numpy as np
from contextlib import ExitStack

import concourse.bass as bass
import concourse.mybir as mybir
from concourse.bass_utils import run_bass_kernel_spmd

F32 = mybir.dt.float32
BF16 = mybir.dt.bfloat16
I32 = mybir.dt.int32
AF = mybir.ActivationFunctionType
ALU = mybir.AluOpType

D = 1024
S = 2048
NSEQ = 2
CH = 8
TT = 512
NG = S // TT
HHC = 24
A_SCALE = 0.125
B_SCALE = float(1.0 / np.sqrt(128.0))
DEQ = float(1.0 / 127.0)
ENGINES = ("tensor", "vector", "scalar", "gpsimd", "sync")

MC_BPROJ, MC_BQK, MC_BV1, MC_BQ2, MC_BK2, MC_BV2, MC_BOUT = 0, 8, 16, 24, 32, 40, 48
MC_EPS, MC_NEG8, MC_BCOMB, MC_RS = 56, 57, 58, 66
MC_BQ2F, MC_BK2F, MC_BVF = 72, 80, 88


def _marks(ws):
    return [m for m in ws if m is not None and m[0] is not None]


class Sched:
    def __init__(self, sems):
        self.sems = sems
        self.ops = {e: [] for e in ENGINES}
        self.cnt = {k: 0 for k in sems}
        self.lastwait = {}
        self.rd = {}
        self.tw = {}
        self.tr = {}
        self.gq = 0
        self.sq = 0

    def op(self, eng, fn, waits=(), inc=None, n=1):
        w = []
        for m in _marks(waits):
            sname, v = m
            assert v <= self.cnt[sname], (sname, v, self.cnt[sname])
            if self.lastwait.get((eng, sname), -1) < v:
                w.append((sname, v))
                self.lastwait[(eng, sname)] = v
        if inc is not None:
            self.cnt[inc] += n
        post = self.cnt[inc] if inc is not None else None
        sems = self.sems

        def emit(e, _fn=fn, _w=tuple(w), _inc=inc, _n=n):
            for sname, v in _w:
                e.wait_ge(sems[sname], v)
            instr = _fn(e) if _fn is not None else None
            if _inc is not None:
                instr.then_inc(sems[_inc], _n)
        self.ops[eng].append(emit)
        return (inc, post) if inc is not None else None

    def wqw(self, key):
        return self.tr.pop(key, []) + ([self.tw[key]] if key in self.tw else [])

    def wrote(self, key, mark):
        if mark is not None:
            self.tw[key] = mark

    def rqw(self, key):
        return [self.tw[key]] if key in self.tw else []

    def read(self, key, mark):
        if mark is not None:
            self.tr.setdefault(key, []).append(mark)


def mm_group(sch, mms, banks, reads=(), waits=(), inc="t"):
    w = list(waits)
    for b in banks:
        w += sch.wqw(("ps", b))
    for k in reads:
        w += sch.rqw(k)
    n = len(mms)
    mark = None
    for i, (o, l, r) in enumerate(mms):
        mark = sch.op(
            "tensor",
            (lambda _o=o, _l=l, _r=r, _st=(i == 0), _sp=(i == n - 1):
             (lambda e: e.matmul(_o, _l, _r, start=_st, stop=_sp)))(),
            waits=(w if i == 0 else ()),
            inc=(inc if i == n - 1 else None))
    for b in banks:
        sch.wrote(("ps", b), mark)
    for k in reads:
        sch.read(k, mark)
    return mark


def ev(sch, eng, fn, rbanks=(), wkeys=(), rkeys=(), waits=(), inc=None, n=1):
    w = list(waits)
    for b in rbanks:
        w += sch.rqw(("ps", b))
    for k in wkeys:
        w += sch.wqw(k)
    for k in rkeys:
        w += sch.rqw(k)
    mark = sch.op(eng, fn, waits=w, inc=inc, n=n)
    for b in rbanks:
        sch.read(("ps", b), mark)
    for k in wkeys:
        sch.wrote(k, mark)
    for k in rkeys:
        sch.read(k, mark)
    return mark


def dma(sch, eng, fn, wkeys=(), rkeys=(), waits=()):
    if eng == "gpsimd":
        sname = "g%d" % sch.gq
        sch.gq = (sch.gq + 1) % 4
    else:
        sname = "y%d" % sch.sq
        sch.sq = (sch.sq + 1) % 2
    w = list(waits) + [(sname, sch.cnt[sname])]
    return ev(sch, eng, fn, wkeys=wkeys, rkeys=rkeys, waits=w, inc=sname, n=16)


def build(taps=None):
    taps = taps or {}
    nc = bass.Bass("TRN2", target_bir_lowering=False)

    d_x = nc.dram_tensor("x", [NSEQ, CH, 128, S], F32, kind="ExternalInput").ap()
    d_wqk = nc.dram_tensor("wqk", [CH, 2, 128, 1024], F32, kind="ExternalInput").ap()
    d_wvT = nc.dram_tensor("wvT", [CH, 128, 1024], F32, kind="ExternalInput").ap()
    d_wpc = nc.dram_tensor("wpc", [CH, 128, 1024], F32, kind="ExternalInput").ap()
    d_wg = nc.dram_tensor("wg", [HHC, 128, 1024], I32, kind="ExternalInput").ap()
    d_wu = nc.dram_tensor("wu", [HHC, 128, 1024], I32, kind="ExternalInput").ap()
    d_wdc = nc.dram_tensor("wdc", [CH, 128, 3072], I32, kind="ExternalInput").ap()
    d_wb = [nc.dram_tensor(f"wb{i}", [CH, 128, 1024], F32, kind="ExternalInput").ap()
            for i in range(6)]
    d_wbn0 = nc.dram_tensor("wbn0", [CH, 128, 1024], F32, kind="ExternalInput").ap()
    d_wbn1 = nc.dram_tensor("wbn1", [CH, 128, 1024], F32, kind="ExternalInput").ap()
    d_fw = nc.dram_tensor("fw", [3, CH, 128, 1024], BF16).ap()
    d_fb = nc.dram_tensor("fb", [128, 40], F32).ap()
    d_fm = nc.dram_tensor("fm", [128, 8192], BF16).ap()
    d_misc = nc.dram_tensor("misc", [128, 58], F32, kind="ExternalInput").ap()
    d_cbf = nc.dram_tensor("cbf", [128, 256], F32, kind="ExternalInput").ap()
    d_y = nc.dram_tensor("y", [NSEQ, CH, 128, S], F32, kind="ExternalOutput").ap()
    d_taps = {name: nc.dram_tensor("tap_" + name, shape, dt, kind="ExternalOutput").ap()
              for name, (shape, dt) in taps.items()}

    with ExitStack() as ctx:
        XF = ctx.enter_context(nc.sbuf_tensor("XF", [128, CH * S], F32))
        XN = ctx.enter_context(nc.sbuf_tensor("XN", [128, CH * S], BF16))
        AR = ctx.enter_context(nc.sbuf_tensor("AR", [128, 24 * S], BF16))
        WB = ctx.enter_context(nc.sbuf_tensor("WB", [128, 4 * 1024], BF16))
        SQ = ctx.enter_context(nc.sbuf_tensor("SQ", [128, 1024], BF16))
        ROW = ctx.enter_context(nc.sbuf_tensor("ROW", [1, 1024], BF16))
        ROWF = ctx.enter_context(nc.sbuf_tensor("ROWF", [1, 512], F32))
        MISC = ctx.enter_context(nc.sbuf_tensor("MISC", [128, 96], F32))
        CBF = ctx.enter_context(nc.sbuf_tensor("CBF", [128, 256], BF16))
        BV2 = ctx.enter_context(nc.sbuf_tensor("BV2", [128, 8], BF16))
        BB = ctx.enter_context(nc.sbuf_tensor("BB", [128, 24], BF16))
        PS = ctx.enter_context(nc.psum_tensor("PS", [128, 4096], F32))
        sems = {k: ctx.enter_context(nc.semaphore("sem_" + k))
                for k in ("t", "v", "s", "y0", "y1", "g0", "g1", "g2", "g3")}
        sch = Sched(sems)

        def ar(c0, n):
            return AR[:, c0:c0 + n]

        def xf(c, t0, n):
            return XF[:, S * c + t0: S * c + t0 + n]

        def xn(c, t0, n):
            return XN[:, S * c + t0: S * c + t0 + n]

        def sl(i, off, n):
            return XN[:, 4096 * i + off: 4096 * i + off + n]

        def wbs(s_, off=0, n=1024):
            return WB[:, 1024 * s_ + off: 1024 * s_ + off + n]

        def bank(b, r0=0, r1=128, c0=0, n=512):
            return PS[r0:r1, 512 * b + c0: 512 * b + c0 + n]

        def mcol(c, n=1):
            return MISC[:, c:c + n]

        ones_col = CBF[:, 128:129]
        ones_row = CBF[0:1, 128:256]
        mask_ap = bass.AP(CBF, 0, [[256, 128], [0, 8], [1, 128]])

        G_B, V_B, M_B, PT_B, WVT_B, O_B = 0, S * 8, S * 16, S * 16, S * 20, S * 20

        def arslot(col0, ncols):
            s0, s1 = col0 // S, (col0 + ncols - 1) // S
            return [("ar", s_) for s_ in range(s0, s1 + 1)]

        def xnslot(col0, ncols):
            s0, s1 = col0 // 512, (col0 + ncols - 1) // 512
            return [("xnc", s_) for s_ in range(s0, s1 + 1)]

        def xfkeys(c, t0, n):
            return [("xfc", c, t_) for t_ in range(t0 // 512, (t0 + n - 1) // 512 + 1)]

        def tap(name, src_ap):
            if name not in d_taps:
                return
            waits = [(s_, sch.cnt[s_]) for s_ in sch.cnt]
            dma(sch, "gpsimd",
                (lambda _s=src_ap, _d=d_taps[name]: (lambda e: e.dma_start(out=_d, in_=_s)))(),
                waits=waits)
            for e in ENGINES:
                sch.op(e, None, waits=[(s_, sch.cnt[s_]) for s_ in sch.cnt])

        mi = dma(sch, "sync", lambda e: e.dma_start(out=MISC[:, 0:58], in_=d_misc))
        cb = dma(sch, "gpsimd", lambda e: e.dma_start(out=CBF[:, :], in_=d_cbf))
        sch.rd["misc"] = mi
        sch.rd["cbf"] = cb
        sch.op("vector", lambda e: e.memset(MISC[:, 68:72], 0.0), inc="v")
        bv2m = sch.op("vector", lambda e: e.tensor_copy(BV2[:, :], MISC[:, MC_BV2:MC_BV2 + 8]),
                      waits=[mi], inc="v")
        sch.rd["bv2"] = bv2m

        def ln_tile(src_t0, dst_fn, dst_keys_fn):
            casts = []
            for c in range(CH):
                m = ev(sch, "vector",
                       (lambda _c=c: (lambda e: e.tensor_copy(
                           dst_fn(_c), xf(_c, src_t0, TT))))(),
                       wkeys=dst_keys_fn(c), rkeys=xfkeys(c, src_t0, TT), inc="v")
                casts.append(m)
            mm_group(sch, [(bank(6, 0, 1), ones_col, dst_fn(c)) for c in range(CH)],
                     banks=[6], waits=[casts[-1], sch.rd["cbf"]])
            s1m = ("t", sch.cnt["t"])
            mubf = ev(sch, "scalar", lambda e: e.activation(
                ROW[0:1, 0:512], bank(6, 0, 1), AF.Copy, scale=1.0 / 1024.0),
                rbanks=[6], wkeys=[("row", 0)], waits=[s1m], inc="s")
            mm_group(sch, [(bank(6), ones_row, ROW[0:1, 0:512])], banks=[6],
                     reads=[("row", 0)], waits=[mubf])
            bcm1 = ("t", sch.cnt["t"])
            tmarks = []
            for c in range(CH):
                m = ev(sch, "vector",
                       (lambda _c=c: (lambda e: e.scalar_tensor_tensor(
                           dst_fn(_c), xf(_c, src_t0, TT), 1.0, bank(6),
                           ALU.bypass, ALU.subtract)))(),
                       rbanks=[6], rkeys=xfkeys(c, src_t0, TT), waits=[bcm1], inc="v")
                for k in dst_keys_fn(c):
                    sch.wrote(k, m)
                tmarks.append(m)
            for c in range(CH):
                sqk = ("sq", c % 2)
                sqh = SQ[:, 512 * (c % 2):512 * (c % 2) + 512]
                sqm = ev(sch, "scalar",
                         (lambda _c=c, _h=sqh: (lambda e: e.activation(
                             _h, dst_fn(_c), AF.Square)))(),
                         wkeys=[sqk], waits=[tmarks[c]], inc="s")
                w = [sqm] + (sch.wqw(("ps", 7)) if c == 0 else [])
                mm = sch.op("tensor",
                            (lambda _h=sqh, _st=(c == 0), _sp=(c == CH - 1):
                             (lambda e: e.matmul(bank(7, 0, 1), ones_col, _h,
                                                 start=_st, stop=_sp)))(),
                            waits=w, inc="t")
                sch.read(sqk, mm)
            s2m = ("t", sch.cnt["t"])
            sch.wrote(("ps", 7), s2m)
            lnv = ev(sch, "scalar", lambda e: e.activation(
                ROWF[0:1, :], bank(7, 0, 1), AF.Ln, scale=1.0 / 1024.0,
                bias=MISC[0:1, MC_EPS:MC_EPS + 1]),
                rbanks=[7], wkeys=[("rowf",)], waits=[s2m, sch.rd["misc"]], inc="s")
            rr = ev(sch, "scalar", lambda e: e.activation(
                ROW[0:1, 512:1024], ROWF[0:1, :], AF.Exp, scale=-0.5),
                rkeys=[("rowf",)], wkeys=[("row", 1)], waits=[lnv], inc="s")
            mm_group(sch, [(bank(7), ones_row, ROW[0:1, 512:1024])], banks=[7],
                     reads=[("row", 1)], waits=[rr])
            bcm2 = ("t", sch.cnt["t"])
            rsb = ev(sch, "vector", lambda e: e.tensor_copy(SQ[:, 0:512], bank(7)),
                     rbanks=[7], wkeys=[("sq", 0)], waits=[bcm2], inc="v")
            for c in range(CH):
                t2 = ev(sch, "vector",
                        (lambda _c=c: (lambda e: e.tensor_tensor(
                            dst_fn(_c), dst_fn(_c), SQ[:, 0:512], ALU.mult)))(),
                        rkeys=[("sq", 0)], waits=[rsb], inc="v")
                for k in dst_keys_fn(c):
                    sch.wrote(k, t2)

        def proj_fm(wtile_fn, wread_keys, src_fn, src_keys_fn, dst_fn, dst_keys_fn,
                    bias_col, banks, waits=()):
            for m in range(CH):
                b = banks[m % len(banks)]
                rds = list(wread_keys)
                for k in range(CH):
                    rds += src_keys_fn(k)
                gm = mm_group(sch, [(bank(b), wtile_fn(k, m), src_fn(k))
                                    for k in range(CH)],
                              banks=[b], reads=rds, waits=list(waits))
                ev(sch, "scalar",
                   (lambda _m=m, _b=b, _bc=bias_col: (lambda e: e.activation(
                       dst_fn(_m), bank(_b), AF.Identity, bias=mcol(_bc + _m))))(),
                   rbanks=[b], wkeys=dst_keys_fn(m),
                   waits=[gm, sch.rd["misc"]], inc="s")

        for s_i in range(NSEQ):
            xw = []
            for c in range(CH):
                for t_ in range(NG):
                    xw += sch.wqw(("xfc", c, t_))
            xld = dma(sch, "sync",
                      (lambda _s=s_i: (lambda e: e.dma_start(
                          out=bass.AP(XF, 0, [[CH * S, 128], [S, CH], [1, S]]),
                          in_=d_x[_s].rearrange("c p t -> p c t"))))(),
                      waits=xw)
            for c in range(CH):
                for t_ in range(NG):
                    sch.wrote(("xfc", c, t_), xld)

            groups = [(ec, eh) for ec in range(CH) for eh in range(2)]
            waves = [groups[0:6], groups[6:12], groups[12:16]]

            def emit_mwave(wave):
                bank_w = {gi: sch.wqw(("ps", gi)) for gi in range(len(wave))}
                for oc in range(CH):
                    mk = None
                    for gi, (ec, eh) in enumerate(wave):
                        mk = sch.op("tensor",
                                    (lambda _oc=oc, _ec=ec, _eh=eh, _b=gi,
                                     _st=(oc == 0), _sp=(oc == CH - 1):
                                     (lambda e: e.matmul(
                                         bank(_b),
                                         ar(2048 * _oc + 128 * _ec, 128),
                                         ar(2048 * _oc + 1024 + 512 * _eh, 512),
                                         start=_st, stop=_sp)))(),
                                    waits=(([wq_dm[oc]] if gi == 0 else []) +
                                           (bank_w[gi] if oc == 0 else [])),
                                    inc=("t" if gi == len(wave) - 1 else None))
                for gi, (ec, eh) in enumerate(wave):
                    mkk = ("t", sch.cnt["t"])
                    sch.wrote(("ps", gi), mkk)
                    for s_ in range(8):
                        sch.read(("ar", s_), mkk)
                    ev(sch, "vector",
                       (lambda _ec=ec, _eh=eh, _b=gi: (lambda e: e.tensor_copy(
                           ar(M_B + 1024 * _ec + 512 * _eh, 512), bank(_b))))(),
                       rbanks=[gi], wkeys=arslot(M_B + 1024 * ec + 512 * eh, 512),
                       inc="v")

            def emit_ln1(g):
                ln_tile(TT * g,
                        (lambda _g=g: (lambda c: xn(c, TT * _g, TT)))(),
                        (lambda _g=g: (lambda c: xnslot(S * c + TT * _g, TT)))())

            def emit_vproj(tg):
                grp = [(tt, h2) for tt in range(4) for h2 in range(2)]
                for wave in (grp[:6], grp[6:]):
                    for gi, (tt, h2) in enumerate(wave):
                        b = gi
                        rds = [k for kk in range(CH) for k in xnslot(S * kk + TT * tg, TT)]
                        rds += arslot(WVT_B, 8192)
                        gm = mm_group(sch,
                                      [(bank(b), xn(k, TT * tg + 128 * tt, 128),
                                        ar(WVT_B + 1024 * k + 512 * h2, 512))
                                       for k in range(CH)],
                                      banks=[b], reads=rds, waits=[wvld])
                        ev(sch, "scalar",
                           (lambda _tg=tg, _tt=tt, _h2=h2, _b=b: (lambda e: e.activation(
                               ar(V_B + 1024 * (4 * _tg + _tt) + 512 * _h2, 512),
                               bank(_b), AF.Copy)))(),
                           rbanks=[b],
                           wkeys=arslot(V_B + 1024 * (4 * tg + tt) + 512 * h2, 512),
                           waits=[gm], inc="s")

            if s_i == 0:
                wq_dm = []
                for oc in range(CH):
                    wq_dm.append(dma(
                        sch, "gpsimd",
                        (lambda _oc=oc: (lambda e: e.dma_start(
                            out=AR[:, 2048 * _oc:2048 * _oc + 2048],
                            in_=d_wqk[_oc].rearrange("w p e -> p w e"))))(),
                        wkeys=[("ar", oc // 1)] if False else [("ar", oc)]))
                emit_mwave(waves[0])
                emit_ln1(0)
                emit_mwave(waves[1])
                emit_ln1(1)
                emit_mwave(waves[2])
                dma(sch, "gpsimd",
                    lambda e: e.dma_start(out=d_fm, in_=AR[:, M_B:M_B + 8192]),
                    rkeys=arslot(M_B, 8192), waits=[("v", sch.cnt["v"])])
                tap("m", AR[:, M_B:M_B + 8192])
            else:
                dma(sch, "gpsimd",
                    lambda e: e.dma_start(out=AR[:, M_B:M_B + 8192], in_=d_fm),
                    wkeys=arslot(M_B, 8192))
                emit_ln1(0)
                emit_ln1(1)
            wvld = dma(sch, "gpsimd",
                       lambda e: e.dma_start(
                           out=bass.AP(AR, WVT_B, [[24 * S, 128], [1024, CH], [1, 1024]]),
                           in_=d_wvT.rearrange("c p e -> p c e")),
                       wkeys=arslot(WVT_B, 8192))
            emit_ln1(2)
            emit_vproj(0)
            emit_ln1(3)
            if s_i == 0:
                tap("xn1", XN[:, :])
            for tg in range(1, NG):
                emit_vproj(tg)
            if s_i == 0:
                tap("v", AR[:, V_B:V_B + 16384])
            def emit_negc(g):
                for c in range(CH):
                    sqk = ("sq", c % 2)
                    sqh = SQ[:, 512 * (c % 2):512 * (c % 2) + 512]
                    sqm = ev(sch, "scalar",
                             (lambda _c=c, _g=g, _h=sqh: (lambda e: e.activation(
                                 _h, ar(G_B + S * _c + TT * _g, TT), AF.Square)))(),
                             wkeys=[sqk], rkeys=arslot(G_B + S * c + TT * g, TT),
                             inc="s")
                    w = [sqm] + (sch.wqw(("ps", 6)) if c == 0 else [])
                    mm = sch.op("tensor",
                                (lambda _h=sqh, _st=(c == 0), _sp=(c == CH - 1):
                                 (lambda e: e.matmul(bank(6, 0, 1), ones_col, _h,
                                                     start=_st, stop=_sp)))(),
                                waits=w, inc="t")
                    sch.read(sqk, mm)
                gsm = ("t", sch.cnt["t"])
                sch.wrote(("ps", 6), gsm)
                l1 = ev(sch, "scalar", lambda e: e.activation(
                    ROWF[0:1, :], bank(6, 0, 1), AF.Ln,
                    bias=MISC[0:1, MC_EPS:MC_EPS + 1]),
                    rbanks=[6], wkeys=[("rowf",)],
                    waits=[gsm, sch.rd["misc"]], inc="s")
                l2 = ev(sch, "scalar", lambda e: e.activation(
                    bank(6, 0, 1), ROWF[0:1, :], AF.Exp, scale=0.5),
                    rbanks=[6], rkeys=[("rowf",)], waits=[l1], inc="s")
                sch.wrote(("ps", 6), l2)
                return ev(sch, "vector",
                          (lambda _g=g: (lambda e: e.tensor_scalar(
                              ROW[0:1, 512 * (_g % 2):512 * (_g % 2) + 512],
                              bank(6, 0, 1), -32.2, None, ALU.mult)))(),
                          rbanks=[6], wkeys=[("row", g % 2)], waits=[l2], inc="v")

            ngc_marks = {}
            for g in range(NG):
                for m in range(CH):
                    b = m % 6
                    rds = arslot(M_B, 8192) + \
                        [k for kk in range(CH) for k in xnslot(S * kk + TT * g, TT)]
                    gm = mm_group(sch,
                                  [(bank(b),
                                    ar(M_B + 1024 * k, 1024)[:, 128 * m:128 * m + 128],
                                    xn(k, TT * g, TT)) for k in range(CH)],
                                  banks=[b], reads=rds)
                    ev(sch, "scalar",
                       (lambda _g=g, _m=m, _b=b: (lambda e: e.activation(
                           ar(G_B + S * _m + TT * _g, TT), bank(_b), AF.Copy)))(),
                       rbanks=[b], wkeys=arslot(G_B + S * m + TT * g, TT),
                       waits=[gm], inc="s")
                if g < 2 and m == CH - 1:
                    ngc_marks[g] = emit_negc(g)

            if s_i == 0:
                tap("g", AR[:, 0:16384])
            for g in range(NG):
                ngc = ngc_marks[g]
                for kt in range(16):
                    b = kt % 2
                    rds = [k for kk in range(CH) for k in xnslot(S * kk, S)]
                    rds += [("ar", kk) for kk in range(CH)] + [("row", g % 2)]
                    mms = [(bank(b), xn(k, 128 * kt, 128),
                            ar(G_B + S * k + TT * g, TT)) for k in range(CH)]
                    mms.append((bank(b), ones_row,
                                ROW[0:1, 512 * (g % 2):512 * (g % 2) + 512]))
                    gm = mm_group(sch, mms, banks=[b], reads=rds, waits=[ngc])
                    ev(sch, "scalar",
                       (lambda _kt=kt, _b=b: (lambda e: e.activation(
                           ar(PT_B + 512 * _kt, 512), bank(_b), AF.Exp,
                           scale=A_SCALE)))(),
                       rbanks=[b], wkeys=arslot(PT_B + 512 * kt, 512),
                       waits=[gm], inc="s")
                expm = ("s", sch.cnt["s"])
                if g + 2 < NG:
                    ngc_marks[g + 2] = emit_negc(g + 2)
                if s_i == 0 and g == 0:
                    tap("pt0", AR[:, PT_B:PT_B + 8192])
                    tap("row0", ROW[0:1, :])
                for j in range(4):
                    ob = 2 if (j % 2 == 0) else 4
                    w = sch.wqw(("ps", ob)) + sch.wqw(("ps", ob + 1)) + \
                        sch.wqw(("ps", 1)) + [expm]
                    rds = arslot(PT_B, 8192) + arslot(V_B, 16384)
                    for k in rds:
                        w += sch.rqw(k)
                    for kt in range(16):
                        lhs = ar(PT_B + 512 * kt + 128 * j, 128)
                        trip = [(PS[:, 512 * ob:512 * ob + 512], lhs,
                                 ar(V_B + 1024 * kt, 512)),
                                (PS[:, 512 * ob + 512:512 * ob + 1024], lhs,
                                 ar(V_B + 1024 * kt + 512, 512)),
                                (PS[:, 512 + j:512 + j + 1], lhs, ones_col)]
                        for u, (o, l, r) in enumerate(trip):
                            sch.op("tensor",
                                   (lambda _o=o, _l=l, _r=r, _st=(kt == 0),
                                    _sp=(kt == 15):
                                    (lambda e: e.matmul(_o, _l, _r, start=_st,
                                                        stop=_sp)))(),
                                   waits=(w if (kt == 0 and u == 0) else []),
                                   inc=("t" if (kt == 15 and u == 2) else None))
                    pvm = ("t", sch.cnt["t"])
                    sch.wrote(("ps", ob), pvm)
                    sch.wrote(("ps", ob + 1), pvm)
                    sch.wrote(("ps", 1), pvm)
                    for k in rds:
                        sch.read(k, pvm)
                    rc = ev(sch, "vector",
                            (lambda _j=j: (lambda e: e.reciprocal(
                                mcol(MC_RS + _j % 2), PS[:, 512 + _j:512 + _j + 1])))(),
                            rbanks=[1], wkeys=[("rs", j % 2)], inc="v")
                    jj = 4 * g + j
                    r_i = jj % 8
                    ev(sch, "scalar",
                       (lambda _r=r_i, _ob=ob, _j=j: (lambda e: e.activation(
                           ar(O_B + 1024 * _r, 1024),
                           PS[:, 512 * _ob:512 * _ob + 1024], AF.Identity,
                           scale=mcol(MC_RS + _j % 2))))(),
                       rbanks=[ob, ob + 1], rkeys=[("rs", j % 2)],
                       wkeys=arslot(O_B + 1024 * r_i, 1024), waits=[rc], inc="s")

                if g in (1, 3):
                    par = 0 if g == 1 else 1
                    for ec in range(CH):
                        slot = ec % 4
                        dm = dma(sch, "gpsimd",
                                 (lambda _ec=ec, _sl=slot: (lambda e: e.dma_start(
                                     out=wbs(_sl), in_=d_wpc[_ec])))(),
                                 wkeys=[("wb", slot)])
                        ob = 2 if (ec % 2 == 0) else 4
                        w = sch.wqw(("ps", ob)) + sch.wqw(("ps", ob + 1)) + [dm]
                        rds = arslot(O_B, 8192) + [("wb", slot)]
                        for k in rds:
                            w += sch.rqw(k)
                        for rc_ in range(CH):
                            for ih in range(2):
                                sch.op("tensor",
                                       (lambda _sl=slot, _rc=rc_, _ih=ih, _ob=ob:
                                        (lambda e: e.matmul(
                                            PS[:, 512 * (_ob + _ih):512 * (_ob + _ih) + 512],
                                            wbs(_sl, 128 * _rc, 128),
                                            ar(O_B + 1024 * _rc + 512 * _ih, 512),
                                            start=(_rc == 0), stop=(_rc == CH - 1))))(),
                                       waits=(w if (rc_ == 0 and ih == 0) else []),
                                       inc=("t" if (rc_ == CH - 1 and ih == 1) else None))
                        opm = ("t", sch.cnt["t"])
                        sch.wrote(("ps", ob), opm)
                        sch.wrote(("ps", ob + 1), opm)
                        for k in rds:
                            sch.read(k, opm)
                        ev(sch, "vector",
                           (lambda _ec=ec, _ob=ob, _par=par: (lambda e: e.scalar_tensor_tensor(
                               bass.AP(XF, S * _ec + _par, [[CH * S, 128], [2, 1024]]),
                               PS[:, 512 * _ob:512 * _ob + 1024],
                               mcol(MC_BPROJ + _ec),
                               bass.AP(XF, S * _ec + _par, [[CH * S, 128], [2, 1024]]),
                               ALU.add, ALU.add)))(),
                           rbanks=[ob, ob + 1],
                           wkeys=[("xfc", ec, t_) for t_ in range(NG)],
                           waits=[opm, sch.rd["misc"]], inc="v")

            if s_i == 0:
                tap("o", AR[:, O_B:O_B + 8192])
                tap("xf1", XF[:, :])
            def ffn_layer(tag):
                def f_ln(g):
                    ln_tile(TT * g,
                            (lambda _g=g: (lambda c: xn(c, TT * _g, TT)))(),
                            (lambda _g=g: (lambda c: xnslot(S * c + TT * _g, TT)))())

                dm_marks = {}

                def f_dm(c):
                    slot = 2 * (c % 2)
                    dmg = dma(sch, "gpsimd",
                              (lambda _c=c, _sl=slot: (lambda e: e.dma_start(
                                  out=wbs(_sl), in_=d_wg[_c])))(),
                              wkeys=[("wb", slot)])
                    dmu = dma(sch, "gpsimd",
                              (lambda _c=c, _sl=slot: (lambda e: e.dma_start(
                                  out=wbs(_sl + 1), in_=d_wu[_c])))(),
                              wkeys=[("wb", slot + 1)])
                    dm_marks[c] = (slot, dmg, dmu)

                def f_unit(c, g):
                    if c not in dm_marks:
                        f_dm(c)
                    slot, dmg, dmu = dm_marks[c]
                    bg = 2 * (g % 2)
                    bu = bg + 1
                    rds = [("wb", slot)] +                         [k for kk in range(CH) for k in xnslot(S * kk + TT * g, TT)]
                    gmg = mm_group(sch,
                                   [(bank(bg), wbs(slot, 128 * k, 128),
                                     xn(k, TT * g, TT)) for k in range(CH)],
                                   banks=[bg], reads=rds, waits=[dmg])
                    gmu = mm_group(sch,
                                   [(bank(bu), wbs(slot + 1, 128 * k, 128),
                                     xn(k, TT * g, TT)) for k in range(CH)],
                                   banks=[bu], reads=[("wb", slot + 1)],
                                   waits=[dmu])
                    sq_i = (2 * c + g) % 2
                    sqh = SQ[:, 512 * sq_i:512 * sq_i + 512]
                    sil = ev(sch, "scalar",
                             (lambda _bg=bg, _h=sqh: (lambda e: e.activation(
                                 _h, bank(_bg), AF.Silu, scale=DEQ)))(),
                             rbanks=[bg], wkeys=[("sq", sq_i)],
                             waits=[gmg], inc="s")
                    ev(sch, "vector",
                       (lambda _bu=bu, _h=sqh, _c=c, _g=g:
                        (lambda e: e.scalar_tensor_tensor(
                            ar(S * _c + TT * _g, TT), bank(_bu), DEQ,
                            _h, ALU.mult, ALU.mult)))(),
                       rbanks=[bu], rkeys=[("sq", sq_i)],
                       wkeys=arslot(S * c + TT * g, TT),
                       waits=[sil, gmu], inc="v")

                f_ln(0)
                order = [(0, 0), "L1", (0, 1), (1, 0), (1, 1), "L2",
                         (0, 2), (1, 2), "L3", (0, 3), (1, 3)]
                order += [(c, g) for c in range(2, HHC) for g in range(NG)]
                for u in order:
                    if u == "L1":
                        f_ln(1)
                    elif u == "L2":
                        f_ln(2)
                    elif u == "L3":
                        f_ln(3)
                    else:
                        f_unit(*u)
                if s_i == 0:
                    tap(tag + "xn", XN[:, :])
                if s_i == 0:
                    tap(tag + "h", AR[:, :])
                for ec in range(CH):
                    pieces = []
                    for i in range(3):
                        p_i = 3 * ec + i
                        slot = p_i % 4
                        dm = dma(sch, "gpsimd",
                                 (lambda _ec=ec, _i=i, _sl=slot: (lambda e: e.dma_start(
                                     out=wbs(_sl),
                                     in_=d_wdc[_ec][:, 1024 * _i:1024 * _i + 1024])))(),
                                 wkeys=[("wb", slot)])
                        pieces.append((slot, dm))
                    bb = 4 * (ec % 2)
                    w0 = sch.wqw(("ps", bb)) + sch.wqw(("ps", bb + 1)) + \
                        sch.wqw(("ps", bb + 2)) + sch.wqw(("ps", bb + 3))
                    rds = [("wb", pieces[i][0]) for i in range(3)]
                    for c in range(HHC):
                        slot, dm = pieces[c // 8]
                        wchunk = []
                        if c % 8 == 0:
                            wchunk.append(dm)
                        wchunk += sch.rqw(("ar", (S * c) // S))
                        if c == 0:
                            wchunk += w0
                        for g in range(NG):
                            mark_here = (c % 4 == 3 and g == NG - 1)
                            sch.op("tensor",
                                   (lambda _sl=slot, _c=c, _g=g, _b=bb + g:
                                    (lambda e: e.matmul(
                                        bank(_b), wbs(_sl, 128 * (_c % 8), 128),
                                        ar(S * _c + TT * _g, TT),
                                        start=(_c == 0), stop=(_c == HHC - 1))))(),
                                   waits=(wchunk if g == 0 else []),
                                   inc=("t" if mark_here else None))
                        if c % 4 == 3:
                            cm = ("t", sch.cnt["t"])
                            for cc in range(c - 3, c + 1):
                                sch.read(("ar", cc), cm)
                    dwm = ("t", sch.cnt["t"])
                    for b_ in range(4):
                        sch.wrote(("ps", bb + b_), dwm)
                    for k in rds:
                        sch.read(k, dwm)
                    for g in range(NG):
                        ev(sch, "vector",
                           (lambda _ec=ec, _g=g, _b=bb + g: (lambda e: e.scalar_tensor_tensor(
                               xf(_ec, TT * _g, TT), bank(_b), DEQ,
                               xf(_ec, TT * _g, TT), ALU.mult, ALU.add)))(),
                           rbanks=[bb + g], wkeys=[("xfc", ec, g)],
                           waits=[dwm], inc="v")

            ffn_layer("f1")
            if s_i == 0:
                tap("xf2", XF[:, :])

            FW_Q, FW_K, FW_V, FW_O = 16384, 24576, 32768, 40960
            if s_i == 0:
                ld_qkn = dma(sch, "gpsimd",
                             lambda e: e.dma_start(
                                 out=bass.AP(AR, 0, [[24 * S, 128], [1024, CH], [1, 1024]]),
                                 in_=d_wbn0.rearrange("c p e -> p c e")),
                             wkeys=[("ar", s_) for s_ in range(4)])
                ld_q2 = dma(sch, "gpsimd",
                            lambda e: e.dma_start(
                                out=bass.AP(AR, FW_Q, [[24 * S, 128], [1024, CH], [1, 1024]]),
                                in_=d_wb[2].rearrange("c p e -> p c e")),
                            wkeys=[("ar", s_) for s_ in range(8, 12)])
                ld_k2 = dma(sch, "gpsimd",
                            lambda e: e.dma_start(
                                out=bass.AP(AR, FW_K, [[24 * S, 128], [1024, CH], [1, 1024]]),
                                in_=d_wb[3].rearrange("c p e -> p c e")),
                            wkeys=[("ar", s_) for s_ in range(12, 16)])
                ld_vn = dma(sch, "gpsimd",
                            lambda e: e.dma_start(
                                out=bass.AP(AR, 8192, [[24 * S, 128], [1024, CH], [1, 1024]]),
                                in_=d_wbn1.rearrange("c p e -> p c e")),
                            wkeys=[("ar", s_) for s_ in range(4, 8)])
                ld_v2 = dma(sch, "gpsimd",
                            lambda e: e.dma_start(
                                out=bass.AP(AR, FW_V, [[24 * S, 128], [1024, CH], [1, 1024]]),
                                in_=d_wb[4].rearrange("c p e -> p c e")),
                            wkeys=[("ar", s_) for s_ in range(16, 20)])
                ld_o = dma(sch, "gpsimd",
                           lambda e: e.dma_start(
                               out=bass.AP(AR, FW_O, [[24 * S, 128], [1024, CH], [1, 1024]]),
                               in_=d_wb[5].rearrange("c p e -> p c e")),
                           wkeys=[("ar", s_) for s_ in range(20, 24)])

                bbm = ev(sch, "vector",
                         lambda e: e.tensor_copy(BB[:, 0:8], MISC[:, MC_BQK:MC_BQK + 8]),
                         waits=[sch.rd["misc"]], inc="v")
                bbm = ev(sch, "vector",
                         lambda e: e.tensor_copy(BB[:, 8:16], MISC[:, MC_BV1:MC_BV1 + 8]),
                         waits=[bbm], inc="v")

                def fused_bias(wbase, bb_col, add_col, dst_col, ldm, bbw):
                    w7 = sch.wqw(("ps", 7))
                    for m in range(CH):
                        for hc in range(CH):
                            sch.op("tensor",
                                   (lambda _m=m, _hc=hc, _wb=wbase, _bc=bb_col,
                                    _st=(hc == 0), _sp=(hc == CH - 1):
                                    (lambda e: e.matmul(
                                        PS[:, 3584 + _m:3584 + _m + 1],
                                        ar(_wb + 1024 * _hc + 128 * _m, 128),
                                        BB[:, _bc + _hc:_bc + _hc + 1],
                                        start=_st, stop=_sp)))(),
                                   waits=([ldm, bbw] + w7 if (m == 0 and hc == 0) else []),
                                   inc=("t" if hc == CH - 1 else None))
                    mk = ("t", sch.cnt["t"])
                    sch.wrote(("ps", 7), mk)
                    for m in range(CH):
                        ev(sch, "vector",
                           (lambda _m=m, _a=add_col, _d=dst_col:
                            (lambda e: e.scalar_tensor_tensor(
                                mcol(_d + _m), PS[:, 3584 + _m:3584 + _m + 1], 1.0,
                                MISC[:, _a + _m:_a + _m + 1], ALU.mult, ALU.add)))(),
                           rbanks=([7] if m == 0 else []),
                           waits=[mk, sch.rd["misc"]], inc="v")
                    mk2 = ("v", sch.cnt["v"])
                    sch.read(("ps", 7), mk2)
                    return mk2

                fused_bias(FW_Q, 0, MC_BQ2, MC_BQ2F, ld_q2, bbm)
                fused_bias(FW_K, 0, MC_BK2, MC_BK2F, ld_k2, bbm)
                bvm = fused_bias(FW_V, 8, MC_BV2, MC_BVF, ld_v2, bbm)
                bbm2 = ev(sch, "vector",
                          lambda e: e.tensor_copy(BB[:, 16:24], MISC[:, MC_BVF:MC_BVF + 8]),
                          waits=[bvm], inc="v")
                fused_bias(FW_O, 16, MC_BOUT, MC_BCOMB, ld_o, bbm2)

                def fused_w(lh_base, rh_base, xn_base, lh_ld, rh_ld):
                    for m in range(CH):
                        for nh in range(2):
                            b = (2 * m + nh) % 6
                            rds = arslot(lh_base, 8192) + arslot(rh_base, 8192)
                            gm = mm_group(
                                sch,
                                [(bank(b), ar(lh_base + 1024 * hc + 128 * m, 128),
                                  ar(rh_base + 1024 * hc + 512 * nh, 512))
                                 for hc in range(CH)],
                                banks=[b], reads=rds, waits=[lh_ld, rh_ld])
                            ev(sch, "scalar",
                               (lambda _m=m, _nh=nh, _b=b, _x=xn_base:
                                (lambda e: e.activation(
                                    XN[:, _x + 1024 * _m + 512 * _nh:
                                       _x + 1024 * _m + 512 * _nh + 512],
                                    bank(_b), AF.Copy)))(),
                               rbanks=[b],
                               wkeys=xnslot(xn_base + 1024 * m + 512 * nh, 512),
                               waits=[gm], inc="s")

                fused_w(0, FW_Q, 0, ld_qkn, ld_q2)
                fused_w(0, FW_K, 8192, ld_qkn, ld_k2)
                cpq = ev(sch, "vector",
                         lambda e: e.tensor_copy(AR[:, FW_Q:FW_Q + 8192],
                                                 XN[:, 0:8192]),
                         wkeys=[("ar", s_) for s_ in range(8, 12)],
                         rkeys=xnslot(0, 8192), inc="v")
                cpk = ev(sch, "vector",
                         lambda e: e.tensor_copy(AR[:, FW_K:FW_K + 8192],
                                                 XN[:, 8192:16384]),
                         wkeys=[("ar", s_) for s_ in range(12, 16)],
                         rkeys=xnslot(8192, 8192), inc="v")
                fused_w(8192, FW_V, 0, ld_vn, ld_v2)
                cpv = ev(sch, "vector",
                         lambda e: e.tensor_copy(AR[:, FW_V:FW_V + 8192],
                                                 XN[:, 0:8192]),
                         wkeys=[("ar", s_) for s_ in range(16, 20)],
                         rkeys=xnslot(0, 8192), inc="v")
                dma(sch, "gpsimd",
                    lambda e: e.dma_start(
                        out=d_fw.rearrange("i c p e -> p i c e"),
                        in_=bass.AP(AR, FW_Q, [[24 * S, 128], [8192, 3], [1, 8192]])),
                    waits=[cpq, cpk, cpv])
                dma(sch, "sync",
                    lambda e: e.dma_start(out=d_fb, in_=MISC[:, 56:96]),
                    waits=[("v", sch.cnt["v"])])
                sch.rd["misc"] = ("v", sch.cnt["v"])
            else:
                m1 = dma(sch, "gpsimd",
                         lambda e: e.dma_start(
                             out=bass.AP(AR, FW_Q, [[24 * S, 128], [8192, 3], [1, 8192]]),
                             in_=d_fw.rearrange("i c p e -> p i c e")),
                         wkeys=[("ar", s_) for s_ in range(8, 20)])
                m2 = dma(sch, "gpsimd",
                         lambda e: e.dma_start(
                             out=bass.AP(AR, FW_O, [[24 * S, 128], [1024, CH], [1, 1024]]),
                             in_=d_wb[5].rearrange("c p e -> p c e")),
                         wkeys=[("ar", s_) for s_ in range(20, 24)])
                m3 = dma(sch, "sync",
                         lambda e: e.dma_start(out=MISC[:, 56:96], in_=d_fb),
                         waits=[("v", sch.cnt["v"]), ("s", sch.cnt["s"])])
                sch.rd["misc"] = m3
                for s_ in range(8, 20):
                    sch.wrote(("ar", s_), m1)
                sch.wrote(("ar", 20), m2)

            def blk_base(b):
                if b % 2 == 0:
                    return XN, 0
                return AR, 0

            def breg(b, which, off, n):
                T_, c0 = blk_base(b)
                base = c0 + {"xnb": 0, "q": 4096, "k": 8192, "v": 12288,
                             "o": 0}[which] + off
                return T_[:, base:base + n]

            def bkeys(b, which, off, n):
                base = {"xnb": 0, "q": 4096, "k": 8192, "v": 12288, "o": 0}[which] + off
                if b % 2 == 0:
                    return xnslot(base, n)
                return arslot(base, n)

            def proj_unit(b, which, wbase, bias_col, m):
                bk = m % 2
                rds = [("ar", (wbase + 1024 * kk) // S) for kk in range(CH)]
                for kk in range(CH):
                    rds += bkeys(b, "xnb", 512 * kk, 512)
                gm = mm_group(sch,
                              [(bank(bk), ar(wbase + 1024 * kk + 128 * m, 128),
                                breg(b, "xnb", 512 * kk, 512)) for kk in range(CH)],
                              banks=[bk], reads=rds)
                if bias_col is None:
                    ev(sch, "scalar",
                       (lambda _b=b, _w=which, _m=m, _bk=bk: (lambda e: e.activation(
                           breg(_b, _w, 512 * _m, 512), bank(_bk), AF.Copy)))(),
                       rbanks=[bk], wkeys=bkeys(b, which, 512 * m, 512),
                       waits=[gm], inc="s")
                else:
                    ev(sch, "scalar",
                       (lambda _b=b, _w=which, _m=m, _bk=bk, _bc=bias_col:
                        (lambda e: e.activation(
                            breg(_b, _w, 512 * _m, 512), bank(_bk), AF.Identity,
                            bias=mcol(_bc + _m))))(),
                       rbanks=[bk], wkeys=bkeys(b, which, 512 * m, 512),
                       waits=[gm, sch.rd["misc"]], inc="s")

            def vtm_unit(b, tt, h2):
                bk = (2 * tt + h2) % 2
                rds = [("ar", (FW_V + 1024 * kk) // S) for kk in range(CH)]
                for kk in range(CH):
                    rds += bkeys(b, "xnb", 512 * kk, 512)
                gm = mm_group(sch,
                              [(bank(bk),
                                breg(b, "xnb", 512 * kk + 128 * tt, 128),
                                ar(FW_V + 1024 * kk + 512 * h2, 512))
                               for kk in range(CH)],
                              banks=[bk], reads=rds)
                ev(sch, "scalar",
                   (lambda _b=b, _tt=tt, _h2=h2, _bk=bk: (lambda e: e.activation(
                       breg(_b, "v", 1024 * _tt + 512 * _h2, 512), bank(_bk),
                       AF.Copy)))(),
                   rbanks=[bk], wkeys=bkeys(b, "v", 1024 * tt + 512 * h2, 512),
                   waits=[gm], inc="s")

            def build_ph1(b):
                units = [lambda b=b: ln_tile(
                    TT * b, (lambda c: breg(b, "xnb", 512 * c, 512)),
                    (lambda c: bkeys(b, "xnb", 512 * c, 512)))]
                for m in range(CH):
                    units.append(lambda b=b, m=m: proj_unit(b, "q", FW_Q, MC_BQ2F, m))
                for m in range(CH):
                    units.append(lambda b=b, m=m: proj_unit(b, "k", FW_K, MC_BK2F, m))
                for tt in range(4):
                    for h2 in range(2):
                        units.append(lambda b=b, tt=tt, h2=h2: vtm_unit(b, tt, h2))
                return units

            def b_scores(b, jj):
                pslot = 3 * (jj % 2)
                w = sch.wqw(("ps", 2)) + sch.wqw(("ps", 3))
                rds = bkeys(b, "k", 0, 4096) + bkeys(b, "q", 0, 4096)
                for h in range(CH):
                    sch.op("tensor",
                           (lambda _b=b, _h=h, _jj=jj: (lambda e: e.matmul(
                               PS[:, 1024 + 128 * _h:1024 + 128 * _h + 128],
                               breg(_b, "k", 512 * _h + 128 * _jj, 128),
                               breg(_b, "q", 512 * _h + 128 * _jj, 128),
                               start=True, stop=True)))(),
                           waits=(w + [sch.rqw(k_)[0] for k_ in rds if sch.rqw(k_)]
                                  if h == 0 else []),
                           inc=("t" if h == CH - 1 else None))
                sw = ("t", sch.cnt["t"])
                sch.wrote(("ps", 2), sw)
                sch.wrote(("ps", 3), sw)
                for k_ in rds:
                    sch.read(k_, sw)
                exm = ev(sch, "scalar",
                         (lambda _ps=pslot: (lambda e: e.activation(
                             wbs(_ps, 0, 1024), PS[:, 1024:2048], AF.Exp,
                             scale=B_SCALE, bias=MISC[:, MC_NEG8:MC_NEG8 + 1])))(),
                         rbanks=[2, 3], wkeys=[("wb", pslot)],
                         waits=[sw, sch.rd["misc"]], inc="s")
                return exm

            def b_tail(b, jj, exm):
                pslot = 3 * (jj % 2)
                pmm = ev(sch, "vector",
                         (lambda _ps=pslot: (lambda e: e.scalar_tensor_tensor(
                             wbs(1, 0, 1024), wbs(_ps, 0, 1024), 0.0,
                             mask_ap, ALU.bypass, ALU.mult)))(),
                         wkeys=[("wb", 1)], rkeys=[("wb", pslot)],
                         waits=[exm, sch.rd["cbf"]], inc="v")
                w2 = sch.wqw(("ps", 2)) + sch.wqw(("ps", 3))
                for h in range(CH):
                    sch.op("tensor",
                           (lambda _h=h: (lambda e: e.matmul(
                               PS[0:1, 1024 + 128 * _h:1024 + 128 * _h + 128],
                               ones_col, wbs(1, 128 * _h, 128),
                               start=True, stop=True)))(),
                           waits=((w2 + [pmm]) if h == 0 else []),
                           inc=("t" if h == CH - 1 else None))
                smw = ("t", sch.cnt["t"])
                sch.wrote(("ps", 2), smw)
                sch.wrote(("ps", 3), smw)
                sch.read(("wb", 1), smw)
                rc0 = ev(sch, "scalar",
                         lambda e: e.activation(PS[0:1, 2048:3072],
                                                PS[0:1, 1024:2048], AF.Ln),
                         rbanks=[2, 3], wkeys=[("psrow45",)], waits=[smw] +
                         sch.wqw(("ps", 4)) + sch.wqw(("ps", 5)), inc="s")
                sch.wrote(("ps", 4), rc0)
                sch.wrote(("ps", 5), rc0)
                rcm = ev(sch, "scalar",
                         lambda e: e.activation(WB[0:1, 2048:3072],
                                                PS[0:1, 2048:3072], AF.Exp,
                                                scale=-1.0),
                         rbanks=[4, 5], wkeys=[("wb", 2)], waits=[rc0], inc="s")
                mm_group(sch, [(PS[:, 1024:1536], ones_row, WB[0:1, 2048:2560])],
                         banks=[2], reads=[("wb", 2)], waits=[rcm])
                mm_group(sch, [(PS[:, 1536:2048], ones_row, WB[0:1, 2560:3072])],
                         banks=[3], reads=[("wb", 2)])
                cbm = ("t", sch.cnt["t"])
                ptm = ev(sch, "vector",
                         (lambda _ps=pslot: (lambda e: e.tensor_tensor(
                             wbs(_ps, 0, 1024), wbs(1, 0, 1024),
                             PS[:, 1024:2048], ALU.mult)))(),
                         rbanks=[2, 3], wkeys=[("wb", pslot)], rkeys=[("wb", 1)],
                         waits=[cbm], inc="v")
                w3 = sch.wqw(("ps", 4)) + sch.wqw(("ps", 5))
                rds3 = bkeys(b, "v", 1024 * jj, 1024)
                for h in range(CH):
                    sch.op("tensor",
                           (lambda _b=b, _h=h, _jj=jj, _ps=pslot: (lambda e: e.matmul(
                               PS[:, 2048 + 128 * _h:2048 + 128 * _h + 128],
                               breg(_b, "v", 1024 * _jj + 128 * _h, 128),
                               wbs(_ps, 128 * _h, 128),
                               start=True, stop=True)))(),
                           waits=((w3 + [ptm] +
                                   [sch.rqw(k_)[0] for k_ in rds3 if sch.rqw(k_)])
                                  if h == 0 else []),
                           inc=("t" if h == CH - 1 else None))
                ovm = ("t", sch.cnt["t"])
                sch.wrote(("ps", 4), ovm)
                sch.wrote(("ps", 5), ovm)
                sch.read(("wb", pslot), ovm)
                for k_ in rds3:
                    sch.read(k_, ovm)
                T_, c0 = blk_base(b)
                pitch = (CH * S) if (b % 2 == 0) else (24 * S)
                ev(sch, "scalar",
                   (lambda _T=T_, _o=c0 + 128 * jj, _p=pitch: (lambda e: e.activation(
                       bass.AP(_T, _o, [[_p, 128], [512, 8], [1, 128]]),
                       PS[:, 2048:3072].rearrange("p (h x) -> p h x", x=128),
                       AF.Copy)))(),
                   rbanks=[4, 5], wkeys=bkeys(b, "o", 0, 4096),
                   waits=[ovm], inc="s")

            def outproj_unit(b, m):
                bk = 4 + m % 2
                t0 = TT * b
                rds = [("ar", (FW_O + 1024 * kk) // S) for kk in range(CH)]
                for kk in range(CH):
                    rds += bkeys(b, "o", 512 * kk, 512)
                gm = mm_group(sch,
                              [(bank(bk), ar(FW_O + 1024 * kk + 128 * m, 128),
                                breg(b, "o", 512 * kk, 512)) for kk in range(CH)],
                              banks=[bk], reads=rds)
                ev(sch, "vector",
                   (lambda _m=m, _bk=bk, _t0=t0: (lambda e: e.scalar_tensor_tensor(
                       xf(_m, _t0, TT), bank(_bk), mcol(MC_BCOMB + _m),
                       xf(_m, _t0, TT), ALU.add, ALU.add)))(),
                   rbanks=[bk], wkeys=[("xfc", m, t0 // 512)],
                   waits=[gm, sch.rd["misc"]], inc="v")

            def build_ph2(b):
                units = []
                exms = {}

                def u_s(jj):
                    exms[jj] = b_scores(b, jj)

                def u_t(jj):
                    b_tail(b, jj, exms[jj])

                units.append(lambda: u_s(0))
                units.append(lambda: (u_s(1), u_t(0)))
                units.append(lambda: (u_s(2), u_t(1)))
                units.append(lambda: (u_s(3), u_t(2)))
                units.append(lambda: u_t(3))
                for m in range(CH):
                    units.append(lambda b=b, m=m: outproj_unit(b, m))
                return units

            def weave(l1, l2):
                n1, n2 = len(l1), len(l2)
                i1 = 0
                for i2 in range(n2):
                    tgt = (i2 + 1) * n1 // n2
                    while i1 < tgt:
                        l1[i1]()
                        i1 += 1
                    l2[i2]()
                while i1 < n1:
                    l1[i1]()
                    i1 += 1

            for b in range(NG + 1):
                l1 = build_ph1(b) if b < NG else []
                l2 = build_ph2(b - 1) if b >= 1 else []
                weave(l1, l2)

            if s_i == 0:
                tap("xf3", XF[:, :])
            ffn_layer("f2")

            for c in range(CH):
                yw2 = []
                for t_ in range(NG):
                    yw2 += sch.rqw(("xfc", c, t_))
                yst = dma(sch, "sync",
                          (lambda _s=s_i, _c=c: (lambda e: e.dma_start(
                              out=d_y[_s][_c], in_=xf(_c, 0, S))))(),
                          waits=yw2)
                for t_ in range(NG):
                    sch.read(("xfc", c, t_), yst)

        sch.op("sync", None, waits=[("y0", sch.cnt["y0"]), ("y1", sch.cnt["y1"])])

        block = ctx.enter_context(nc.Block())

        @block.tensor
        def _(e):
            for f in sch.ops["tensor"]:
                f(e)

        @block.vector
        def _(e):
            for f in sch.ops["vector"]:
                f(e)

        @block.scalar
        def _(e):
            for f in sch.ops["scalar"]:
                f(e)

        @block.gpsimd
        def _(e):
            for f in sch.ops["gpsimd"]:
                f(e)

        @block.sync
        def _(e):
            for f in sch.ops["sync"]:
                f(e)

    return nc


def host_inputs(L0_attn_a_w_qkv, L0_attn_a_w_proj, L0_attn_a_b_proj,
                L0_attn_b_proj_qk_w, L0_attn_b_proj_qk_b, L0_attn_b_proj_v_w,
                L0_attn_b_proj_v_b, L0_attn_b_in_proj_w, L0_attn_b_in_proj_b,
                L0_attn_b_out_proj_w, L0_attn_b_out_proj_b,
                L0_ffn_w_gate, L0_ffn_w_up, L0_ffn_w_down):
    f = lambda a: np.ascontiguousarray(np.asarray(a, dtype=np.float32))
    ii = lambda a: np.ascontiguousarray(np.asarray(a, dtype=np.int32))
    wqkv = f(L0_attn_a_w_qkv)
    wq_img = wqkv[0:1024].reshape(8, 128, 1024)
    wk_img = wqkv[1024:2048].reshape(8, 128, 1024)
    wqk = np.ascontiguousarray(np.stack([wq_img, wk_img], axis=1))
    wvT = np.ascontiguousarray(wqkv[2048:3072].T).reshape(8, 128, 1024)
    wpT = np.ascontiguousarray(f(L0_attn_a_w_proj).T)
    wpc = np.ascontiguousarray(
        wpT.reshape(8, 128, 8, 128).transpose(2, 1, 0, 3)).reshape(8, 128, 1024)
    wg = np.ascontiguousarray(
        ii(L0_ffn_w_gate).reshape(24, 128, 8, 128).transpose(0, 3, 2, 1)).reshape(24, 128, 1024)
    wu = np.ascontiguousarray(
        ii(L0_ffn_w_up).reshape(24, 128, 8, 128).transpose(0, 3, 2, 1)).reshape(24, 128, 1024)
    wdc = np.ascontiguousarray(
        ii(L0_ffn_w_down).reshape(8, 128, 24, 128).transpose(0, 3, 2, 1)).reshape(8, 128, 3072)
    inw = f(L0_attn_b_in_proj_w)
    wb_list = []
    for W in (f(L0_attn_b_proj_qk_w), f(L0_attn_b_proj_v_w), inw[0:1024],
              inw[1024:2048], inw[2048:3072], f(L0_attn_b_out_proj_w)):
        wb_list.append(np.ascontiguousarray(np.ascontiguousarray(W.T)).reshape(8, 128, 1024))
    misc = np.zeros((128, 58), np.float32)
    inb = f(L0_attn_b_in_proj_b)
    for c0, b in ((MC_BPROJ, f(L0_attn_a_b_proj)), (MC_BQK, f(L0_attn_b_proj_qk_b)),
                  (MC_BV1, f(L0_attn_b_proj_v_b)), (MC_BQ2, inb[0:1024]),
                  (MC_BK2, inb[1024:2048]), (MC_BV2, inb[2048:3072]),
                  (MC_BOUT, f(L0_attn_b_out_proj_b))):
        misc[:, c0:c0 + 8] = b.reshape(8, 128).T
    misc[:, MC_EPS] = 1e-5
    misc[:, MC_NEG8] = -8.0
    cbf = np.zeros((128, 256), np.float32)
    p = np.arange(128)
    m01 = ((p[:, None] // 4 == p[None, :] // 4) & (p[:, None] != p[None, :]))
    cbf[:, 0:128] = m01.astype(np.float32)
    cbf[:, 128:256] = 1.0
    wbn0 = np.ascontiguousarray(f(L0_attn_b_proj_qk_w)).reshape(8, 128, 1024)
    wbn1 = np.ascontiguousarray(f(L0_attn_b_proj_v_w)).reshape(8, 128, 1024)
    return dict(wqk=wqk, wvT=wvT, wpc=wpc, wg=wg, wu=wu, wdc=wdc,
                wb0=wb_list[0], wb1=wb_list[1], wb2=wb_list[2], wb3=wb_list[3],
                wb4=wb_list[4], wb5=wb_list[5], wbn0=wbn0, wbn1=wbn1,
                misc=misc, cbf=cbf)


def kernel(L0_attn_a_x, L0_attn_a_w_qkv, L0_attn_a_w_proj, L0_attn_a_b_proj,
           L0_attn_b_proj_qk_w, L0_attn_b_proj_qk_b, L0_attn_b_proj_v_w,
           L0_attn_b_proj_v_b, L0_attn_b_in_proj_w, L0_attn_b_in_proj_b,
           L0_attn_b_out_proj_w, L0_attn_b_out_proj_b,
           L0_ffn_w_gate, L0_ffn_w_up, L0_ffn_w_down):
    x = np.ascontiguousarray(np.asarray(L0_attn_a_x, dtype=np.float32))
    shared = host_inputs(L0_attn_a_w_qkv, L0_attn_a_w_proj, L0_attn_a_b_proj,
                         L0_attn_b_proj_qk_w, L0_attn_b_proj_qk_b,
                         L0_attn_b_proj_v_w, L0_attn_b_proj_v_b,
                         L0_attn_b_in_proj_w, L0_attn_b_in_proj_b,
                         L0_attn_b_out_proj_w, L0_attn_b_out_proj_b,
                         L0_ffn_w_gate, L0_ffn_w_up, L0_ffn_w_down)
    nc = build()
    in_maps = []
    for c in range(8):
        xi = np.ascontiguousarray(
            x[2 * c:2 * c + 2].transpose(0, 2, 1)).reshape(NSEQ, CH, 128, S)
        m = dict(shared)
        m["x"] = xi
        in_maps.append(m)
    res = run_bass_kernel_spmd(nc, in_maps, core_ids=list(range(8)))
    out = np.empty((16, S, D), np.float32)
    for c in range(8):
        y = np.asarray(res.results[c]["y"])
        out[2 * c:2 * c + 2] = y.reshape(NSEQ, D, S).transpose(0, 2, 1)
    return out
